# version 08
# baseline (speedup 1.0000x reference)
import numpy as np
import ml_dtypes
from contextlib import ExitStack

import concourse.bass as bass
import concourse.mybir as mybir
from concourse.bass_utils import run_bass_kernel_spmd

F32 = mybir.dt.float32
BF16 = mybir.dt.bfloat16
FP8 = mybir.dt.float8e4
U32 = mybir.dt.uint32
AF = mybir.ActivationFunctionType
ALU = mybir.AluOpType
AX = mybir.AxisListType

D = 1024
S = 8192
CQ = 4096
WIN = 256
NTILE = 32
NSEL_T = 18
NCOMP_T = 4
THIRDS = [(0, 8), (8, 16), (16, 24), (24, 32)]
NEG = -1.0e30
bfl = ml_dtypes.bfloat16
PIPELINE = True
JCOL, PAR1000, NEG1, NEGHALF, EPSC, ONEC, INV1024, EIGHTH, BIGC = range(9)


def sel_window(t, nkv):
    lo = max(0, 128 * t - 256)
    hi = min(nkv, 128 * t + 128 + 256)
    return lo, hi - lo


def _win_mask(t, B):
    r = np.arange(128)[:, None]
    c = np.arange(640)[None, :]
    j = B + 128 * t - 256 + c
    i = B + 128 * t + r
    valid = (j >= i - WIN) & (j <= i + WIN) & (j >= 0) & (j < S)
    return np.where(valid, 0.0, NEG).astype(bfl)


def _sel_mask(t, kv_len):
    lo, n = sel_window(t, kv_len)
    r = np.arange(128)[:, None]
    c = np.arange(640)[None, :]
    j = lo + c
    i = 128 * t + r
    valid = (j >= i - WIN) & (j <= i + WIN) & (c < n)
    return np.where(valid, 0.0, NEG).astype(bfl)


def stage_core(x, w, core):
    b, h = core // 2, core % 2
    B = h * CQ
    xb = x[b]
    st = {}
    st["xT"] = np.ascontiguousarray(xb.T)
    wt = np.zeros((D, 5120), np.float32)
    lo, hi = B - 256, B + 4864
    slo, shi = max(lo, 0), min(hi, S)
    wt[:, slo - lo:shi - lo] = xb[slo:shi].T
    st["xwinT"] = wt
    st["xownT"] = np.ascontiguousarray(xb[B:B + CQ].T)
    st["xown"] = np.ascontiguousarray(xb[B:B + CQ])
    st["xtm"] = xb

    wq3 = w["w_qkv"].reshape(16, 3, 64, D)
    for nm, idx in (("wq", 0), ("wk", 1), ("wv", 2)):
        m = np.ascontiguousarray(wq3[:, idx].reshape(D, D))
        st[nm + "T"] = np.ascontiguousarray(m.T)
        if nm != "wq":
            st[nm + "nat"] = m
    bq3 = w["b_qkv"].reshape(16, 3, 64)
    fm = lambda v: np.ascontiguousarray(v.reshape(8, 128).T)
    st["bq_fm"] = fm(bq3[:, 0].reshape(D))
    st["bk_fm"] = fm(bq3[:, 1].reshape(D))
    st["bv_fm"] = fm(bq3[:, 2].reshape(D))
    st["bv_row"] = bq3[:, 2].reshape(1, D).copy()
    st["ws_fm"] = fm(w["w_score"][0])
    st["wgT"] = np.ascontiguousarray(w["w_gate"].T)
    st["bg_row"] = w["b_gate"].reshape(1, 3).copy()
    st["woT"] = np.ascontiguousarray(w["w_o"].T)
    st["bo_row"] = w["b_o"].reshape(1, D).copy()
    st["wcT"] = np.ascontiguousarray(w["w_comp"].T)
    st["bc_row"] = w["b_comp"].reshape(1, D).copy()
    st["fker"] = np.ascontiguousarray(w["fker"])
    st["fbias_bc"] = np.broadcast_to(np.asarray(w["fbias"]).reshape(1, D), (128, D)).copy()
    sc2 = np.array([[float(np.asarray(w["s_in"]).reshape(-1)[0]),
                     float(np.asarray(w["s_k"]).reshape(-1)[0])]], np.float32)
    st["scales_bc"] = np.broadcast_to(sc2, (128, 2)).copy()

    st["mwin"] = np.stack([_win_mask(t, B) for t in range(NTILE)])
    st["msel"] = np.stack([_sel_mask(t, 2048) for t in range(NSEL_T)])
    st["mcomp"] = np.stack([_sel_mask(t, 256) for t in range(NCOMP_T)])
    one = np.ones((128, 1), np.float32)
    f = 1.0 if h == 0 else 0.0
    st["flags"] = np.concatenate([one * f, one * f], 1)
    st["identb"] = np.eye(128, dtype=bfl)
    p = np.arange(128, dtype=np.float32)[:, None]
    st["consts"] = np.concatenate([
        (p % 64) + 1000.0 * (p // 64), 1000.0 * (p % 2), -one, -0.5 * one,
        1e-5 * one, one, one / 1024.0, 0.125 * one, 1e5 * one,
        0 * one, 0 * one, 0 * one], 1)
    st["iota64"] = np.broadcast_to(np.arange(64, dtype=np.float32), (128, 64)).copy()
    st["ones_row"] = np.ones((1, 128), np.float32)
    return st


class Builder:
    def __init__(self, debug=False):
        self.debug = debug
        nc = self.nc = bass.Bass("TRN2", target_bir_lowering=False, debug=False,
                                 detect_race_conditions=False)
        self.di = {}
        for name, shape, dt in [
            ("xT", (D, S), F32), ("xwinT", (D, 5120), F32), ("xownT", (D, CQ), F32),
            ("xown", (CQ, D), F32), ("xtm", (S, D), F32),
            ("wqT", (D, D), F32), ("wkT", (D, D), F32), ("wvT", (D, D), F32),
            ("wknat", (D, D), F32), ("wvnat", (D, D), F32),
            ("bq_fm", (128, 8), F32), ("bk_fm", (128, 8), F32), ("bv_fm", (128, 8), F32),
            ("bv_row", (1, D), F32), ("ws_fm", (128, 8), F32),
            ("wgT", (D, 3), F32), ("bg_row", (1, 3), F32),
            ("woT", (D, D), F32), ("bo_row", (1, D), F32),
            ("wcT", (32 * D, D), F32), ("bc_row", (1, D), F32),
            ("fker", (D, D), BF16), ("fbias_bc", (128, D), BF16),
            ("scales_bc", (128, 2), F32),
            ("mwin", (NTILE, 128, 640), BF16), ("msel", (NSEL_T, 128, 640), BF16),
            ("mcomp", (NCOMP_T, 128, 640), BF16), ("flags", (128, 2), F32),
            ("identb", (128, 128), BF16),
            ("consts", (128, 12), F32), ("iota64", (128, 64), F32),
            ("ones_row", (1, 128), F32),
        ]:
            self.di[name] = nc.dram_tensor(name, list(shape), dt, kind="ExternalInput").ap()
        self.out = nc.dram_tensor("out", [CQ, D], F32, kind="ExternalOutput").ap()
        self.sscore = nc.dram_tensor("sscore", [2, S], F32, kind="ExternalOutput").ap()
        self.sidx = nc.dram_tensor("sidx", [2, 2048], F32, kind="ExternalOutput").ap()
        self.sosel = nc.dram_tensor("sosel", [NSEL_T, 128, D], BF16, kind="ExternalOutput").ap()
        self.socomp = nc.dram_tensor("socomp", [NCOMP_T, 128, D], BF16, kind="ExternalOutput").ap()
        self.dbg = {}
        if debug:
            for name, shape, dt in [
                ("d_ck", (128, 8, 256), BF16), ("d_cv", (128, 2, D), BF16),
                ("d_sk", (128, 8, 2048), BF16), ("d_sv", (128, 16, D), BF16),
                ("d_kw", (128, 8, 1536), BF16), ("d_vw", (128, 12, D), BF16),
                ("d_q", (128, 8, 512), BF16), ("d_gate", (128, 4, 3), F32),
                ("d_comb", (4, 128, D), F32), ("d_z", (4, 128, D), F32),
                ("d_probs", (128, 640), BF16), ("d_rsum", (128, 4), F32),
                ("d_pT", (128, 5, 128), BF16),
                ("d_y1", (4, 128, D), F32),
            ]:
                self.dbg[name] = nc.dram_tensor(name, list(shape), dt, kind="ExternalOutput").ap()
        self.bankfree = {}
        self.buffree = {}

    def init_sems(self, E):
        self._E = E
        self.h, self.c = {}, {}
        for n in ("mm", "dv", "ac"):
            self.h[n] = E(self.nc.semaphore("sem_" + n))
            self.c[n] = 0

    def sem(self, n):
        if n not in self.h:
            self.h[n] = self._E(self.nc.semaphore("sem_" + n))
            self.c[n] = 0
        return self.h[n]

    def inc(self, inst, n, k=1):
        self.sem(n)
        inst.then_inc(self.h[n], k)
        self.c[n] += k
        return self.c[n]

    def at(self, n):
        self.sem(n)
        return self.c[n]

    def wt(self, eng, n, v=None):
        self.sem(n)
        v = self.c[n] if v is None else v
        if v > 0:
            eng.wait_ge(self.h[n], v)

    def use_bank(self, eng, bank):
        if bank in self.bankfree:
            n, v = self.bankfree.pop(bank)
            if v > 0:
                eng.wait_ge(self.h[n], v)

    def free_bank(self, bank, n):
        self.bankfree[bank] = (n, self.c[n])

    def free_bank_at(self, bank, n, count):
        self.sem(n)
        self.bankfree[bank] = (n, count)

    def fence(self, eng):
        for n in ("mm", "dv", "ac"):
            self.wt(eng, n)

    def vsync(self, v, ins):
        c = self.inc(ins, "dvs", 1)
        v.wait_ge(self.h["dvs"], c)
        return ins

    def use_buf(self, eng, name):
        for n, v in self.buffree.pop(name, ()):
            if v > 0:
                eng.wait_ge(self.h[n], v)

    def free_buf(self, name, n):
        self.buffree.setdefault(name, []).append((n, self.c[n]))

    def build(self):
        nc = self.nc
        with ExitStack() as ctx:
            E = ctx.enter_context
            self.init_sems(E)

            def sbuf(name, shape, dt):
                return E(nc.sbuf_tensor(name, list(shape), dt))

            self.wk_s = sbuf("wk_s", (128, 8, D), BF16)
            self.wv_s = sbuf("wv_s", (128, 8, D), BF16)
            self.bq_c = sbuf("bq_c", (128, 8), F32)
            self.bk_c = sbuf("bk_c", (128, 8), F32)
            self.bv_c = sbuf("bv_c", (128, 8), F32)
            self.ws_s = sbuf("ws_s", (128, 8), F32)
            self.cons = sbuf("cons", (128, 12), F32)
            self.iota64 = sbuf("iota64_s", (128, 64), F32)
            self.flg = sbuf("flg", (128, 2), F32)
            self.idb = sbuf("idb", (128, 128), BF16)
            self.u2b = sbuf("u2b", (128, 8, 2), BF16)
            self.scal = sbuf("scal", (128, 4), F32)
            self.onesb = sbuf("onesb", (1, 128), BF16)
            self.bvrow = sbuf("bvrow", (1, D), BF16)
            self.bcrow = sbuf("bcrow", (1, D), BF16)
            self.borow = sbuf("borow", (1, D), BF16)
            self.bgrow = sbuf("bgrow", (1, 3), BF16)
            self.psA = E(nc.psum_tensor("psA", [128, 512], F32))
            self.psB = E(nc.psum_tensor("psB", [128, 512], F32))
            self.psC = E(nc.psum_tensor("psC", [128, 512], F32))
            self.psD = E(nc.psum_tensor("psD", [128, 512], F32))
            self.psE = E(nc.psum_tensor("psE", [128, 512], F32))
            self.psF = E(nc.psum_tensor("psF", [128, 512], F32))
            self.psT = E(nc.psum_tensor("psT", [128, 8, 128], BF16))
            self.psG = E(nc.psum_tensor("psG", [128, 512], F32))

            self.block = E(nc.Block())

            with ExitStack() as c14:
                self.ckfm = c14.enter_context(nc.sbuf_tensor("ckfm", [128, 8, 256], BF16))
                self.cvtm = c14.enter_context(nc.sbuf_tensor("cvtm", [128, 2, D], BF16))
                with ExitStack() as c0:
                    self.p0_setup(c0)
                with ExitStack() as c1:
                    self.p1_kv_compress(c1)
                with ExitStack() as c2:
                    self.p2_topk(c2)
                with ExitStack() as c3:
                    self.p34_sel(c3)
                self.P4MM = self.at("mm")
            with ExitStack() as c5:
                self.p5_main(c5)
            self.finish()
        return nc

    def p0_setup(self, ctx):
        nc, di, block = self.nc, self.di, self.block
        E = ctx.enter_context
        fm128 = lambda ap: ap.rearrange("(c p) d -> p c d", p=128)

        @block.sync
        def _(s):
            for dst, src in [
                (self.bq_c[:], di["bq_fm"]), (self.bk_c[:], di["bk_fm"]),
                (self.bv_c[:], di["bv_fm"]), (self.ws_s[:], di["ws_fm"]),
                (self.cons[:], di["consts"]), (self.flg[:], di["flags"]),
                (self.iota64[:], di["iota64"]),
                (self.idb[:], di["identb"]), (self.scal[:, 0:2], di["scales_bc"]),
            ]:
                self.inc(s.dma_start(dst, src), "sP0S", 16)

        @block.gpsimd
        def _(g):
            for dst, src in [
                (self.wk_s[:], fm128(di["wkT"])),
                (self.wv_s[:], fm128(di["wvT"])),
                (self.onesb[:], di["ones_row"]),
                (self.bvrow[:], di["bv_row"]), (self.bcrow[:], di["bc_row"]),
                (self.borow[:], di["bo_row"]), (self.bgrow[:], di["bg_row"]),
            ]:
                self.inc(g.dma_start(dst, src), "sP0G", 16)

        @block.vector
        def _(v):
            self.wt(v, "sP0S")
            self.wt(v, "sP0G")
            self.vsync(v, v.tensor_copy(self.scal[:, 2:3], self.scal[:, 1:2]))
            self.vsync(v, v.tensor_tensor(self.scal[:, 1:2], self.scal[:, 0:1], self.scal[:, 1:2], ALU.mult))
            self.vsync(v, v.reciprocal(self.scal[:, 0:1], self.scal[:, 0:1]))
            self.vsync(v, v.reciprocal(self.scal[:, 2:3], self.scal[:, 2:3]))
            v.tensor_scalar(self.bq_c[:], self.bq_c[:], self.cons[:, EIGHTH:EIGHTH + 1], None, ALU.mult)
            v.memset(self.ckfm[:], 0.0)
            self.inc(v.memset(self.cvtm[:], 0.0), "dv", 1)

        unat = E(nc.sbuf_tensor("unat", [128, 8, D], F32))
        u2f = E(nc.sbuf_tensor("u2f", [128, 16], F32))

        @block.sync
        def _(s):
            self.inc(s.dma_start(unat[:], fm128(di["wknat"])), "sUN", 16)
        UKLD = self.at("sUN")

        @block.tensor
        def _(t):
            self.wt(t, "sUN", UKLD)
            self.wt(t, "sP0S")
            for dj in range(8):
                for cj in range(8):
                    mm = t.matmul(self.psG[:, dj:dj + 1], unat[:, cj, dj * 128:(dj + 1) * 128],
                                  self.ws_s[:, cj:cj + 1], start=(cj == 0), stop=(cj == 7))
            self.inc(mm, "mm", 1)
        UKMM = self.at("mm")

        @block.vector
        def _(v):
            self.wt(v, "mm", UKMM)
            self.inc(v.tensor_copy(u2f[:, 0:8], self.psG[:, 0:8]), "dv", 1)
        UKEV = self.at("dv")

        @block.sync
        def _(s):
            self.wt(s, "dv", UKEV)
            self.inc(s.dma_start(unat[:], fm128(di["wvnat"])), "sUN", 16)
        UVLD = self.at("sUN")

        @block.tensor
        def _(t):
            self.wt(t, "sUN", UVLD)
            for dj in range(8):
                for cj in range(8):
                    mm = t.matmul(self.psG[:, 8 + dj:9 + dj], unat[:, cj, dj * 128:(dj + 1) * 128],
                                  self.ws_s[:, cj:cj + 1], start=(cj == 0), stop=(cj == 7))
            self.inc(mm, "mm", 1)
        UVMM = self.at("mm")

        @block.vector
        def _(v):
            self.wt(v, "mm", UVMM)
            v.tensor_copy(u2f[:, 8:16], self.psG[:, 8:16])
            v.tensor_copy(self.u2b[:, :, 0:1], u2f[:, 0:8].rearrange("p (c o) -> p c o", o=1))
            self.inc(v.tensor_copy(self.u2b[:, :, 1:2], u2f[:, 8:16].rearrange("p (c o) -> p c o", o=1)), "dv", 1)
        self.free_bank("G", "dv")
        self.U2B = self.at("dv")

    def p1_kv_compress(self, ctx):
        nc, di, block = self.nc, self.di, self.block
        E = ctx.enter_context
        kfm = E(nc.sbuf_tensor("kfm", [128, 8, 4096], BF16))
        vfm = E(nc.sbuf_tensor("vfm", [128, 8, 4096], BF16))
        xb = E(nc.sbuf_tensor("xb", [128, 8, 2, 512], BF16))
        wcb = E(nc.sbuf_tensor("wcb", [128, 4, D], BF16))
        ssc = E(nc.sbuf_tensor("ssc", [2, 512], F32))
        cktmp = E(nc.sbuf_tensor("cktmp", [128, 128], BF16))

        xb_free = [0, 0]
        wc_free = [0] * 4
        ssc_free = [0]

        for SC in range(2):
            for ch in range(8):
                gc = SC * 8 + ch
                slot = gc % 2

                @block.gpsimd
                def _(g, gc=gc, slot=slot):
                    if xb_free[slot]:
                        g.wait_ge(self.h["mm"], xb_free[slot])
                    src = di["xT"].rearrange("(c p) t -> p c t", p=128)
                    for half in range(2):
                        self.inc(g.dma_start(xb[:, half * 4:half * 4 + 4, slot, :],
                                             src[:, half * 4:half * 4 + 4, gc * 512:(gc + 1) * 512]),
                                 "sXB%d" % slot, 16)
                XLD = self.at("sXB%d" % slot)

                KVMM0 = [0]

                @block.tensor
                def _(t, gc=gc, slot=slot, XLD=XLD, KVMM0=KVMM0):
                    self.wt(t, "sXB%d" % slot, XLD)
                    if gc == 0:
                        self.wt(t, "dv", self.U2B)
                    self.use_bank(t, "G")
                    if ssc_free[0]:
                        t.wait_ge(self.h["sSSC"], ssc_free[0])
                    for dj in range(8):
                        mm = t.matmul(self.psG[0:2, :], self.u2b[:, dj, :], xb[:, dj, slot, :],
                                      start=(dj == 0), stop=(dj == 7))
                    KVMM0[0] = self.inc(mm, "mm", 1)
                    self.free_bank_at("G", "dv", self.at("dv") + 1)
                    DV0 = self.at("dv") + 1
                    for kind, wsb in enumerate((self.wk_s, self.wv_s)):
                        for cc in range(8):
                            bank = "ABCD"[kind * 2 + cc % 2]
                            self.use_bank(t, bank)
                            pb = (self.psA, self.psB, self.psC, self.psD)[kind * 2 + cc % 2]
                            for dj in range(8):
                                mm = t.matmul(pb[:], wsb[:, dj, cc * 128:(cc + 1) * 128],
                                              xb[:, dj, slot, :], start=(dj == 0), stop=(dj == 7))
                            self.inc(mm, "mm", 1)
                            self.free_bank_at(bank, "dv", DV0 + kind * 8 + cc + 1)
                xb_free[slot] = self.at("mm")

                @block.vector
                def _(v, KVMM0=KVMM0):
                    self.wt(v, "mm", KVMM0[0])
                    self.inc(v.tensor_copy(ssc[:], self.psG[0:2, :]), "dv", 1)
                SSC_DV = self.at("dv")

                @block.sync
                def _(s, gc=gc, SSC_DV=SSC_DV):
                    self.wt(s, "dv", SSC_DV)
                    self.inc(s.dma_start(self.sscore[:, gc * 512:(gc + 1) * 512], ssc[:]), "sSSC", 16)
                ssc_free[0] = self.at("sSSC")

                @block.vector
                def _(v, ch=ch, KVMM0=KVMM0):
                    for kind, (buf, bias) in enumerate(((kfm, self.bk_c), (vfm, self.bv_c))):
                        for cc in range(8):
                            pb = (self.psA, self.psB, self.psC, self.psD)[kind * 2 + cc % 2]
                            v.wait_ge(self.h["mm"], KVMM0[0] + kind * 8 + cc + 1)
                            self.inc(v.tensor_scalar(buf[:, cc, ch * 512:(ch + 1) * 512], pb[:],
                                                     bias[:, cc:cc + 1], None, ALU.add), "dv", 1)
            KV_EVAC = self.at("dv")

            wcld = []
            COMP0 = self.at("mm")

            @block.gpsimd
            def _(g, SC=SC, wcld=wcld, COMP0=COMP0):
                for w in range(256):
                    slot = w % 4
                    if w >= 4:
                        g.wait_ge(self.h["mm"], COMP0 + w - 3)
                    elif wc_free[slot]:
                        g.wait_ge(self.h["mm"], wc_free[slot])
                    src = di["wcT"].rearrange("(w p) c -> p w c", p=128)[:, w, :]
                    wcld.append((slot, self.inc(g.dma_start(wcb[:, slot, :], src), "sWC%d" % slot, 16)))

            @block.tensor
            def _(t, SC=SC, wcld=wcld, KV_EVAC=KV_EVAC):
                self.wt(t, "dv", KV_EVAC)
                for b_ in "ABCD":
                    self.use_bank(t, b_)
                for w in range(256):
                    slot = w % 4
                    p_, dj = w // 8, w % 8
                    t.wait_ge(self.h["sWC%d" % wcld[w][0]], wcld[w][1])
                    st_ = (w == 0)
                    kap = kfm[:, dj, p_::32]
                    vap = vfm[:, dj, p_::32]
                    t.matmul(self.psA[:], kap, wcb[:, slot, 0:512], start=st_, stop=False)
                    t.matmul(self.psB[:], kap, wcb[:, slot, 512:1024], start=st_, stop=False)
                    t.matmul(self.psC[:], vap, wcb[:, slot, 0:512], start=st_, stop=False)
                    mm = t.matmul(self.psD[:], vap, wcb[:, slot, 512:1024], start=st_, stop=False)
                    wc_free[slot] = self.inc(mm, "mm", 1)
                t.matmul(self.psA[:], self.onesb[:], self.bcrow[:, 0:512], start=False, stop=True)
                t.matmul(self.psB[:], self.onesb[:], self.bcrow[:, 512:1024], start=False, stop=True)
                t.matmul(self.psC[:], self.onesb[:], self.bcrow[:, 0:512], start=False, stop=True)
                mm = t.matmul(self.psD[:], self.onesb[:], self.bcrow[:, 512:1024], start=False, stop=True)
                self.inc(mm, "mm", 1)
            COMPMM = self.at("mm")

            @block.vector
            def _(v, SC=SC, COMPMM=COMPMM):
                self.wt(v, "mm", COMPMM)
                v.tensor_copy(self.cvtm[:, SC, 0:512], self.psC[:])
                self.inc(v.tensor_copy(self.cvtm[:, SC, 512:1024], self.psD[:]), "dv", 1)
            self.free_bank("C", "dv")
            self.free_bank("D", "dv")

            for half, pb in ((0, self.psA), (1, self.psB)):
                for q4 in range(4):
                    cc = half * 4 + q4

                    @block.vector
                    def _(v, pb=pb, q4=q4):
                        self.use_buf(v, "cktmp")
                        self.inc(v.tensor_copy(cktmp[:], pb[:, q4 * 128:(q4 + 1) * 128]), "dv", 1)
                    CKDV = self.at("dv")

                    @block.tensor
                    def _(t, CKDV=CKDV):
                        self.wt(t, "dv", CKDV)
                        self.use_bank(t, "T")
                        self.inc(t.transpose(self.psT[:, 0, :], cktmp[:], self.idb[:]), "mm", 1)
                        self.free_bank_at("T", "dv", self.at("dv") + 1)
                    self.free_buf("cktmp", "mm")
                    CKMM = self.at("mm")

                    @block.vector
                    def _(v, SC=SC, cc=cc, CKMM=CKMM):
                        self.wt(v, "mm", CKMM)
                        self.inc(v.tensor_copy(self.ckfm[:, cc, SC * 128:SC * 128 + 128],
                                               self.psT[:, 0, :]), "dv", 1)
            self.free_bank("A", "dv")
            self.free_bank("B", "dv")

        if self.debug:
            DBGDV = self.at("dv")

            @block.sync
            def _(s, DBGDV=DBGDV):
                self.wt(s, "dv", DBGDV)
                self.inc(s.dma_start(self.dbg["d_ck"], self.ckfm[:]), "sDBG", 16)
                self.inc(s.dma_start(self.dbg["d_cv"], self.cvtm[:]), "sDBG", 16)

    def p2_topk(self, ctx):
        nc, block = self.nc, self.block
        E = ctx.enter_context
        blk = E(nc.sbuf_tensor("blk", [128, 2, 64], F32))
        wrk = E(nc.sbuf_tensor("wrk", [128, 64], F32))
        prev = E(nc.sbuf_tensor("prev", [128, 64], F32))
        scr64 = E(nc.sbuf_tensor("scr64", [128, 64], F32))
        e1 = E(nc.sbuf_tensor("e1", [128, 64], F32))
        cand = E(nc.sbuf_tensor("cand", [128, 64], F32))
        t16v = E(nc.sbuf_tensor("t16v", [128, 16], F32))
        one8 = E(nc.sbuf_tensor("one8", [128, 8], F32))
        idxf = E(nc.sbuf_tensor("idxf", [128, 16], F32))

        @block.sync
        def _(s):
            self.wt(s, "sSSC")
            self.fence(s)
            for Ti in range(2):
                self.inc(s.dma_start(blk[:, Ti, :],
                                     self.sscore[Ti, :].rearrange("(p f) -> p f", p=128)), "sBLK", 16)
        BLKLD = self.at("sBLK")

        for Ti in range(2):
            @block.vector
            def _(v, Ti=Ti, BLKLD=BLKLD):
                self.wt(v, "sBLK", BLKLD)
                if Ti == 1:
                    self.use_buf(v, "idxf")
                self.vsync(v, v.tensor_copy(wrk[:], blk[:, Ti, :]))
                self.vsync(v, v.max(out=t16v[:, 0:8], in_=wrk[:]))
                self.vsync(v, v.match_replace(scr64[:], t16v[:, 0:8], wrk[:], NEG))
                self.vsync(v, v.max(out=t16v[:, 8:16], in_=scr64[:]))
                v.tensor_copy(prev[:], wrk[:])
                v.memset(one8[:], 1e30)
                for r in range(16):
                    self.vsync(v, v.tensor_copy(one8[:, 0:1], t16v[:, r:r + 1]))
                    self.vsync(v, v.match_replace(wrk[:], one8[:], wrk[:], NEG))
                    v.tensor_tensor(e1[:], wrk[:], prev[:], ALU.is_equal)
                    v.tensor_scalar(cand[:], e1[:], self.cons[:, BIGC:BIGC + 1], None, ALU.mult)
                    self.vsync(v, v.tensor_tensor(cand[:], cand[:], self.iota64[:], ALU.add))
                    self.vsync(v, v.tensor_reduce(idxf[:, r:r + 1], cand[:], AX.X, ALU.min))
                    v.tensor_copy(prev[:], wrk[:])
                self.inc(v.tensor_scalar(idxf[:], idxf[:], self.cons[:, PAR1000:PAR1000 + 1],
                                         None, ALU.add), "dv", 1)
            IDXDV = self.at("dv")

            @block.sync
            def _(s, Ti=Ti, IDXDV=IDXDV):
                self.wt(s, "dv", IDXDV)
                self.inc(s.dma_start(self.sidx[Ti, :].rearrange("(p f) -> p f", p=128), idxf[:]), "sIDX", 16)
            self.free_buf("idxf", "sIDX")
        self.IDXST = self.at("sIDX")

    def p34_sel(self, ctx):
        nc, di, block = self.nc, self.di, self.block
        E = ctx.enter_context
        skfm = E(nc.sbuf_tensor("skfm", [128, 8, 2048], BF16))
        svtm = E(nc.sbuf_tensor("svtm", [128, 16, D], BF16))
        xselK = E(nc.sbuf_tensor("xselK", [128, 8, 512], BF16))
        xselV = E(nc.sbuf_tensor("xselV", [128, 8, 512], BF16))
        bcidx = E(nc.sbuf_tensor("bcidx", [128, 2, 512], F32))
        xtmb = E(nc.sbuf_tensor("xtmb", [128, 4, D], BF16))
        oneh = E(nc.sbuf_tensor("oneh", [128, 2, 32], BF16))
        wq_s = E(nc.sbuf_tensor("wq_s4", [128, 8, D], BF16))
        xob = E(nc.sbuf_tensor("xob4", [128, 8, 512], BF16))
        qfm = E(nc.sbuf_tensor("qfm4", [128, 8, 512], BF16))

        @block.gpsimd
        def _(g):
            self.fence(g)
            self.inc(g.dma_start(wq_s[:],
                                 di["wqT"].rearrange("(c p) d -> p c d", p=128)), "sWQ4", 16)
        self.WQLD = ("sWQ4", self.at("sWQ4"))
        FENCED = [False]

        xtm_free = [0] * 4
        for g4 in range(4):
            @block.sync
            def _(s, g4=g4):
                if not FENCED[0]:
                    self.fence(s)
                    FENCED[0] = True
                self.wt(s, "sIDX", self.IDXST)
                self.use_buf(s, "bcidx")
                for Ti in range(2):
                    src = self.sidx[Ti:Ti + 1, g4 * 512:(g4 + 1) * 512].to_broadcast([128, 512])
                    self.inc(s.dma_start(bcidx[:, Ti, :], src), "sBC", 16)
            BCLD = self.at("sBC")

            for c16 in range(16):
                c64 = g4 * 16 + c16
                slot = c64 % 4

                @block.gpsimd
                def _(g, c64=c64, slot=slot):
                    if xtm_free[slot]:
                        g.wait_ge(self.h["mm"], xtm_free[slot])
                    src = di["xtm"].rearrange("(a p) d -> p a d", p=128)[:, c64, :]
                    self.inc(g.dma_start(xtmb[:, slot, :], src), "sXT%d" % slot, 16)
                XTLD = self.at("sXT%d" % slot)

                @block.vector
                def _(v, c16=c16, BCLD=BCLD):
                    self.wt(v, "sBC", BCLD)
                    self.use_buf(v, "oneh")
                    for Ti in range(2):
                        self.inc(v.tensor_scalar(oneh[:, Ti, :], bcidx[:, Ti, c16 * 32:(c16 + 1) * 32],
                                                 self.cons[:, JCOL:JCOL + 1], None, ALU.is_equal), "dv", 1)
                OHDV = self.at("dv")

                @block.tensor
                def _(t, slot=slot, XTLD=XTLD, OHDV=OHDV):
                    self.wt(t, "sXT%d" % slot, XTLD)
                    self.wt(t, "dv", OHDV)
                    for Ti, pb, bk in ((0, self.psE, "E"), (1, self.psF, "F")):
                        self.use_bank(t, bk)
                        for dj in range(8):
                            mm = t.matmul(pb[:, dj * 32:(dj + 1) * 32],
                                          xtmb[:, slot, dj * 128:(dj + 1) * 128],
                                          oneh[:, Ti, :], start=True, stop=True)
                        self.inc(mm, "mm", 1)
                        self.free_bank_at(bk, "dv", self.at("dv") + 1)
                xtm_free[slot] = self.at("mm")
                self.free_buf("oneh", "mm")
                GAMM = self.at("mm")

                @block.vector
                def _(v, c16=c16, GAMM=GAMM):
                    self.wt(v, "mm", GAMM)
                    v.tensor_copy(xselK[:, 0:8, c16 * 32:(c16 + 1) * 32],
                                  self.psE[:, 0:256].rearrange("p (c f) -> p c f", c=8))
                    self.inc(v.tensor_copy(xselV[:, 0:8, c16 * 32:(c16 + 1) * 32],
                                           self.psF[:, 0:256].rearrange("p (c f) -> p c f", c=8)), "dv", 1)
            self.free_buf("bcidx", "dv")
            XSELDV = self.at("dv")

            SKMM0 = [0]

            @block.tensor
            def _(t, XSELDV=XSELDV, SKMM0=SKMM0):
                self.wt(t, "dv", XSELDV)
                first = True
                AC0 = self.at("ac")
                for cc in range(8):
                    bank = "AB"[cc % 2]
                    self.use_bank(t, bank)
                    pb = (self.psA, self.psB)[cc % 2]
                    for dj in range(8):
                        mm = t.matmul(pb[:], self.wk_s[:, dj, cc * 128:(cc + 1) * 128],
                                      xselK[:, dj, :], start=(dj == 0), stop=(dj == 7))
                    v_ = self.inc(mm, "mm", 1)
                    if first:
                        SKMM0[0] = v_ - 1
                        first = False
                    self.free_bank_at(bank, "ac", AC0 + cc + 1)

            @block.scalar
            def _(sc, g4=g4, SKMM0=SKMM0):
                for cc in range(8):
                    pb = (self.psA, self.psB)[cc % 2]
                    sc.wait_ge(self.h["mm"], SKMM0[0] + cc + 1)
                    self.inc(sc.activation(skfm[:, cc, g4 * 512:(g4 + 1) * 512], pb[:],
                                           AF.Identity, bias=self.bk_c[:, cc:cc + 1]), "ac", 1)

            SVMM0 = [0]

            @block.tensor
            def _(t, SVMM0=SVMM0):
                first = True
                AC0 = self.at("ac")
                for s4 in range(4):
                    for half in range(2):
                        bank = "CD"[half]
                        self.use_bank(t, bank)
                        pb = (self.psC, self.psD)[half]
                        for dj in range(8):
                            t.matmul(pb[:], xselV[:, dj, s4 * 128:(s4 + 1) * 128],
                                     self.wv_s[:, dj, half * 512:(half + 1) * 512],
                                     start=(dj == 0), stop=False)
                        mm = t.matmul(pb[:], self.onesb[:], self.bvrow[:, half * 512:(half + 1) * 512],
                                      start=False, stop=True)
                        v_ = self.inc(mm, "mm", 1)
                        if first:
                            SVMM0[0] = v_ - 1
                            first = False
                        self.free_bank_at(bank, "ac", AC0 + s4 * 2 + half + 1)

            @block.scalar
            def _(sc, g4=g4, SVMM0=SVMM0):
                for s4 in range(4):
                    for half in range(2):
                        pb = (self.psC, self.psD)[half]
                        sc.wait_ge(self.h["mm"], SVMM0[0] + s4 * 2 + half + 1)
                        self.inc(sc.copy(svtm[:, g4 * 4 + s4, half * 512:(half + 1) * 512], pb[:]), "ac", 1)
        P3AC = self.at("ac")

        if self.debug:
            @block.sync
            def _(s, P3AC=P3AC):
                self.wt(s, "ac", P3AC)
                self.inc(s.dma_start(self.dbg["d_sk"], skfm[:]), "sDBG", 16)
                self.inc(s.dma_start(self.dbg["d_sv"], svtm[:]), "sDBG", 16)

        mbuf = E(nc.sbuf_tensor("mbuf4", [128, 2, 640], BF16))
        probs = E(nc.sbuf_tensor("probs4", [128, 2, 640], BF16))
        probsT = E(nc.sbuf_tensor("probsT4", [128, 2, 5, 128], BF16))
        negmax = E(nc.sbuf_tensor("negmax4", [128, 2], F32))
        rowsum = E(nc.sbuf_tensor("rowsum4", [128, 2], F32))
        rsum = E(nc.sbuf_tensor("rsum4", [128, 2], F32))
        oebuf = E(nc.sbuf_tensor("oebuf4", [128, D], BF16))
        self.fr = dict(mbuf=mbuf, probs=probs, probsT=probsT, negmax=negmax,
                       rowsum=rowsum, rsum=rsum, npar=2)

        jobs = []
        for t in range(NSEL_T):
            jobs.append((t, "sel"))
            if t < NCOMP_T:
                jobs.append((t, "comp"))

        def p4_front(j):
            t, kind = jobs[j]
            if kind == "sel" and t % 4 == 0:
                self.emit_qproj(di["xownT"], t // 4, xob, qfm, wq_s, self.WQLD)
            if kind == "sel":
                return self.front_a(t, qfm, t % 4, skfm, 2048, di["msel"], kind="sel", par=j)
            return self.front_a(t, qfm, t % 4, self.ckfm, 256, di["mcomp"], kind="comp", par=j)

        pend4 = p4_front(0)
        for j in range(len(jobs)):
            t, kind = jobs[j]
            ctx = pend4
            if j + 1 < len(jobs):
                pend4 = p4_front(j + 1)
            if kind == "sel":
                self.front_b(ctx, svtm, obuf=oebuf, flag_col=1)
            else:
                self.front_b(ctx, self.cvtm, obuf=oebuf, flag_col=0)

    def emit_qproj(self, xsrc, g, xob, qfm, wq_s, wrdy):
        block = self.block

        @block.gpsimd
        def _(gp, g=g):
            self.use_buf(gp, "xob")
            src = xsrc.rearrange("(c p) t -> p c t", p=128)
            for half in range(2):
                self.inc(gp.dma_start(xob[:, half * 4:half * 4 + 4, :],
                                      src[:, half * 4:half * 4 + 4, g * 512:(g + 1) * 512]), "sXOB", 16)
        XOLD = self.at("sXOB")

        QMM0 = [0]

        @block.tensor
        def _(t, XOLD=XOLD, QMM0=QMM0, wrdy=wrdy):
            self.wt(t, "sXOB", XOLD)
            self.wt(t, wrdy[0], wrdy[1])
            self.use_buf(t, "qfm")
            first = True
            AC0 = self.at("ac")
            for cc in range(8):
                self.use_bank(t, "G")
                for dj in range(8):
                    mm = t.matmul(self.psG[:], wq_s[:, dj, cc * 128:(cc + 1) * 128],
                                  xob[:, dj, :], start=(dj == 0), stop=(dj == 7))
                v_ = self.inc(mm, "mm", 1)
                if first:
                    QMM0[0] = v_ - 1
                    first = False
                self.free_bank_at("G", "ac", AC0 + cc + 1)
        self.free_buf("xob", "mm")

        @block.scalar
        def _(sc, QMM0=QMM0):
            for cc in range(8):
                sc.wait_ge(self.h["mm"], QMM0[0] + cc + 1)
                self.inc(sc.activation(qfm[:, cc, :], self.psG[:], AF.Identity,
                                       bias=self.bq_c[:, cc:cc + 1],
                                       scale=self.cons[:, EIGHTH:EIGHTH + 1]), "ac", 1)
        self.q_ready = self.at("ac")

    def front_a(self, t, qfm, ti, kbuf, nkv, msrc, kind, win_third=None, gates=None, par=0):
        block = self.block
        fr = self.fr
        par %= fr["npar"]
        mbuf = fr["mbuf"]
        probs = fr["probs"][:, par]
        negmax = fr["negmax"][:, par:par + 1]
        rowsum = fr["rowsum"][:, par:par + 1]
        rsum = fr["rsum"][:, par:par + 1]
        P = "%d" % par
        if kind == "win":
            lo, n = 128 * (t - win_third), 640
            vchunk0 = t - win_third
        else:
            lo, n = sel_window(t, nkv)
            vchunk0 = lo // 128
        nck = (n + 127) // 128
        mslot = self.msl
        self.msl = (self.msl + 1) % 2
        sb0, sb1 = (("A", "B"), ("C", "D"))[self.scpar]
        pb0, pb1 = ((self.psA, self.psB), (self.psC, self.psD))[self.scpar]
        self.scpar ^= 1
        n0 = min(n, 512)
        n1 = n - n0

        @block.sync
        def _(s, t=t, n=n, mslot=mslot):
            self.use_buf(s, "mbuf%d" % mslot)
            self.inc(s.dma_start(mbuf[:, mslot, 0:n], msrc[t, :, 0:n]), "sMB%d" % mslot, 16)
        MLD = self.at("sMB%d" % mslot)

        SCMM = [0]
        GMM = [0]

        @block.tensor
        def _(tt, ti=ti, lo=lo, n0=n0, n1=n1, SCMM=SCMM, GMM=GMM, gates=gates):
            self.wt(tt, "ac", self.q_ready)
            self.use_bank(tt, sb0)
            self.use_bank(tt, sb1)
            if gates is not None:
                xob, wg_s = gates[0], gates[1]
                self.use_bank(tt, "E")
                for dj in range(8):
                    tt.matmul(self.psE[:, 0:3], xob[:, dj, ti * 128:(ti + 1) * 128],
                              wg_s[:, dj, :], start=(dj == 0), stop=False)
                mm = tt.matmul(self.psE[:, 0:3], self.onesb[:], self.bgrow[:], start=False, stop=True)
                GMM[0] = self.inc(mm, "mm", 1)
            last = None
            for dj in range(8):
                last = tt.matmul(pb0[:, 0:n0], qfm[:, dj, ti * 128:(ti + 1) * 128],
                                 kbuf[:, dj, lo:lo + n0], start=(dj == 0), stop=(dj == 7))
                if n1 > 0:
                    last = tt.matmul(pb1[:, 0:n1], qfm[:, dj, ti * 128:(ti + 1) * 128],
                                     kbuf[:, dj, lo + n0:lo + n], start=(dj == 0), stop=(dj == 7))
            SCMM[0] = self.inc(last, "mm", 1)
        if gates is not None:
            self.free_buf("xob", "mm")

        @block.vector
        def _(v, n=n, SCMM=SCMM, MLD=MLD, mslot=mslot):
            self.wt(v, "mm", SCMM[0])
            self.wt(v, "sMB%d" % mslot, MLD)
            self.use_buf(v, "negmax" + P)
            if n > 512:
                self.vsync(v, v.tensor_tensor(pb0[:, 0:512], pb0[:, 0:512], mbuf[:, mslot, 0:512], ALU.add))
                self.vsync(v, v.tensor_tensor(pb1[:, 0:n - 512], pb1[:, 0:n - 512], mbuf[:, mslot, 512:n], ALU.add))
                self.vsync(v, v.reduce_max(negmax[:], pb0[:, 0:512], axis=AX.X, negate=True))
                self.vsync(v, v.reduce_max(rsum[:], pb1[:, 0:n - 512], axis=AX.X, negate=True))
                self.inc(v.tensor_tensor(negmax[:], negmax[:], rsum[:], ALU.min), "dv", 1)
            else:
                self.vsync(v, v.tensor_tensor(pb0[:, 0:n], pb0[:, 0:n], mbuf[:, mslot, 0:n], ALU.add))
                self.inc(v.reduce_max(negmax[:], pb0[:, 0:n], axis=AX.X, negate=True), "dv", 1)
        MSKDV = self.at("dv")
        self.free_buf("mbuf%d" % mslot, "dv")

        GEXAC = [0]

        @block.scalar
        def _(sc, n=n, MSKDV=MSKDV, GMM=GMM, gates=gates, GEXAC=GEXAC, SCMM=SCMM):
            if gates is not None:
                gex = gates[2]
                sc.wait_ge(self.h["mm"], GMM[0])
                self.use_buf(sc, "gex")
                GEXAC[0] = self.inc(sc.activation(gex[:], self.psE[:, 0:3], AF.Exp,
                                                  scale=self.cons[:, NEG1:NEG1 + 1]), "ac", 1)
            self.wt(sc, "dv", MSKDV)
            self.use_buf(sc, "probs" + P)
            self.use_buf(sc, "rowsum" + P)
            if n > 512:
                sc.activation(probs[:, 0:512], pb0[:, 0:512], AF.Exp,
                              bias=negmax[:, 0:1], accum_out=rowsum[:, 0:1])
                self.inc(sc.activation(probs[:, 512:n], pb1[:, 0:n - 512], AF.Exp,
                                       bias=negmax[:, 0:1], accum_out=rsum[:, 0:1]), "ac", 1)
            else:
                self.inc(sc.activation(probs[:, 0:n], pb0[:, 0:n], AF.Exp,
                                       bias=negmax[:, 0:1], accum_out=rowsum[:, 0:1]), "ac", 1)
        EXPAC = self.at("ac")
        self.free_buf("negmax" + P, "ac")
        self.free_bank(sb0, "ac")
        self.free_bank(sb1, "ac")
        if gates is not None:
            self.free_bank("E", "ac")

        GDV = 0
        if gates is not None:
            gex, gcolb, gslot = gates[2], gates[3], gates[4]

            @block.vector
            def _(v, GEXAC=GEXAC, gslot=gslot):
                self.wt(v, "ac", GEXAC[0])
                self.use_buf(v, "gcol%d" % gslot)
                self.vsync(v, v.tensor_scalar(gcolb[:, gslot, :], gex[:],
                                              self.cons[:, ONEC:ONEC + 1], None, ALU.add))
                self.vsync(v, v.reciprocal(gcolb[:, gslot, :], gcolb[:, gslot, :]))
                self.inc(v.engine_nop(), "dv", 1)
            GDV = self.at("dv")
            self.free_buf("gex", "dv")

        @block.vector
        def _(v, n=n, EXPAC=EXPAC):
            self.wt(v, "ac", EXPAC)
            self.use_buf(v, "rsum" + P)
            if n > 512:
                self.vsync(v, v.tensor_tensor(rowsum[:], rowsum[:], rsum[:], ALU.add))
            self.vsync(v, v.reciprocal(rsum[:], rowsum[:]))
            self.inc(v.engine_nop(), "dv", 1)
        RDV = self.at("dv")
        self.free_buf("rowsum" + P, "dv")
        return dict(t=t, kind=kind, n=n, n0=n0, n1=n1, nck=nck, lo=lo, vchunk0=vchunk0,
                    EXPAC=EXPAC, RDV=RDV, GDV=GDV, gates=gates, par=par)

    def front_b(self, ctx, vbuf, obuf=None, flag_col=None, comb=None):
        block = self.block
        fr = self.fr
        par = ctx["par"]
        probs = fr["probs"][:, par]
        probsT = fr["probsT"][:, par]
        rsum = fr["rsum"][:, par:par + 1]
        P = "%d" % par
        t, kind, n, nck, vchunk0 = ctx["t"], ctx["kind"], ctx["n"], ctx["nck"], ctx["vchunk0"]
        EXPAC, RDV = ctx["EXPAC"], ctx["RDV"]
        TMM = [0]

        @block.tensor
        def _(tt, nck=nck, n=n, EXPAC=EXPAC, TMM=TMM):
            self.wt(tt, "ac", EXPAC)
            self.use_bank(tt, "T")
            for ci in range(nck):
                w_ = min(128, n - ci * 128)
                mm = tt.transpose(self.psT[0:w_, ci, :], probs[:, ci * 128:ci * 128 + w_], self.idb[:])
            TMM[0] = self.inc(mm, "mm", 1)
            self.free_bank_at("T", "dv", self.at("dv") + 1)
        self.free_buf("probs" + P, "mm")

        @block.vector
        def _(v, nck=nck, TMM=TMM):
            self.wt(v, "mm", TMM[0])
            self.use_buf(v, "probsT" + P)
            self.inc(v.tensor_copy(probsT[:, 0:nck, :], self.psT[:, 0:nck, :]), "dv", 1)
        PTDV = self.at("dv")

        AVMM = [0]

        @block.tensor
        def _(tt, nck=nck, n=n, vchunk0=vchunk0, PTDV=PTDV, AVMM=AVMM):
            self.wt(tt, "dv", PTDV)
            self.use_bank(tt, "E")
            self.use_bank(tt, "F")
            for half, pe in ((0, self.psE), (1, self.psF)):
                for ci in range(nck):
                    w_ = min(128, n - ci * 128)
                    mm = tt.matmul(pe[:], probsT[0:w_, ci, :],
                                   vbuf[0:w_, vchunk0 + ci, half * 512:(half + 1) * 512],
                                   start=(ci == 0), stop=(ci == nck - 1))
                self.inc(mm, "mm", 1)
            AVMM[0] = self.at("mm")
        self.free_buf("probsT" + P, "mm")

        if kind == "win":
            gcolb, gslot = ctx["gates"][3], ctx["gates"][4]
            if self.debug and t == 0:
                @block.sync
                def _(s, EXPAC=EXPAC, RDV=RDV, PTDV=PTDV):
                    self.wt(s, "ac", EXPAC)
                    self.wt(s, "dv", PTDV)
                    self.inc(s.dma_start(self.dbg["d_probs"], probs[:]), "sDBG", 16)
                    self.inc(s.dma_start(self.dbg["d_pT"], probsT[:]), "sDBG", 16)
                self.free_buf("probs" + P, "sDBG")
                self.free_buf("probsT" + P, "sDBG")

            @block.vector
            def _(v, RDV=RDV, gslot=gslot):
                self.wt(v, "dv", RDV)
                self.wt(v, "dv", ctx["GDV"])
                self.inc(v.tensor_tensor(rsum[:], rsum[:], gcolb[:, gslot, 2:3], ALU.mult), "dv", 1)
            RG2 = self.at("dv")

            if self.debug and t == 0:
                @block.sync
                def _(s, RG2=RG2):
                    self.wt(s, "dv", RG2)
                    self.inc(s.dma_start(self.dbg["d_rsum"][:, 0:1], rsum[:], allow_slow_non_contiguous=True), "sDBG", 16)
                self.free_buf("rsum" + P, "sDBG")

            @block.scalar
            def _(sc, AVMM=AVMM, RG2=RG2):
                self.wt(sc, "mm", AVMM[0])
                self.wt(sc, "dv", RG2)
                self.use_buf(sc, "comb")
                sc.activation(comb[:, 0:512], self.psE[:], AF.Identity, scale=rsum[:, 0:1])
                self.inc(sc.activation(comb[:, 512:1024], self.psF[:], AF.Identity, scale=rsum[:, 0:1]), "ac", 1)
            self.free_buf("rsum" + P, "ac")
            self.free_bank("E", "ac")
            self.free_bank("F", "ac")
            self.comb_ac = self.at("ac")
        else:
            @block.vector
            def _(v, RDV=RDV, AVMM=AVMM, flag_col=flag_col):
                self.wt(v, "dv", RDV)
                self.wt(v, "mm", AVMM[0])
                self.use_buf(v, "oebuf")
                self.vsync(v, v.tensor_tensor(rsum[:], rsum[:], self.flg[:, flag_col:flag_col + 1], ALU.mult))
                v.tensor_scalar(obuf[:, 0:512], self.psE[:], rsum[:, 0:1], None, ALU.mult)
                self.inc(v.tensor_scalar(obuf[:, 512:1024], self.psF[:], rsum[:, 0:1], None, ALU.mult), "dv", 1)
            ODV = self.at("dv")
            self.free_buf("rsum" + P, "dv")
            self.free_bank("E", "dv")
            self.free_bank("F", "dv")
            dst = self.sosel[t] if kind == "sel" else self.socomp[t]
            stn = "sOSEL" if kind == "sel" else "sOCMP"

            @block.sync
            def _(s, dst=dst, ODV=ODV):
                self.wt(s, "dv", ODV)
                self.inc(s.dma_start(dst, obuf[:]), stn, 16)
            self.free_buf("oebuf", stn)

    def p5_main(self, ctx):
        nc, di, block = self.nc, self.di, self.block
        E = ctx.enter_context
        kwin = E(nc.sbuf_tensor("kwin", [128, 8, 1536], BF16))
        vwin = E(nc.sbuf_tensor("vwin", [128, 12, D], BF16))
        wq_s = E(nc.sbuf_tensor("wq_s5", [128, 8, D], BF16))
        wo_s = E(nc.sbuf_tensor("wo_s", [128, 8, D], BF16))
        wg_s = E(nc.sbuf_tensor("wg_s", [128, 8, 3], BF16))
        fkq = E(nc.sbuf_tensor("fkq", [128, 8, D], FP8))
        fbbc = E(nc.sbuf_tensor("fbbc", [128, D], BF16))
        xwb = E(nc.sbuf_tensor("xwb", [128, 8, 2, 512], BF16))
        xob = E(nc.sbuf_tensor("xob5", [128, 8, 512], BF16))
        qfm = E(nc.sbuf_tensor("qfm5", [128, 8, 512], BF16))
        mbuf = E(nc.sbuf_tensor("mbuf5", [128, 2, 640], BF16))
        probs = E(nc.sbuf_tensor("probs5", [128, 1, 640], BF16))
        probsT = E(nc.sbuf_tensor("probsT5", [128, 1, 5, 128], BF16))
        negmax = E(nc.sbuf_tensor("negmax5", [128, 1], F32))
        rowsum = E(nc.sbuf_tensor("rowsum5", [128, 1], F32))
        rsum = E(nc.sbuf_tensor("rsum5", [128, 1], F32))
        gcolb = E(nc.sbuf_tensor("gcolb", [128, 2, 3], F32))
        gex = E(nc.sbuf_tensor("gex", [128, 3], F32))
        comb = E(nc.sbuf_tensor("comb", [128, D], F32))
        combb = E(nc.sbuf_tensor("combb", [128, D], BF16))
        combT = E(nc.sbuf_tensor("combT", [128, 8, 128], BF16))
        oselb = E(nc.sbuf_tensor("oselb", [128, 2, D], BF16))
        ocompb = E(nc.sbuf_tensor("ocompb", [128, 2, D], BF16))
        xot = E(nc.sbuf_tensor("xot", [128, 2, D], F32))
        ztile = E(nc.sbuf_tensor("ztile", [128, D], F32))
        zscr = E(nc.sbuf_tensor("zscr", [128, D], BF16))
        y1 = E(nc.sbuf_tensor("y1", [128, D], F32))
        y1q = E(nc.sbuf_tensor("y1q", [128, 8, 128], FP8))
        ffnb = E(nc.sbuf_tensor("ffnb", [128, D], BF16))
        cols = E(nc.sbuf_tensor("cols", [128, 4], F32))
        lnw = E(nc.sbuf_tensor("lnw", [128, D], BF16))
        self.lnw = lnw
        self.fr = dict(mbuf=mbuf, probs=probs, probsT=probsT, negmax=negmax,
                       rowsum=rowsum, rsum=rsum, npar=1)

        @block.gpsimd
        def _(g):
            self.fence(g)
            fm128 = lambda ap: ap.rearrange("(c p) d -> p c d", p=128)
            for dst, src in [
                (wq_s[:], fm128(di["wqT"])),
                (wo_s[:], fm128(di["woT"])),
                (wg_s[:], fm128(di["wgT"])),
            ]:
                self.inc(g.dma_start(dst, src), "sW5", 16)
        WLD = ("sW5", self.at("sW5"))

        @block.sync
        def _(s):
            self.fence(s)
            self.wt(s, "sOSEL")
            self.wt(s, "sOCMP")
            self.inc(s.dma_start(xwb[:].rearrange("p c a b -> p c (a b)"),
                                 di["fker"].rearrange("(c p) d -> p c d", p=128)), "sFK", 16)
            self.inc(s.dma_start(fbbc[:], di["fbias_bc"]), "sFK", 16)
        FKLD = self.at("sFK")

        @block.vector
        def _(v, FKLD=FKLD):
            self.wt(v, "sFK", FKLD)
            for j in range(8):
                self.inc(v.tensor_scalar(fkq[:, j, :], xwb[:, j, :, :].rearrange("p a b -> p (a b)"),
                                         self.scal[:, 2:3], None, ALU.mult), "dv", 1)
        self.FKQDV = self.at("dv")

        xw_free = [0, 0]
        self.kvw_use = 0

        for TH, (T0, T1) in enumerate(THIRDS):
            for c in range(3):
                slot = c % 2

                @block.gpsimd
                def _(g, T0=T0, c=c, slot=slot):
                    self.wt(g, "dv", self.FKQDV)
                    if xw_free[slot]:
                        g.wait_ge(self.h["mm"], xw_free[slot])
                    src = di["xwinT"].rearrange("(c p) t -> p c t", p=128)
                    base = 128 * T0 + c * 512
                    for half in range(2):
                        self.inc(g.dma_start(xwb[:, half * 4:half * 4 + 4, slot, :],
                                             src[:, half * 4:half * 4 + 4, base:base + 512]),
                                 "sXW%d" % slot, 16)
                XWLD = self.at("sXW%d" % slot)

                KMM0 = [0]

                @block.tensor
                def _(t, c=c, slot=slot, XWLD=XWLD, KMM0=KMM0, KVW=self.kvw_use,
                      KDBG=getattr(self, "kvw_dbg", None)):
                    self.wt(t, "sXW%d" % slot, XWLD)
                    if c == 0:
                        self.wt(t, WLD[0], WLD[1])
                        if KVW:
                            t.wait_ge(self.h["mm"], KVW)
                        if KDBG:
                            self.wt(t, KDBG[0], KDBG[1])
                    first = True
                    AC0 = self.at("ac")
                    for cc in range(8):
                        bank = "AB"[cc % 2]
                        self.use_bank(t, bank)
                        pb = (self.psA, self.psB)[cc % 2]
                        for dj in range(8):
                            mm = t.matmul(pb[:], self.wk_s[:, dj, cc * 128:(cc + 1) * 128],
                                          xwb[:, dj, slot, :], start=(dj == 0), stop=(dj == 7))
                        v_ = self.inc(mm, "mm", 1)
                        if first:
                            KMM0[0] = v_ - 1
                            first = False
                        self.free_bank_at(bank, "ac", AC0 + cc + 1)

                @block.scalar
                def _(sc, c=c, KMM0=KMM0):
                    for cc in range(8):
                        pb = (self.psA, self.psB)[cc % 2]
                        sc.wait_ge(self.h["mm"], KMM0[0] + cc + 1)
                        self.inc(sc.activation(kwin[:, cc, c * 512:(c + 1) * 512], pb[:],
                                               AF.Identity, bias=self.bk_c[:, cc:cc + 1]), "ac", 1)

                VMM0 = [0]

                @block.tensor
                def _(t, slot=slot, VMM0=VMM0):
                    first = True
                    AC0 = self.at("ac")
                    for s4 in range(4):
                        for half in range(2):
                            bank = "CD"[half]
                            self.use_bank(t, bank)
                            pb = (self.psC, self.psD)[half]
                            for dj in range(8):
                                t.matmul(pb[:], xwb[:, dj, slot, s4 * 128:(s4 + 1) * 128],
                                         self.wv_s[:, dj, half * 512:(half + 1) * 512],
                                         start=(dj == 0), stop=False)
                            mm = t.matmul(pb[:], self.onesb[:],
                                          self.bvrow[:, half * 512:(half + 1) * 512],
                                          start=False, stop=True)
                            v_ = self.inc(mm, "mm", 1)
                            if first:
                                VMM0[0] = v_ - 1
                                first = False
                            self.free_bank_at(bank, "ac", AC0 + s4 * 2 + half + 1)
                xw_free[slot] = self.at("mm")

                @block.scalar
                def _(sc, c=c, VMM0=VMM0):
                    for s4 in range(4):
                        for half in range(2):
                            pb = (self.psC, self.psD)[half]
                            sc.wait_ge(self.h["mm"], VMM0[0] + s4 * 2 + half + 1)
                            self.inc(sc.copy(vwin[:, c * 4 + s4, half * 512:(half + 1) * 512], pb[:]), "ac", 1)
            KVW_AC = self.at("ac")

            if self.debug and TH == 0:
                @block.sync
                def _(s, KVW_AC=KVW_AC):
                    self.wt(s, "ac", KVW_AC)
                    self.inc(s.dma_start(self.dbg["d_kw"], kwin[:]), "sDBG", 16)
                    self.inc(s.dma_start(self.dbg["d_vw"], vwin[:]), "sDBG", 16)
                self.kvw_dbg = ("sDBG", self.at("sDBG"))

            pend = [None]

            def p5_front(t):
                ti = t % 4
                slot = t % 2
                OSLD = OCLD = 0
                if t < NSEL_T:
                    @block.sync
                    def _(s, t=t, slot=slot):
                        self.wt(s, "dv", self.FKQDV)
                        self.use_buf(s, "oselb%d" % slot)
                        self.inc(s.dma_start(oselb[:, slot, :], self.sosel[t]), "sOSL%d" % slot, 16)
                    OSLD = self.at("sOSL%d" % slot)
                if t < NCOMP_T:
                    @block.sync
                    def _(s, t=t, slot=slot):
                        self.use_buf(s, "ocompb%d" % slot)
                        self.inc(s.dma_start(ocompb[:, slot, :], self.socomp[t]),
                                 "sOCB%d" % slot, 16)
                    OCLD = self.at("sOCB%d" % slot)

                @block.sync
                def _(s, t=t, slot=slot):
                    self.use_buf(s, "xot%d" % slot)
                    self.inc(s.dma_start(xot[:, slot, :], di["xown"][t * 128:(t + 1) * 128, :]),
                             "sXOT%d" % slot, 16)
                XOTLD = self.at("sXOT%d" % slot)
                ctx = self.front_a(t, qfm, ti, kwin, S, di["mwin"], kind="win",
                                   win_third=T0, gates=(xob, wg_s, gex, gcolb, slot))
                ctx.update(OSLD=OSLD, OCLD=OCLD, XOTLD=XOTLD, slot=slot)
                return ctx

            for t in range(T0, T1):
                g, ti = t // 4, t % 4
                if pend[0] is None:
                    if ti == 0:
                        self.emit_qproj(di["xownT"], g, xob, qfm, wq_s, WLD)
                    pend[0] = p5_front(t)
                ctx = pend[0]
                slot = ctx["slot"]
                OSLD, OCLD, XOTLD = ctx["OSLD"], ctx["OCLD"], ctx["XOTLD"]
                GDV = ctx["GDV"]

                if self.debug and t < 4:
                    @block.sync
                    def _(s, t=t, GDV=GDV, slot=slot):
                        self.wt(s, "dv", GDV)
                        self.inc(s.dma_start(self.dbg["d_gate"][:, t, :], gcolb[:, slot, :]), "sDBG", 16)
                    self.free_buf("gcol%d" % slot, "sDBG")

                self.front_b(ctx, vwin, comb=comb)
                self.kvw_use = self.at("mm")

                if PIPELINE and t + 1 < T1:
                    if (t + 1) % 4 == 0:
                        self.emit_qproj(di["xownT"], (t + 1) // 4, xob, qfm, wq_s, WLD)
                    pend[0] = p5_front(t + 1)
                else:
                    pend[0] = None

                @block.vector
                def _(v, t=t, slot=slot, OSLD=OSLD, OCLD=OCLD):
                    self.wt(v, "ac", self.comb_ac)
                    self.use_buf(v, "combb")
                    if t < NSEL_T:
                        self.wt(v, "sOSL%d" % slot, OSLD)
                        v.scalar_tensor_tensor(comb[:], oselb[:, slot, :], gcolb[:, slot, 1:2],
                                               comb[:], ALU.mult, ALU.add)
                    if t < NCOMP_T:
                        self.wt(v, "sOCB%d" % slot, OCLD)
                        v.scalar_tensor_tensor(comb[:], ocompb[:, slot, :], gcolb[:, slot, 0:1],
                                               comb[:], ALU.mult, ALU.add)
                    self.inc(v.tensor_copy(combb[:], comb[:]), "dv", 1)
                CBDV = self.at("dv")
                self.free_buf("comb", "dv")
                self.free_buf("gcol%d" % slot, "dv")
                if t < NSEL_T:
                    self.free_buf("oselb%d" % slot, "dv")
                if t < NCOMP_T:
                    self.free_buf("ocompb%d" % slot, "dv")

                if self.debug and t < 4:
                    @block.sync
                    def _(s, t=t, CBDV=CBDV):
                        self.wt(s, "dv", CBDV)
                        self.inc(s.dma_start(self.dbg["d_comb"][t], comb[:]), "sDBG", 16)
                    self.free_buf("comb", "sDBG")

                CTMM = [0]

                @block.tensor
                def _(tt, CBDV=CBDV, CTMM=CTMM):
                    self.wt(tt, "dv", CBDV)
                    self.use_bank(tt, "T")
                    for dj in range(8):
                        mm = tt.transpose(self.psT[:, dj, :], combb[:, dj * 128:(dj + 1) * 128], self.idb[:])
                    CTMM[0] = self.inc(mm, "mm", 1)
                    self.free_bank_at("T", "dv", self.at("dv") + 1)
                self.free_buf("combb", "mm")

                @block.vector
                def _(v, CTMM=CTMM):
                    self.wt(v, "mm", CTMM[0])
                    self.use_buf(v, "combT")
                    self.inc(v.tensor_copy(combT[:], self.psT[:]), "dv", 1)
                CTDV = self.at("dv")

                ZDV = 0
                for half in range(2):
                    WOMM = [0]

                    @block.tensor
                    def _(tt, half=half, CTDV=CTDV, WOMM=WOMM):
                        self.wt(tt, "dv", CTDV)
                        self.use_bank(tt, "G")
                        for dj in range(8):
                            tt.matmul(self.psG[:], combT[:, dj, :],
                                      wo_s[:, dj, half * 512:(half + 1) * 512],
                                      start=(dj == 0), stop=False)
                        mm = tt.matmul(self.psG[:], self.onesb[:],
                                       self.borow[:, half * 512:(half + 1) * 512],
                                       start=False, stop=True)
                        WOMM[0] = self.inc(mm, "mm", 1)

                    @block.vector
                    def _(v, half=half, slot=slot, WOMM=WOMM, XOTLD=XOTLD):
                        self.wt(v, "mm", WOMM[0])
                        self.wt(v, "sXOT%d" % slot, XOTLD)
                        if half == 0:
                            self.use_buf(v, "ztile")
                        self.inc(v.tensor_tensor(ztile[:, half * 512:(half + 1) * 512], self.psG[:],
                                                 xot[:, slot, half * 512:(half + 1) * 512], ALU.add), "dv", 1)
                    self.free_bank("G", "dv")
                    ZDV = self.at("dv")
                self.free_buf("combT", "mm")
                self.free_buf("xot%d" % slot, "dv")

                if self.debug and t < 4:
                    @block.sync
                    def _(s, t=t, ZDV=ZDV):
                        self.wt(s, "dv", ZDV)
                        self.inc(s.dma_start(self.dbg["d_z"][t], ztile[:]), "sDBG", 16)
                    self.free_buf("ztile", "sDBG")

                self.emit_ln(ztile, y1, zscr, cols, ZDV, dst_buf="y1ot")
                Y1DV = self.at("dv")

                if self.debug and t < 4:
                    @block.sync
                    def _(s, t=t, Y1DV=Y1DV):
                        self.wt(s, "dv", Y1DV)
                        self.inc(s.dma_start(self.dbg["d_y1"][t], y1[:]), "sDBG", 16)

                @block.vector
                def _(v, Y1DV=Y1DV):
                    self.wt(v, "dv", Y1DV)
                    self.inc(v.tensor_copy(zscr[:], y1[:]), "dv", 1)
                Y1BDV = self.at("dv")

                FTMM = [0]

                @block.tensor
                def _(tt, Y1BDV=Y1BDV, FTMM=FTMM):
                    self.wt(tt, "dv", Y1BDV)
                    self.use_bank(tt, "T")
                    for dj in range(8):
                        mm = tt.transpose(self.psT[:, dj, :], zscr[:, dj * 128:(dj + 1) * 128], self.idb[:])
                    FTMM[0] = self.inc(mm, "mm", 1)
                    self.free_bank_at("T", "dv", self.at("dv") + 1)

                @block.vector
                def _(v, FTMM=FTMM):
                    self.wt(v, "mm", FTMM[0])
                    self.use_buf(v, "y1q")
                    self.inc(v.tensor_scalar(y1q[:].rearrange("p c d -> p (c d)"),
                                             self.psT[:].rearrange("p c d -> p (c d)"),
                                             self.scal[:, 0:1], None, ALU.mult), "dv", 1)
                YQDV = self.at("dv")

                for half in range(2):
                    FFMM = [0]

                    @block.tensor
                    def _(tt, half=half, YQDV=YQDV, FFMM=FFMM):
                        self.wt(tt, "dv", YQDV)
                        self.use_bank(tt, "G")
                        for dj in range(8):
                            mm = tt.matmul(self.psG[:], y1q[:, dj, :],
                                           fkq[:, dj, half * 512:(half + 1) * 512],
                                           start=(dj == 0), stop=(dj == 7))
                        FFMM[0] = self.inc(mm, "mm", 1)

                    @block.scalar
                    def _(sc, half=half, FFMM=FFMM):
                        self.wt(sc, "mm", FFMM[0])
                        if half == 0:
                            self.use_buf(sc, "ffnb")
                        self.inc(sc.activation(ffnb[:, half * 512:(half + 1) * 512], self.psG[:],
                                               AF.Identity, scale=self.scal[:, 1:2]), "ac", 1)
                    self.free_bank("G", "ac")
                FFAC = self.at("ac")
                self.free_buf("y1q", "mm")

                @block.vector
                def _(v, FFAC=FFAC):
                    self.wt(v, "ac", FFAC)
                    v.tensor_tensor(ffnb[:], ffnb[:], fbbc[:], ALU.add)
                    self.inc(v.tensor_tensor(ztile[:], y1[:], ffnb[:], ALU.add), "dv", 1)
                Z2DV = self.at("dv")
                self.free_buf("ffnb", "dv")

                self.emit_ln(ztile, y1, zscr, cols, Z2DV, dst_buf="y1ot")
                ODV = self.at("dv")
                self.free_buf("ztile", "dv")

                @block.sync
                def _(s, t=t, ODV=ODV):
                    self.wt(s, "dv", ODV)
                    self.inc(s.dma_start(self.out[t * 128:(t + 1) * 128, :], y1[:]), "sOUT", 16)
                self.free_buf("y1ot", "sOUT")

    def emit_ln(self, src, dst, scr, cols, SRCDV, dst_buf=None):
        block = self.block

        @block.scalar
        def _(sc, SRCDV=SRCDV):
            self.wt(sc, "dv", SRCDV)
            self.use_buf(sc, "cols")
            self.inc(sc.activation(self.lnw[:], src[:], AF.Identity, accum_out=cols[:, 0:1]), "ac", 1)
        SAC = self.at("ac")

        @block.vector
        def _(v, SAC=SAC):
            self.wt(v, "ac", SAC)
            self.vsync(v, v.scalar_tensor_tensor(scr[:], src[:], self.cons[:, ONEC:ONEC + 1],
                                                 src[:], ALU.mult, ALU.mult,
                                                 accum_out=cols[:, 1:2]))
            self.vsync(v, v.tensor_scalar(cols[:, 2:3], cols[:, 0:1],
                                          self.cons[:, INV1024:INV1024 + 1], None, ALU.mult))
            self.vsync(v, v.tensor_tensor(cols[:, 0:1], cols[:, 2:3], cols[:, 2:3], ALU.mult))
            self.inc(v.tensor_scalar(cols[:, 3:4], cols[:, 1:2], self.cons[:, INV1024:INV1024 + 1],
                                     cols[:, 0:1], ALU.mult, ALU.subtract), "dv", 1)
        VDV = self.at("dv")

        @block.scalar
        def _(sc, VDV=VDV):
            self.wt(sc, "dv", VDV)
            c = self.inc(sc.activation(cols[:, 1:2], cols[:, 3:4], AF.Ln,
                                       bias=self.cons[:, EPSC:EPSC + 1]), "acs", 1)
            sc.wait_ge(self.h["acs"], c)
            self.inc(sc.activation(cols[:, 3:4], cols[:, 1:2], AF.Exp,
                                   scale=self.cons[:, NEGHALF:NEGHALF + 1]), "ac", 1)
        RAC = self.at("ac")

        @block.vector
        def _(v, RAC=RAC, dst_buf=dst_buf):
            self.wt(v, "ac", RAC)
            if dst_buf:
                self.use_buf(v, dst_buf)
            self.inc(v.tensor_scalar(dst[:], src[:], cols[:, 2:3], cols[:, 3:4],
                                     ALU.subtract, ALU.mult), "dv", 1)
        self.free_buf("cols", "dv")

    def finish(self):
        block = self.block

        @block.sync
        def _(s):
            for n in ("sSSC", "sIDX", "sOSEL", "sOCMP", "sOUT", "sDBG"):
                if n in self.c and self.c[n]:
                    s.wait_ge(self.h[n], self.c[n])


def build_nc(debug=False):
    b = Builder(debug=debug)
    b.msl = 0
    b.scpar = 0
    nc = b.build()
    return nc, b


def kernel(L0_attn_x, L0_attn_w_qkv, L0_attn_b_qkv, L0_attn_w_comp, L0_attn_b_comp,
           L0_attn_w_score, L0_attn_b_score, L0_attn_w_gate, L0_attn_b_gate,
           L0_attn_w_o, L0_attn_b_o, L0_ffn_kernel, L0_ffn_input_scale,
           L0_ffn_kernel_scale, L0_ffn_bias):
    x = np.asarray(L0_attn_x)
    w = {
        "w_qkv": np.asarray(L0_attn_w_qkv), "b_qkv": np.asarray(L0_attn_b_qkv),
        "w_comp": np.asarray(L0_attn_w_comp), "b_comp": np.asarray(L0_attn_b_comp),
        "w_score": np.asarray(L0_attn_w_score), "b_score": np.asarray(L0_attn_b_score),
        "w_gate": np.asarray(L0_attn_w_gate), "b_gate": np.asarray(L0_attn_b_gate),
        "w_o": np.asarray(L0_attn_w_o), "b_o": np.asarray(L0_attn_b_o),
        "fker": np.asarray(L0_ffn_kernel), "s_in": np.asarray(L0_ffn_input_scale),
        "s_k": np.asarray(L0_ffn_kernel_scale), "fbias": np.asarray(L0_ffn_bias),
    }
    nc, _ = build_nc(debug=False)
    in_maps = [stage_core(x, w, core) for core in range(8)]
    res = run_bass_kernel_spmd(nc, in_maps, core_ids=list(range(8)))
    out = np.empty((4, S, D), np.float32)
    for core in range(8):
        b, h = core // 2, core % 2
        out[b, h * CQ:(h + 1) * CQ] = res.results[core]["out"]
    return out


# version 09 (final)
# speedup vs baseline: 1.0125x; 1.0125x over previous
import numpy as np
import ml_dtypes
from contextlib import ExitStack

import concourse.bass as bass
import concourse.mybir as mybir
from concourse.bass_utils import run_bass_kernel_spmd

F32 = mybir.dt.float32
BF16 = mybir.dt.bfloat16
FP8 = mybir.dt.float8e4
U32 = mybir.dt.uint32
AF = mybir.ActivationFunctionType
ALU = mybir.AluOpType
AX = mybir.AxisListType

D = 1024
S = 8192
CQ = 4096
WIN = 256
NTILE = 32
NSEL_T = 18
NCOMP_T = 4
THIRDS = [(0, 8), (8, 16), (16, 24), (24, 32)]
NEG = -1.0e30
bfl = ml_dtypes.bfloat16
PIPELINE = True
JCOL, PAR1000, NEG1, NEGHALF, EPSC, ONEC, INV1024, EIGHTH, BIGC = range(9)


def sel_window(t, nkv):
    lo = max(0, 128 * t - 256)
    hi = min(nkv, 128 * t + 128 + 256)
    return lo, hi - lo


def _win_mask(t, B):
    r = np.arange(128)[:, None]
    c = np.arange(640)[None, :]
    j = B + 128 * t - 256 + c
    i = B + 128 * t + r
    valid = (j >= i - WIN) & (j <= i + WIN) & (j >= 0) & (j < S)
    return np.where(valid, 0.0, NEG).astype(bfl)


def _sel_mask(t, kv_len):
    lo, n = sel_window(t, kv_len)
    r = np.arange(128)[:, None]
    c = np.arange(640)[None, :]
    j = lo + c
    i = 128 * t + r
    valid = (j >= i - WIN) & (j <= i + WIN) & (c < n)
    return np.where(valid, 0.0, NEG).astype(bfl)


def stage_core(x, w, core):
    b, h = core // 2, core % 2
    B = h * CQ
    xb = x[b]
    st = {}
    st["xT"] = np.ascontiguousarray(xb.T)
    wt = np.zeros((D, 5120), np.float32)
    lo, hi = B - 256, B + 4864
    slo, shi = max(lo, 0), min(hi, S)
    wt[:, slo - lo:shi - lo] = xb[slo:shi].T
    st["xwinT"] = wt
    st["xownT"] = np.ascontiguousarray(xb[B:B + CQ].T)
    st["xown"] = np.ascontiguousarray(xb[B:B + CQ])
    st["xtm"] = xb

    wq3 = w["w_qkv"].reshape(16, 3, 64, D)
    for nm, idx in (("wq", 0), ("wk", 1), ("wv", 2)):
        m = np.ascontiguousarray(wq3[:, idx].reshape(D, D))
        st[nm + "T"] = np.ascontiguousarray(m.T)
        if nm != "wq":
            st[nm + "nat"] = m
    bq3 = w["b_qkv"].reshape(16, 3, 64)
    fm = lambda v: np.ascontiguousarray(v.reshape(8, 128).T)
    st["bq_fm"] = fm(bq3[:, 0].reshape(D))
    st["bk_fm"] = fm(bq3[:, 1].reshape(D))
    st["bv_fm"] = fm(bq3[:, 2].reshape(D))
    st["bv_row"] = bq3[:, 2].reshape(1, D).copy()
    st["ws_fm"] = fm(w["w_score"][0])
    st["wgT"] = np.ascontiguousarray(w["w_gate"].T)
    st["bg_row"] = w["b_gate"].reshape(1, 3).copy()
    st["woT"] = np.ascontiguousarray(w["w_o"].T)
    st["bo_row"] = w["b_o"].reshape(1, D).copy()
    st["wcT"] = np.ascontiguousarray(w["w_comp"].T)
    st["bc_row"] = w["b_comp"].reshape(1, D).copy()
    st["fker"] = np.ascontiguousarray(w["fker"])
    st["fbias_bc"] = np.broadcast_to(np.asarray(w["fbias"]).reshape(1, D), (128, D)).copy()
    sc2 = np.array([[float(np.asarray(w["s_in"]).reshape(-1)[0]),
                     float(np.asarray(w["s_k"]).reshape(-1)[0])]], np.float32)
    st["scales_bc"] = np.broadcast_to(sc2, (128, 2)).copy()

    st["mwin"] = np.stack([_win_mask(t, B) for t in range(NTILE)])
    st["msel"] = np.stack([_sel_mask(t, 2048) for t in range(NSEL_T)])
    st["mcomp"] = np.stack([_sel_mask(t, 256) for t in range(NCOMP_T)])
    one = np.ones((128, 1), np.float32)
    f = 1.0 if h == 0 else 0.0
    st["flags"] = np.concatenate([one * f, one * f], 1)
    st["identb"] = np.eye(128, dtype=bfl)
    p = np.arange(128, dtype=np.float32)[:, None]
    st["consts"] = np.concatenate([
        (p % 64) + 1000.0 * (p // 64), 1000.0 * (p % 2), -one, -0.5 * one,
        1e-5 * one, one, one / 1024.0, 0.125 * one, 1e5 * one,
        0 * one, 0 * one, 0 * one], 1)
    st["iota64"] = np.broadcast_to(np.arange(64, dtype=np.float32), (128, 64)).copy()
    st["ones_row"] = np.ones((1, 128), np.float32)
    return st


class Builder:
    def __init__(self, debug=False):
        self.debug = debug
        nc = self.nc = bass.Bass("TRN2", target_bir_lowering=False, debug=False,
                                 detect_race_conditions=False)
        self.di = {}
        for name, shape, dt in [
            ("xT", (D, S), F32), ("xwinT", (D, 5120), F32), ("xownT", (D, CQ), F32),
            ("xown", (CQ, D), F32), ("xtm", (S, D), F32),
            ("wqT", (D, D), F32), ("wkT", (D, D), F32), ("wvT", (D, D), F32),
            ("wknat", (D, D), F32), ("wvnat", (D, D), F32),
            ("bq_fm", (128, 8), F32), ("bk_fm", (128, 8), F32), ("bv_fm", (128, 8), F32),
            ("bv_row", (1, D), F32), ("ws_fm", (128, 8), F32),
            ("wgT", (D, 3), F32), ("bg_row", (1, 3), F32),
            ("woT", (D, D), F32), ("bo_row", (1, D), F32),
            ("wcT", (32 * D, D), F32), ("bc_row", (1, D), F32),
            ("fker", (D, D), BF16), ("fbias_bc", (128, D), BF16),
            ("scales_bc", (128, 2), F32),
            ("mwin", (NTILE, 128, 640), BF16), ("msel", (NSEL_T, 128, 640), BF16),
            ("mcomp", (NCOMP_T, 128, 640), BF16), ("flags", (128, 2), F32),
            ("identb", (128, 128), BF16),
            ("consts", (128, 12), F32), ("iota64", (128, 64), F32),
            ("ones_row", (1, 128), F32),
        ]:
            self.di[name] = nc.dram_tensor(name, list(shape), dt, kind="ExternalInput").ap()
        self.out = nc.dram_tensor("out", [CQ, D], F32, kind="ExternalOutput").ap()
        self.sscore = nc.dram_tensor("sscore", [2, S], F32, kind="ExternalOutput").ap()
        self.sidx = nc.dram_tensor("sidx", [2, 2048], F32, kind="ExternalOutput").ap()
        self.sosel = nc.dram_tensor("sosel", [NSEL_T, 128, D], BF16, kind="ExternalOutput").ap()
        self.socomp = nc.dram_tensor("socomp", [NCOMP_T, 128, D], BF16, kind="ExternalOutput").ap()
        self.dbg = {}
        if debug:
            for name, shape, dt in [
                ("d_ck", (128, 8, 256), BF16), ("d_cv", (128, 2, D), BF16),
                ("d_sk", (128, 8, 2048), BF16), ("d_sv", (128, 16, D), BF16),
                ("d_kw", (128, 8, 1536), BF16), ("d_vw", (128, 12, D), BF16),
                ("d_q", (128, 8, 512), BF16), ("d_gate", (128, 4, 3), F32),
                ("d_comb", (4, 128, D), F32), ("d_z", (4, 128, D), F32),
                ("d_probs", (128, 640), BF16), ("d_rsum", (128, 4), F32),
                ("d_pT", (128, 5, 128), BF16),
                ("d_y1", (4, 128, D), F32),
            ]:
                self.dbg[name] = nc.dram_tensor(name, list(shape), dt, kind="ExternalOutput").ap()
        self.bankfree = {}
        self.buffree = {}

    def init_sems(self, E):
        self._E = E
        self.h, self.c = {}, {}
        for n in ("mm", "dv", "ac"):
            self.h[n] = E(self.nc.semaphore("sem_" + n))
            self.c[n] = 0

    def sem(self, n):
        if n not in self.h:
            self.h[n] = self._E(self.nc.semaphore("sem_" + n))
            self.c[n] = 0
        return self.h[n]

    def inc(self, inst, n, k=1):
        self.sem(n)
        inst.then_inc(self.h[n], k)
        self.c[n] += k
        return self.c[n]

    def at(self, n):
        self.sem(n)
        return self.c[n]

    def wt(self, eng, n, v=None):
        self.sem(n)
        v = self.c[n] if v is None else v
        if v > 0:
            eng.wait_ge(self.h[n], v)

    def use_bank(self, eng, bank):
        if bank in self.bankfree:
            n, v = self.bankfree.pop(bank)
            if v > 0:
                eng.wait_ge(self.h[n], v)

    def free_bank(self, bank, n):
        self.bankfree[bank] = (n, self.c[n])

    def free_bank_at(self, bank, n, count):
        self.sem(n)
        self.bankfree[bank] = (n, count)

    def fence(self, eng):
        for n in ("mm", "dv", "ac"):
            self.wt(eng, n)

    def vsync(self, v, ins):
        c = self.inc(ins, "dvs", 1)
        v.wait_ge(self.h["dvs"], c)
        return ins

    def use_buf(self, eng, name):
        for n, v in self.buffree.pop(name, ()):
            if v > 0:
                eng.wait_ge(self.h[n], v)

    def free_buf(self, name, n):
        self.buffree.setdefault(name, []).append((n, self.c[n]))

    def build(self):
        nc = self.nc
        with ExitStack() as ctx:
            E = ctx.enter_context
            self.init_sems(E)

            def sbuf(name, shape, dt):
                return E(nc.sbuf_tensor(name, list(shape), dt))

            self.wk_s = sbuf("wk_s", (128, 8, D), BF16)
            self.wv_s = sbuf("wv_s", (128, 8, D), BF16)
            self.bq_c = sbuf("bq_c", (128, 8), F32)
            self.bk_c = sbuf("bk_c", (128, 8), F32)
            self.bv_c = sbuf("bv_c", (128, 8), F32)
            self.ws_s = sbuf("ws_s", (128, 8), F32)
            self.cons = sbuf("cons", (128, 12), F32)
            self.iota64 = sbuf("iota64_s", (128, 64), F32)
            self.flg = sbuf("flg", (128, 2), F32)
            self.idb = sbuf("idb", (128, 128), BF16)
            self.u2b = sbuf("u2b", (128, 8, 2), BF16)
            self.scal = sbuf("scal", (128, 4), F32)
            self.onesb = sbuf("onesb", (1, 128), BF16)
            self.bvrow = sbuf("bvrow", (1, D), BF16)
            self.bcrow = sbuf("bcrow", (1, D), BF16)
            self.borow = sbuf("borow", (1, D), BF16)
            self.bgrow = sbuf("bgrow", (1, 3), BF16)
            self.psA = E(nc.psum_tensor("psA", [128, 512], F32))
            self.psB = E(nc.psum_tensor("psB", [128, 512], F32))
            self.psC = E(nc.psum_tensor("psC", [128, 512], F32))
            self.psD = E(nc.psum_tensor("psD", [128, 512], F32))
            self.psE = E(nc.psum_tensor("psE", [128, 512], F32))
            self.psF = E(nc.psum_tensor("psF", [128, 512], F32))
            self.psT = E(nc.psum_tensor("psT", [128, 8, 128], BF16))
            self.psG = E(nc.psum_tensor("psG", [128, 512], F32))

            self.block = E(nc.Block())

            with ExitStack() as c14:
                self.ckfm = c14.enter_context(nc.sbuf_tensor("ckfm", [128, 8, 256], BF16))
                self.cvtm = c14.enter_context(nc.sbuf_tensor("cvtm", [128, 2, D], BF16))
                with ExitStack() as c0:
                    self.p0_setup(c0)
                with ExitStack() as c1:
                    self.p1_kv_compress(c1)
                with ExitStack() as c2:
                    self.p2_topk(c2)
                with ExitStack() as c3:
                    self.p34_sel(c3)
                self.P4MM = self.at("mm")
            with ExitStack() as c5:
                self.p5_main(c5)
            self.finish()
        return nc

    def p0_setup(self, ctx):
        nc, di, block = self.nc, self.di, self.block
        E = ctx.enter_context
        fm128 = lambda ap: ap.rearrange("(c p) d -> p c d", p=128)

        @block.sync
        def _(s):
            for dst, src in [
                (self.bq_c[:], di["bq_fm"]), (self.bk_c[:], di["bk_fm"]),
                (self.bv_c[:], di["bv_fm"]), (self.ws_s[:], di["ws_fm"]),
                (self.cons[:], di["consts"]), (self.flg[:], di["flags"]),
                (self.iota64[:], di["iota64"]),
                (self.idb[:], di["identb"]), (self.scal[:, 0:2], di["scales_bc"]),
            ]:
                self.inc(s.dma_start(dst, src), "sP0S", 16)

        @block.gpsimd
        def _(g):
            for dst, src in [
                (self.wk_s[:], fm128(di["wkT"])),
                (self.wv_s[:], fm128(di["wvT"])),
                (self.onesb[:], di["ones_row"]),
                (self.bvrow[:], di["bv_row"]), (self.bcrow[:], di["bc_row"]),
                (self.borow[:], di["bo_row"]), (self.bgrow[:], di["bg_row"]),
            ]:
                self.inc(g.dma_start(dst, src), "sP0G", 16)

        @block.vector
        def _(v):
            self.wt(v, "sP0S")
            self.wt(v, "sP0G")
            self.vsync(v, v.tensor_copy(self.scal[:, 2:3], self.scal[:, 1:2]))
            self.vsync(v, v.tensor_tensor(self.scal[:, 1:2], self.scal[:, 0:1], self.scal[:, 1:2], ALU.mult))
            self.vsync(v, v.reciprocal(self.scal[:, 0:1], self.scal[:, 0:1]))
            self.vsync(v, v.reciprocal(self.scal[:, 2:3], self.scal[:, 2:3]))
            v.tensor_scalar(self.bq_c[:], self.bq_c[:], self.cons[:, EIGHTH:EIGHTH + 1], None, ALU.mult)
            v.memset(self.ckfm[:], 0.0)
            self.inc(v.memset(self.cvtm[:], 0.0), "dv", 1)

        unat = E(nc.sbuf_tensor("unat", [128, 8, D], F32))
        u2f = E(nc.sbuf_tensor("u2f", [128, 16], F32))

        @block.sync
        def _(s):
            self.inc(s.dma_start(unat[:], fm128(di["wknat"])), "sUN", 16)
        UKLD = self.at("sUN")

        @block.tensor
        def _(t):
            self.wt(t, "sUN", UKLD)
            self.wt(t, "sP0S")
            for dj in range(8):
                for cj in range(8):
                    mm = t.matmul(self.psG[:, dj:dj + 1], unat[:, cj, dj * 128:(dj + 1) * 128],
                                  self.ws_s[:, cj:cj + 1], start=(cj == 0), stop=(cj == 7))
            self.inc(mm, "mm", 1)
        UKMM = self.at("mm")

        @block.vector
        def _(v):
            self.wt(v, "mm", UKMM)
            self.inc(v.tensor_copy(u2f[:, 0:8], self.psG[:, 0:8]), "dv", 1)
        UKEV = self.at("dv")

        @block.sync
        def _(s):
            self.wt(s, "dv", UKEV)
            self.inc(s.dma_start(unat[:], fm128(di["wvnat"])), "sUN", 16)
        UVLD = self.at("sUN")

        @block.tensor
        def _(t):
            self.wt(t, "sUN", UVLD)
            for dj in range(8):
                for cj in range(8):
                    mm = t.matmul(self.psG[:, 8 + dj:9 + dj], unat[:, cj, dj * 128:(dj + 1) * 128],
                                  self.ws_s[:, cj:cj + 1], start=(cj == 0), stop=(cj == 7))
            self.inc(mm, "mm", 1)
        UVMM = self.at("mm")

        @block.vector
        def _(v):
            self.wt(v, "mm", UVMM)
            v.tensor_copy(u2f[:, 8:16], self.psG[:, 8:16])
            v.tensor_copy(self.u2b[:, :, 0:1], u2f[:, 0:8].rearrange("p (c o) -> p c o", o=1))
            self.inc(v.tensor_copy(self.u2b[:, :, 1:2], u2f[:, 8:16].rearrange("p (c o) -> p c o", o=1)), "dv", 1)
        self.free_bank("G", "dv")
        self.U2B = self.at("dv")

    def p1_kv_compress(self, ctx):
        nc, di, block = self.nc, self.di, self.block
        E = ctx.enter_context
        kfm = E(nc.sbuf_tensor("kfm", [128, 8, 4096], BF16))
        vfm = E(nc.sbuf_tensor("vfm", [128, 8, 4096], BF16))
        xb = E(nc.sbuf_tensor("xb", [128, 8, 2, 512], BF16))
        wcb = E(nc.sbuf_tensor("wcb", [128, 4, D], BF16))
        ssc = E(nc.sbuf_tensor("ssc", [2, 512], F32))
        cktmp = E(nc.sbuf_tensor("cktmp", [128, 128], BF16))

        xb_free = [0, 0]
        wc_free = [0] * 4
        ssc_free = [0]

        for SC in range(2):
            for ch in range(8):
                gc = SC * 8 + ch
                slot = gc % 2

                @block.gpsimd
                def _(g, gc=gc, slot=slot):
                    if xb_free[slot]:
                        g.wait_ge(self.h["mm"], xb_free[slot])
                    src = di["xT"].rearrange("(c p) t -> p c t", p=128)
                    for half in range(2):
                        self.inc(g.dma_start(xb[:, half * 4:half * 4 + 4, slot, :],
                                             src[:, half * 4:half * 4 + 4, gc * 512:(gc + 1) * 512]),
                                 "sXB%d" % slot, 16)
                XLD = self.at("sXB%d" % slot)

                KVMM0 = [0]

                @block.tensor
                def _(t, gc=gc, slot=slot, XLD=XLD, KVMM0=KVMM0):
                    self.wt(t, "sXB%d" % slot, XLD)
                    if gc == 0:
                        self.wt(t, "dv", self.U2B)
                    self.use_bank(t, "G")
                    if ssc_free[0]:
                        t.wait_ge(self.h["sSSC"], ssc_free[0])
                    for dj in range(8):
                        mm = t.matmul(self.psG[0:2, :], self.u2b[:, dj, :], xb[:, dj, slot, :],
                                      start=(dj == 0), stop=(dj == 7))
                    KVMM0[0] = self.inc(mm, "mm", 1)
                    self.free_bank_at("G", "dv", self.at("dv") + 1)
                    DV0 = self.at("dv") + 1
                    for kind, wsb in enumerate((self.wk_s, self.wv_s)):
                        for cc in range(8):
                            bank = "ABCD"[kind * 2 + cc % 2]
                            self.use_bank(t, bank)
                            pb = (self.psA, self.psB, self.psC, self.psD)[kind * 2 + cc % 2]
                            for dj in range(8):
                                mm = t.matmul(pb[:], wsb[:, dj, cc * 128:(cc + 1) * 128],
                                              xb[:, dj, slot, :], start=(dj == 0), stop=(dj == 7))
                            self.inc(mm, "mm", 1)
                            self.free_bank_at(bank, "dv", DV0 + kind * 8 + cc + 1)
                xb_free[slot] = self.at("mm")

                @block.vector
                def _(v, KVMM0=KVMM0):
                    self.wt(v, "mm", KVMM0[0])
                    self.inc(v.tensor_copy(ssc[:], self.psG[0:2, :]), "dv", 1)
                SSC_DV = self.at("dv")

                @block.sync
                def _(s, gc=gc, SSC_DV=SSC_DV):
                    self.wt(s, "dv", SSC_DV)
                    self.inc(s.dma_start(self.sscore[:, gc * 512:(gc + 1) * 512], ssc[:]), "sSSC", 16)
                ssc_free[0] = self.at("sSSC")

                @block.vector
                def _(v, ch=ch, KVMM0=KVMM0):
                    for kind, (buf, bias) in enumerate(((kfm, self.bk_c), (vfm, self.bv_c))):
                        for cc in range(8):
                            pb = (self.psA, self.psB, self.psC, self.psD)[kind * 2 + cc % 2]
                            v.wait_ge(self.h["mm"], KVMM0[0] + kind * 8 + cc + 1)
                            self.inc(v.tensor_scalar(buf[:, cc, ch * 512:(ch + 1) * 512], pb[:],
                                                     bias[:, cc:cc + 1], None, ALU.add), "dv", 1)
            KV_EVAC = self.at("dv")

            wcld = []
            COMP0 = self.at("mm")

            @block.gpsimd
            def _(g, SC=SC, wcld=wcld, COMP0=COMP0):
                for w in range(256):
                    slot = w % 4
                    if w >= 4:
                        g.wait_ge(self.h["mm"], COMP0 + w - 3)
                    elif wc_free[slot]:
                        g.wait_ge(self.h["mm"], wc_free[slot])
                    src = di["wcT"].rearrange("(w p) c -> p w c", p=128)[:, w, :]
                    wcld.append((slot, self.inc(g.dma_start(wcb[:, slot, :], src), "sWC%d" % slot, 16)))

            @block.tensor
            def _(t, SC=SC, wcld=wcld, KV_EVAC=KV_EVAC):
                self.wt(t, "dv", KV_EVAC)
                for b_ in "ABCD":
                    self.use_bank(t, b_)
                for w in range(256):
                    slot = w % 4
                    p_, dj = w // 8, w % 8
                    t.wait_ge(self.h["sWC%d" % wcld[w][0]], wcld[w][1])
                    st_ = (w == 0)
                    kap = kfm[:, dj, p_::32]
                    vap = vfm[:, dj, p_::32]
                    t.matmul(self.psA[:], kap, wcb[:, slot, 0:512], start=st_, stop=False)
                    t.matmul(self.psB[:], kap, wcb[:, slot, 512:1024], start=st_, stop=False)
                    t.matmul(self.psC[:], vap, wcb[:, slot, 0:512], start=st_, stop=False)
                    mm = t.matmul(self.psD[:], vap, wcb[:, slot, 512:1024], start=st_, stop=False)
                    wc_free[slot] = self.inc(mm, "mm", 1)
                t.matmul(self.psA[:], self.onesb[:], self.bcrow[:, 0:512], start=False, stop=True)
                t.matmul(self.psB[:], self.onesb[:], self.bcrow[:, 512:1024], start=False, stop=True)
                t.matmul(self.psC[:], self.onesb[:], self.bcrow[:, 0:512], start=False, stop=True)
                mm = t.matmul(self.psD[:], self.onesb[:], self.bcrow[:, 512:1024], start=False, stop=True)
                self.inc(mm, "mm", 1)
            COMPMM = self.at("mm")

            @block.vector
            def _(v, SC=SC, COMPMM=COMPMM):
                self.wt(v, "mm", COMPMM)
                v.tensor_copy(self.cvtm[:, SC, 0:512], self.psC[:])
                self.inc(v.tensor_copy(self.cvtm[:, SC, 512:1024], self.psD[:]), "dv", 1)
            self.free_bank("C", "dv")
            self.free_bank("D", "dv")

            for half, pb in ((0, self.psA), (1, self.psB)):
                for q4 in range(4):
                    cc = half * 4 + q4

                    @block.vector
                    def _(v, pb=pb, q4=q4):
                        self.use_buf(v, "cktmp")
                        self.inc(v.tensor_copy(cktmp[:], pb[:, q4 * 128:(q4 + 1) * 128]), "dv", 1)
                    CKDV = self.at("dv")

                    @block.tensor
                    def _(t, CKDV=CKDV):
                        self.wt(t, "dv", CKDV)
                        self.use_bank(t, "T")
                        self.inc(t.transpose(self.psT[:, 0, :], cktmp[:], self.idb[:]), "mm", 1)
                        self.free_bank_at("T", "dv", self.at("dv") + 1)
                    self.free_buf("cktmp", "mm")
                    CKMM = self.at("mm")

                    @block.vector
                    def _(v, SC=SC, cc=cc, CKMM=CKMM):
                        self.wt(v, "mm", CKMM)
                        self.inc(v.tensor_copy(self.ckfm[:, cc, SC * 128:SC * 128 + 128],
                                               self.psT[:, 0, :]), "dv", 1)
            self.free_bank("A", "dv")
            self.free_bank("B", "dv")

        if self.debug:
            DBGDV = self.at("dv")

            @block.sync
            def _(s, DBGDV=DBGDV):
                self.wt(s, "dv", DBGDV)
                self.inc(s.dma_start(self.dbg["d_ck"], self.ckfm[:]), "sDBG", 16)
                self.inc(s.dma_start(self.dbg["d_cv"], self.cvtm[:]), "sDBG", 16)

    def p2_topk(self, ctx):
        nc, block = self.nc, self.block
        E = ctx.enter_context
        blk = E(nc.sbuf_tensor("blk", [128, 2, 64], F32))
        wrk = E(nc.sbuf_tensor("wrk", [128, 64], F32))
        prev = E(nc.sbuf_tensor("prev", [128, 64], F32))
        scr64 = E(nc.sbuf_tensor("scr64", [128, 64], F32))
        e1 = E(nc.sbuf_tensor("e1", [128, 64], F32))
        cand = E(nc.sbuf_tensor("cand", [128, 64], F32))
        t16v = E(nc.sbuf_tensor("t16v", [128, 16], F32))
        one8 = E(nc.sbuf_tensor("one8", [128, 8], F32))
        idxf = E(nc.sbuf_tensor("idxf", [128, 16], F32))

        @block.sync
        def _(s):
            self.wt(s, "sSSC")
            self.fence(s)
            for Ti in range(2):
                self.inc(s.dma_start(blk[:, Ti, :],
                                     self.sscore[Ti, :].rearrange("(p f) -> p f", p=128)), "sBLK", 16)
        BLKLD = self.at("sBLK")

        for Ti in range(2):
            @block.vector
            def _(v, Ti=Ti, BLKLD=BLKLD):
                self.wt(v, "sBLK", BLKLD)
                if Ti == 1:
                    self.use_buf(v, "idxf")
                self.vsync(v, v.tensor_copy(wrk[:], blk[:, Ti, :]))
                self.vsync(v, v.max(out=t16v[:, 0:8], in_=wrk[:]))
                self.vsync(v, v.match_replace(scr64[:], t16v[:, 0:8], wrk[:], NEG))
                self.vsync(v, v.max(out=t16v[:, 8:16], in_=scr64[:]))
                v.tensor_copy(prev[:], wrk[:])
                v.memset(one8[:], 1e30)
                for r in range(16):
                    self.vsync(v, v.tensor_copy(one8[:, 0:1], t16v[:, r:r + 1]))
                    self.vsync(v, v.match_replace(wrk[:], one8[:], wrk[:], NEG))
                    v.tensor_tensor(e1[:], wrk[:], prev[:], ALU.is_equal)
                    v.tensor_scalar(cand[:], e1[:], self.cons[:, BIGC:BIGC + 1], None, ALU.mult)
                    self.vsync(v, v.tensor_tensor(cand[:], cand[:], self.iota64[:], ALU.add))
                    self.vsync(v, v.tensor_reduce(idxf[:, r:r + 1], cand[:], AX.X, ALU.min))
                    v.tensor_copy(prev[:], wrk[:])
                self.inc(v.tensor_scalar(idxf[:], idxf[:], self.cons[:, PAR1000:PAR1000 + 1],
                                         None, ALU.add), "dv", 1)
            IDXDV = self.at("dv")

            @block.sync
            def _(s, Ti=Ti, IDXDV=IDXDV):
                self.wt(s, "dv", IDXDV)
                self.inc(s.dma_start(self.sidx[Ti, :].rearrange("(p f) -> p f", p=128), idxf[:]), "sIDX", 16)
            self.free_buf("idxf", "sIDX")
        self.IDXST = self.at("sIDX")

    def p34_sel(self, ctx):
        nc, di, block = self.nc, self.di, self.block
        E = ctx.enter_context
        skfm = E(nc.sbuf_tensor("skfm", [128, 8, 2048], BF16))
        svtm = E(nc.sbuf_tensor("svtm", [128, 16, D], BF16))
        xselK = E(nc.sbuf_tensor("xselK", [128, 8, 512], BF16))
        xselV = E(nc.sbuf_tensor("xselV", [128, 8, 512], BF16))
        bcidx = E(nc.sbuf_tensor("bcidx", [128, 2, 512], F32))
        xtmb = E(nc.sbuf_tensor("xtmb", [128, 4, D], BF16))
        oneh = E(nc.sbuf_tensor("oneh", [128, 2, 32], BF16))
        wq_s = E(nc.sbuf_tensor("wq_s4", [128, 8, D], BF16))
        xob = E(nc.sbuf_tensor("xob4", [128, 8, 512], BF16))
        qfm = E(nc.sbuf_tensor("qfm4", [128, 8, 512], BF16))

        @block.gpsimd
        def _(g):
            self.fence(g)
            self.inc(g.dma_start(wq_s[:],
                                 di["wqT"].rearrange("(c p) d -> p c d", p=128)), "sWQ4", 16)
        self.WQLD = ("sWQ4", self.at("sWQ4"))
        FENCED = [False]

        xtm_free = [0] * 4
        for g4 in range(4):
            @block.sync
            def _(s, g4=g4):
                if not FENCED[0]:
                    self.fence(s)
                    FENCED[0] = True
                self.wt(s, "sIDX", self.IDXST)
                self.use_buf(s, "bcidx")
                for Ti in range(2):
                    src = self.sidx[Ti:Ti + 1, g4 * 512:(g4 + 1) * 512].to_broadcast([128, 512])
                    self.inc(s.dma_start(bcidx[:, Ti, :], src), "sBC", 16)
            BCLD = self.at("sBC")

            for c16 in range(16):
                c64 = g4 * 16 + c16
                slot = c64 % 4

                @block.gpsimd
                def _(g, c64=c64, slot=slot):
                    if xtm_free[slot]:
                        g.wait_ge(self.h["mm"], xtm_free[slot])
                    src = di["xtm"].rearrange("(a p) d -> p a d", p=128)[:, c64, :]
                    self.inc(g.dma_start(xtmb[:, slot, :], src), "sXT%d" % slot, 16)
                XTLD = self.at("sXT%d" % slot)

                @block.vector
                def _(v, c16=c16, BCLD=BCLD):
                    self.wt(v, "sBC", BCLD)
                    self.use_buf(v, "oneh")
                    for Ti in range(2):
                        self.inc(v.tensor_scalar(oneh[:, Ti, :], bcidx[:, Ti, c16 * 32:(c16 + 1) * 32],
                                                 self.cons[:, JCOL:JCOL + 1], None, ALU.is_equal), "dv", 1)
                OHDV = self.at("dv")

                @block.tensor
                def _(t, slot=slot, XTLD=XTLD, OHDV=OHDV):
                    self.wt(t, "sXT%d" % slot, XTLD)
                    self.wt(t, "dv", OHDV)
                    for Ti, pb, bk in ((0, self.psE, "E"), (1, self.psF, "F")):
                        self.use_bank(t, bk)
                        for dj in range(8):
                            mm = t.matmul(pb[:, dj * 32:(dj + 1) * 32],
                                          xtmb[:, slot, dj * 128:(dj + 1) * 128],
                                          oneh[:, Ti, :], start=True, stop=True)
                        self.inc(mm, "mm", 1)
                        self.free_bank_at(bk, "dv", self.at("dv") + 1)
                xtm_free[slot] = self.at("mm")
                self.free_buf("oneh", "mm")
                GAMM = self.at("mm")

                @block.vector
                def _(v, c16=c16, GAMM=GAMM):
                    self.wt(v, "mm", GAMM)
                    v.tensor_copy(xselK[:, 0:8, c16 * 32:(c16 + 1) * 32],
                                  self.psE[:, 0:256].rearrange("p (c f) -> p c f", c=8))
                    self.inc(v.tensor_copy(xselV[:, 0:8, c16 * 32:(c16 + 1) * 32],
                                           self.psF[:, 0:256].rearrange("p (c f) -> p c f", c=8)), "dv", 1)
            self.free_buf("bcidx", "dv")
            XSELDV = self.at("dv")

            SKMM0 = [0]

            @block.tensor
            def _(t, XSELDV=XSELDV, SKMM0=SKMM0):
                self.wt(t, "dv", XSELDV)
                first = True
                AC0 = self.at("ac")
                for cc in range(8):
                    bank = "AB"[cc % 2]
                    self.use_bank(t, bank)
                    pb = (self.psA, self.psB)[cc % 2]
                    for dj in range(8):
                        mm = t.matmul(pb[:], self.wk_s[:, dj, cc * 128:(cc + 1) * 128],
                                      xselK[:, dj, :], start=(dj == 0), stop=(dj == 7))
                    v_ = self.inc(mm, "mm", 1)
                    if first:
                        SKMM0[0] = v_ - 1
                        first = False
                    self.free_bank_at(bank, "ac", AC0 + cc + 1)

            @block.scalar
            def _(sc, g4=g4, SKMM0=SKMM0):
                for cc in range(8):
                    pb = (self.psA, self.psB)[cc % 2]
                    sc.wait_ge(self.h["mm"], SKMM0[0] + cc + 1)
                    self.inc(sc.activation(skfm[:, cc, g4 * 512:(g4 + 1) * 512], pb[:],
                                           AF.Identity, bias=self.bk_c[:, cc:cc + 1]), "ac", 1)

            SVMM0 = [0]

            @block.tensor
            def _(t, SVMM0=SVMM0):
                first = True
                AC0 = self.at("ac")
                for s4 in range(4):
                    for half in range(2):
                        bank = "CD"[half]
                        self.use_bank(t, bank)
                        pb = (self.psC, self.psD)[half]
                        for dj in range(8):
                            t.matmul(pb[:], xselV[:, dj, s4 * 128:(s4 + 1) * 128],
                                     self.wv_s[:, dj, half * 512:(half + 1) * 512],
                                     start=(dj == 0), stop=False)
                        mm = t.matmul(pb[:], self.onesb[:], self.bvrow[:, half * 512:(half + 1) * 512],
                                      start=False, stop=True)
                        v_ = self.inc(mm, "mm", 1)
                        if first:
                            SVMM0[0] = v_ - 1
                            first = False
                        self.free_bank_at(bank, "ac", AC0 + s4 * 2 + half + 1)

            @block.scalar
            def _(sc, g4=g4, SVMM0=SVMM0):
                for s4 in range(4):
                    for half in range(2):
                        pb = (self.psC, self.psD)[half]
                        sc.wait_ge(self.h["mm"], SVMM0[0] + s4 * 2 + half + 1)
                        self.inc(sc.copy(svtm[:, g4 * 4 + s4, half * 512:(half + 1) * 512], pb[:]), "ac", 1)
        P3AC = self.at("ac")

        if self.debug:
            @block.sync
            def _(s, P3AC=P3AC):
                self.wt(s, "ac", P3AC)
                self.inc(s.dma_start(self.dbg["d_sk"], skfm[:]), "sDBG", 16)
                self.inc(s.dma_start(self.dbg["d_sv"], svtm[:]), "sDBG", 16)

        mbuf = E(nc.sbuf_tensor("mbuf4", [128, 2, 640], BF16))
        probs = E(nc.sbuf_tensor("probs4", [128, 2, 640], BF16))
        probsT = E(nc.sbuf_tensor("probsT4", [128, 2, 5, 128], BF16))
        negmax = E(nc.sbuf_tensor("negmax4", [128, 2], F32))
        rowsum = E(nc.sbuf_tensor("rowsum4", [128, 2], F32))
        rsum = E(nc.sbuf_tensor("rsum4", [128, 2], F32))
        oebuf = E(nc.sbuf_tensor("oebuf4", [128, D], BF16))
        self.fr = dict(mbuf=mbuf, probs=probs, probsT=probsT, negmax=negmax,
                       rowsum=rowsum, rsum=rsum, npar=2)

        jobs = []
        for t in range(NSEL_T):
            jobs.append((t, "sel"))
            if t < NCOMP_T:
                jobs.append((t, "comp"))

        def p4_front(j):
            t, kind = jobs[j]
            if kind == "sel" and t % 4 == 0:
                self.emit_qproj(di["xownT"], t // 4, xob, qfm, wq_s, self.WQLD)
            if kind == "sel":
                return self.front_a(t, qfm, t % 4, skfm, 2048, di["msel"], kind="sel", par=j)
            return self.front_a(t, qfm, t % 4, self.ckfm, 256, di["mcomp"], kind="comp", par=j)

        pend4 = p4_front(0)
        for j in range(len(jobs)):
            t, kind = jobs[j]
            ctx = pend4
            if j + 1 < len(jobs):
                pend4 = p4_front(j + 1)
            if kind == "sel":
                self.front_b(ctx, svtm, obuf=oebuf, flag_col=1)
            else:
                self.front_b(ctx, self.cvtm, obuf=oebuf, flag_col=0)

    def emit_qproj(self, xsrc, g, xob, qfm, wq_s, wrdy):
        block = self.block

        @block.gpsimd
        def _(gp, g=g):
            self.use_buf(gp, "xob")
            src = xsrc.rearrange("(c p) t -> p c t", p=128)
            for half in range(2):
                self.inc(gp.dma_start(xob[:, half * 4:half * 4 + 4, :],
                                      src[:, half * 4:half * 4 + 4, g * 512:(g + 1) * 512]), "sXOB", 16)
        XOLD = self.at("sXOB")

        QMM0 = [0]

        @block.tensor
        def _(t, XOLD=XOLD, QMM0=QMM0, wrdy=wrdy):
            self.wt(t, "sXOB", XOLD)
            self.wt(t, wrdy[0], wrdy[1])
            self.use_buf(t, "qfm")
            first = True
            AC0 = self.at("ac")
            for cc in range(8):
                self.use_bank(t, "G")
                for dj in range(8):
                    mm = t.matmul(self.psG[:], wq_s[:, dj, cc * 128:(cc + 1) * 128],
                                  xob[:, dj, :], start=(dj == 0), stop=(dj == 7))
                v_ = self.inc(mm, "mm", 1)
                if first:
                    QMM0[0] = v_ - 1
                    first = False
                self.free_bank_at("G", "ac", AC0 + cc + 1)
        self.free_buf("xob", "mm")

        @block.scalar
        def _(sc, QMM0=QMM0):
            for cc in range(8):
                sc.wait_ge(self.h["mm"], QMM0[0] + cc + 1)
                self.inc(sc.activation(qfm[:, cc, :], self.psG[:], AF.Identity,
                                       bias=self.bq_c[:, cc:cc + 1],
                                       scale=self.cons[:, EIGHTH:EIGHTH + 1]), "ac", 1)
        self.q_ready = self.at("ac")

    def front_a(self, t, qfm, ti, kbuf, nkv, msrc, kind, win_third=None, gates=None, par=0):
        block = self.block
        fr = self.fr
        par %= fr["npar"]
        mbuf = fr["mbuf"]
        probs = fr["probs"][:, par]
        negmax = fr["negmax"][:, par:par + 1]
        rowsum = fr["rowsum"][:, par:par + 1]
        rsum = fr["rsum"][:, par:par + 1]
        P = "%d" % par
        if kind == "win":
            lo, n = 128 * (t - win_third), 640
            vchunk0 = t - win_third
        else:
            lo, n = sel_window(t, nkv)
            vchunk0 = lo // 128
        nck = (n + 127) // 128
        mslot = self.msl
        self.msl = (self.msl + 1) % 2
        sb0, sb1 = (("A", "B"), ("C", "D"))[self.scpar]
        pb0, pb1 = ((self.psA, self.psB), (self.psC, self.psD))[self.scpar]
        self.scpar ^= 1
        n0 = min(n, 512)
        n1 = n - n0

        @block.sync
        def _(s, t=t, n=n, mslot=mslot):
            self.use_buf(s, "mbuf%d" % mslot)
            self.inc(s.dma_start(mbuf[:, mslot, 0:n], msrc[t, :, 0:n]), "sMB%d" % mslot, 16)
        MLD = self.at("sMB%d" % mslot)

        SCMM = [0]
        GMM = [0]

        @block.tensor
        def _(tt, ti=ti, lo=lo, n0=n0, n1=n1, SCMM=SCMM, GMM=GMM, gates=gates):
            self.wt(tt, "ac", self.q_ready)
            self.use_bank(tt, sb0)
            self.use_bank(tt, sb1)
            if gates is not None:
                xob, wg_s = gates[0], gates[1]
                self.use_bank(tt, "E")
                for dj in range(8):
                    tt.matmul(self.psE[:, 0:3], xob[:, dj, ti * 128:(ti + 1) * 128],
                              wg_s[:, dj, :], start=(dj == 0), stop=False)
                mm = tt.matmul(self.psE[:, 0:3], self.onesb[:], self.bgrow[:], start=False, stop=True)
                GMM[0] = self.inc(mm, "mm", 1)
            last = None
            for dj in range(8):
                last = tt.matmul(pb0[:, 0:n0], qfm[:, dj, ti * 128:(ti + 1) * 128],
                                 kbuf[:, dj, lo:lo + n0], start=(dj == 0), stop=(dj == 7))
                if n1 > 0:
                    last = tt.matmul(pb1[:, 0:n1], qfm[:, dj, ti * 128:(ti + 1) * 128],
                                     kbuf[:, dj, lo + n0:lo + n], start=(dj == 0), stop=(dj == 7))
            SCMM[0] = self.inc(last, "mm", 1)
        if gates is not None:
            self.free_buf("xob", "mm")

        @block.vector
        def _(v, n=n, SCMM=SCMM, MLD=MLD, mslot=mslot):
            self.wt(v, "mm", SCMM[0])
            self.wt(v, "sMB%d" % mslot, MLD)
            self.use_buf(v, "negmax" + P)
            if n > 512:
                self.vsync(v, v.tensor_tensor(pb0[:, 0:512], pb0[:, 0:512], mbuf[:, mslot, 0:512], ALU.add))
                self.vsync(v, v.tensor_tensor(pb1[:, 0:n - 512], pb1[:, 0:n - 512], mbuf[:, mslot, 512:n], ALU.add))
                self.vsync(v, v.reduce_max(negmax[:], pb0[:, 0:512], axis=AX.X, negate=True))
                self.vsync(v, v.reduce_max(rsum[:], pb1[:, 0:n - 512], axis=AX.X, negate=True))
                self.inc(v.tensor_tensor(negmax[:], negmax[:], rsum[:], ALU.min), "dv", 1)
            else:
                self.vsync(v, v.tensor_tensor(pb0[:, 0:n], pb0[:, 0:n], mbuf[:, mslot, 0:n], ALU.add))
                self.inc(v.reduce_max(negmax[:], pb0[:, 0:n], axis=AX.X, negate=True), "dv", 1)
        MSKDV = self.at("dv")
        self.free_buf("mbuf%d" % mslot, "dv")

        GEXAC = [0]

        @block.scalar
        def _(sc, n=n, MSKDV=MSKDV, GMM=GMM, gates=gates, GEXAC=GEXAC, SCMM=SCMM):
            if gates is not None:
                gex = gates[2]
                sc.wait_ge(self.h["mm"], GMM[0])
                self.use_buf(sc, "gex")
                GEXAC[0] = self.inc(sc.activation(gex[:], self.psE[:, 0:3], AF.Exp,
                                                  scale=self.cons[:, NEG1:NEG1 + 1]), "ac", 1)
            self.wt(sc, "dv", MSKDV)
            self.use_buf(sc, "probs" + P)
            self.use_buf(sc, "rowsum" + P)
            if n > 512:
                sc.activation(probs[:, 0:512], pb0[:, 0:512], AF.Exp,
                              bias=negmax[:, 0:1], accum_out=rowsum[:, 0:1])
                self.inc(sc.activation(probs[:, 512:n], pb1[:, 0:n - 512], AF.Exp,
                                       bias=negmax[:, 0:1], accum_out=rsum[:, 0:1]), "ac", 1)
            else:
                self.inc(sc.activation(probs[:, 0:n], pb0[:, 0:n], AF.Exp,
                                       bias=negmax[:, 0:1], accum_out=rowsum[:, 0:1]), "ac", 1)
        EXPAC = self.at("ac")
        self.free_buf("negmax" + P, "ac")
        self.free_bank(sb0, "ac")
        self.free_bank(sb1, "ac")
        if gates is not None:
            self.free_bank("E", "ac")

        GDV = 0
        if gates is not None:
            gex, gcolb, gslot = gates[2], gates[3], gates[4]

            @block.vector
            def _(v, GEXAC=GEXAC, gslot=gslot):
                self.wt(v, "ac", GEXAC[0])
                self.use_buf(v, "gcol%d" % gslot)
                self.vsync(v, v.tensor_scalar(gcolb[:, gslot, :], gex[:],
                                              self.cons[:, ONEC:ONEC + 1], None, ALU.add))
                self.vsync(v, v.reciprocal(gcolb[:, gslot, :], gcolb[:, gslot, :]))
                self.inc(v.engine_nop(), "dv", 1)
            GDV = self.at("dv")
            self.free_buf("gex", "dv")

        @block.vector
        def _(v, n=n, EXPAC=EXPAC):
            self.wt(v, "ac", EXPAC)
            self.use_buf(v, "rsum" + P)
            if n > 512:
                self.vsync(v, v.tensor_tensor(rowsum[:], rowsum[:], rsum[:], ALU.add))
            self.vsync(v, v.reciprocal(rsum[:], rowsum[:]))
            self.inc(v.engine_nop(), "dv", 1)
        RDV = self.at("dv")
        self.free_buf("rowsum" + P, "dv")
        return dict(t=t, kind=kind, n=n, n0=n0, n1=n1, nck=nck, lo=lo, vchunk0=vchunk0,
                    EXPAC=EXPAC, RDV=RDV, GDV=GDV, gates=gates, par=par)

    def front_b(self, ctx, vbuf, obuf=None, flag_col=None, comb=None):
        block = self.block
        fr = self.fr
        par = ctx["par"]
        probs = fr["probs"][:, par]
        probsT = fr["probsT"][:, par]
        rsum = fr["rsum"][:, par:par + 1]
        P = "%d" % par
        t, kind, n, nck, vchunk0 = ctx["t"], ctx["kind"], ctx["n"], ctx["nck"], ctx["vchunk0"]
        EXPAC, RDV = ctx["EXPAC"], ctx["RDV"]
        TMM = [0]

        @block.tensor
        def _(tt, nck=nck, n=n, EXPAC=EXPAC, TMM=TMM):
            self.wt(tt, "ac", EXPAC)
            self.use_bank(tt, "T")
            for ci in range(nck):
                w_ = min(128, n - ci * 128)
                mm = tt.transpose(self.psT[0:w_, ci, :], probs[:, ci * 128:ci * 128 + w_], self.idb[:])
            TMM[0] = self.inc(mm, "mm", 1)
            self.free_bank_at("T", "dv", self.at("dv") + 1)
        self.free_buf("probs" + P, "mm")

        @block.vector
        def _(v, nck=nck, TMM=TMM):
            self.wt(v, "mm", TMM[0])
            self.use_buf(v, "probsT" + P)
            self.inc(v.tensor_copy(probsT[:, 0:nck, :], self.psT[:, 0:nck, :]), "dv", 1)
        PTDV = self.at("dv")

        AVMM = [0]

        @block.tensor
        def _(tt, nck=nck, n=n, vchunk0=vchunk0, PTDV=PTDV, AVMM=AVMM):
            self.wt(tt, "dv", PTDV)
            self.use_bank(tt, "E")
            self.use_bank(tt, "F")
            for half, pe in ((0, self.psE), (1, self.psF)):
                for ci in range(nck):
                    w_ = min(128, n - ci * 128)
                    mm = tt.matmul(pe[:], probsT[0:w_, ci, :],
                                   vbuf[0:w_, vchunk0 + ci, half * 512:(half + 1) * 512],
                                   start=(ci == 0), stop=(ci == nck - 1))
                self.inc(mm, "mm", 1)
            AVMM[0] = self.at("mm")
        self.free_buf("probsT" + P, "mm")

        if kind == "win":
            gcolb, gslot = ctx["gates"][3], ctx["gates"][4]
            if self.debug and t == 0:
                @block.sync
                def _(s, EXPAC=EXPAC, RDV=RDV, PTDV=PTDV):
                    self.wt(s, "ac", EXPAC)
                    self.wt(s, "dv", PTDV)
                    self.inc(s.dma_start(self.dbg["d_probs"], probs[:]), "sDBG", 16)
                    self.inc(s.dma_start(self.dbg["d_pT"], probsT[:]), "sDBG", 16)
                self.free_buf("probs" + P, "sDBG")
                self.free_buf("probsT" + P, "sDBG")

            @block.vector
            def _(v, RDV=RDV, gslot=gslot):
                self.wt(v, "dv", RDV)
                self.wt(v, "dv", ctx["GDV"])
                self.inc(v.tensor_tensor(rsum[:], rsum[:], gcolb[:, gslot, 2:3], ALU.mult), "dv", 1)
            RG2 = self.at("dv")

            if self.debug and t == 0:
                @block.sync
                def _(s, RG2=RG2):
                    self.wt(s, "dv", RG2)
                    self.inc(s.dma_start(self.dbg["d_rsum"][:, 0:1], rsum[:], allow_slow_non_contiguous=True), "sDBG", 16)
                self.free_buf("rsum" + P, "sDBG")

            @block.scalar
            def _(sc, AVMM=AVMM, RG2=RG2):
                self.wt(sc, "mm", AVMM[0])
                self.wt(sc, "dv", RG2)
                self.use_buf(sc, "comb")
                sc.activation(comb[:, 0:512], self.psE[:], AF.Identity, scale=rsum[:, 0:1])
                self.inc(sc.activation(comb[:, 512:1024], self.psF[:], AF.Identity, scale=rsum[:, 0:1]), "ac", 1)
            self.free_buf("rsum" + P, "ac")
            self.free_bank("E", "ac")
            self.free_bank("F", "ac")
            self.comb_ac = self.at("ac")
        else:
            @block.vector
            def _(v, RDV=RDV, AVMM=AVMM, flag_col=flag_col):
                self.wt(v, "dv", RDV)
                self.wt(v, "mm", AVMM[0])
                self.use_buf(v, "oebuf")
                self.vsync(v, v.tensor_tensor(rsum[:], rsum[:], self.flg[:, flag_col:flag_col + 1], ALU.mult))
                v.tensor_scalar(obuf[:, 0:512], self.psE[:], rsum[:, 0:1], None, ALU.mult)
                self.inc(v.tensor_scalar(obuf[:, 512:1024], self.psF[:], rsum[:, 0:1], None, ALU.mult), "dv", 1)
            ODV = self.at("dv")
            self.free_buf("rsum" + P, "dv")
            self.free_bank("E", "dv")
            self.free_bank("F", "dv")
            dst = self.sosel[t] if kind == "sel" else self.socomp[t]
            stn = "sOSEL" if kind == "sel" else "sOCMP"

            @block.sync
            def _(s, dst=dst, ODV=ODV):
                self.wt(s, "dv", ODV)
                self.inc(s.dma_start(dst, obuf[:]), stn, 16)
            self.free_buf("oebuf", stn)

    def p5_main(self, ctx):
        nc, di, block = self.nc, self.di, self.block
        E = ctx.enter_context
        kwin = E(nc.sbuf_tensor("kwin", [128, 8, 1536], BF16))
        vwin = E(nc.sbuf_tensor("vwin", [128, 12, D], BF16))
        wq_s = E(nc.sbuf_tensor("wq_s5", [128, 8, D], BF16))
        wo_s = E(nc.sbuf_tensor("wo_s", [128, 8, D], BF16))
        wg_s = E(nc.sbuf_tensor("wg_s", [128, 8, 3], BF16))
        fkq = E(nc.sbuf_tensor("fkq", [128, 8, D], FP8))
        fbbc = E(nc.sbuf_tensor("fbbc", [128, D], BF16))
        xwb = E(nc.sbuf_tensor("xwb", [128, 8, 2, 512], BF16))
        xob = E(nc.sbuf_tensor("xob5", [128, 8, 512], BF16))
        qfm = E(nc.sbuf_tensor("qfm5", [128, 8, 512], BF16))
        mbuf = E(nc.sbuf_tensor("mbuf5", [128, 2, 640], BF16))
        probs = E(nc.sbuf_tensor("probs5", [128, 1, 640], BF16))
        probsT = E(nc.sbuf_tensor("probsT5", [128, 1, 5, 128], BF16))
        negmax = E(nc.sbuf_tensor("negmax5", [128, 1], F32))
        rowsum = E(nc.sbuf_tensor("rowsum5", [128, 1], F32))
        rsum = E(nc.sbuf_tensor("rsum5", [128, 1], F32))
        gcolb = E(nc.sbuf_tensor("gcolb", [128, 2, 3], F32))
        gex = E(nc.sbuf_tensor("gex", [128, 3], F32))
        comb = E(nc.sbuf_tensor("comb", [128, D], F32))
        combb = E(nc.sbuf_tensor("combb", [128, D], BF16))
        combT = E(nc.sbuf_tensor("combT", [128, 8, 128], BF16))
        oselb = E(nc.sbuf_tensor("oselb", [128, 2, D], BF16))
        ocompb = E(nc.sbuf_tensor("ocompb", [128, 2, D], BF16))
        xot = E(nc.sbuf_tensor("xot", [128, 2, D], F32))
        ztile = E(nc.sbuf_tensor("ztile", [128, D], F32))
        zscr = E(nc.sbuf_tensor("zscr", [128, D], BF16))
        y1 = E(nc.sbuf_tensor("y1", [128, D], F32))
        y1q = E(nc.sbuf_tensor("y1q", [128, 8, 128], FP8))
        ffnb = E(nc.sbuf_tensor("ffnb", [128, D], BF16))
        cols = E(nc.sbuf_tensor("cols", [128, 4], F32))
        lnw = E(nc.sbuf_tensor("lnw", [128, D], BF16))
        self.lnw = lnw
        self.fr = dict(mbuf=mbuf, probs=probs, probsT=probsT, negmax=negmax,
                       rowsum=rowsum, rsum=rsum, npar=1)

        @block.gpsimd
        def _(g):
            self.fence(g)
            fm128 = lambda ap: ap.rearrange("(c p) d -> p c d", p=128)
            for dst, src in [
                (wq_s[:], fm128(di["wqT"])),
                (wo_s[:], fm128(di["woT"])),
                (wg_s[:], fm128(di["wgT"])),
            ]:
                self.inc(g.dma_start(dst, src), "sW5", 16)
        WLD = ("sW5", self.at("sW5"))

        @block.sync
        def _(s):
            self.fence(s)
            self.wt(s, "sOSEL")
            self.wt(s, "sOCMP")
            self.inc(s.dma_start(xwb[:].rearrange("p c a b -> p c (a b)"),
                                 di["fker"].rearrange("(c p) d -> p c d", p=128)), "sFK", 16)
            self.inc(s.dma_start(fbbc[:], di["fbias_bc"]), "sFK", 16)
        FKLD = self.at("sFK")

        @block.vector
        def _(v, FKLD=FKLD):
            self.wt(v, "sFK", FKLD)
            for j in range(8):
                self.inc(v.tensor_scalar(fkq[:, j, :], xwb[:, j, :, :].rearrange("p a b -> p (a b)"),
                                         self.scal[:, 2:3], None, ALU.mult), "dv", 1)
        self.FKQDV = self.at("dv")

        xw_free = [0, 0]
        self.kvw_use = 0

        for TH, (T0, T1) in enumerate(THIRDS):
            for c in range(3):
                slot = c % 2

                @block.gpsimd
                def _(g, T0=T0, c=c, slot=slot):
                    self.wt(g, "dv", self.FKQDV)
                    if xw_free[slot]:
                        g.wait_ge(self.h["mm"], xw_free[slot])
                    src = di["xwinT"].rearrange("(c p) t -> p c t", p=128)
                    base = 128 * T0 + c * 512
                    for half in range(2):
                        self.inc(g.dma_start(xwb[:, half * 4:half * 4 + 4, slot, :],
                                             src[:, half * 4:half * 4 + 4, base:base + 512]),
                                 "sXW%d" % slot, 16)
                XWLD = self.at("sXW%d" % slot)

                KMM0 = [0]

                @block.tensor
                def _(t, c=c, slot=slot, XWLD=XWLD, KMM0=KMM0, KVW=self.kvw_use,
                      KDBG=getattr(self, "kvw_dbg", None)):
                    self.wt(t, "sXW%d" % slot, XWLD)
                    if c == 0:
                        self.wt(t, WLD[0], WLD[1])
                        if KVW:
                            t.wait_ge(self.h["mm"], KVW)
                        if KDBG:
                            self.wt(t, KDBG[0], KDBG[1])
                    first = True
                    AC0 = self.at("ac")
                    for cc in range(8):
                        bank = "AB"[cc % 2]
                        self.use_bank(t, bank)
                        pb = (self.psA, self.psB)[cc % 2]
                        for dj in range(8):
                            mm = t.matmul(pb[:], self.wk_s[:, dj, cc * 128:(cc + 1) * 128],
                                          xwb[:, dj, slot, :], start=(dj == 0), stop=(dj == 7))
                        v_ = self.inc(mm, "mm", 1)
                        if first:
                            KMM0[0] = v_ - 1
                            first = False
                        self.free_bank_at(bank, "ac", AC0 + cc + 1)

                @block.scalar
                def _(sc, c=c, KMM0=KMM0):
                    for cc in range(8):
                        pb = (self.psA, self.psB)[cc % 2]
                        sc.wait_ge(self.h["mm"], KMM0[0] + cc + 1)
                        self.inc(sc.activation(kwin[:, cc, c * 512:(c + 1) * 512], pb[:],
                                               AF.Identity, bias=self.bk_c[:, cc:cc + 1]), "ac", 1)

                VMM0 = [0]

                @block.tensor
                def _(t, slot=slot, VMM0=VMM0):
                    first = True
                    AC0 = self.at("ac")
                    for s4 in range(4):
                        for half in range(2):
                            bank = "CD"[half]
                            self.use_bank(t, bank)
                            pb = (self.psC, self.psD)[half]
                            for dj in range(8):
                                t.matmul(pb[:], xwb[:, dj, slot, s4 * 128:(s4 + 1) * 128],
                                         self.wv_s[:, dj, half * 512:(half + 1) * 512],
                                         start=(dj == 0), stop=False)
                            mm = t.matmul(pb[:], self.onesb[:],
                                          self.bvrow[:, half * 512:(half + 1) * 512],
                                          start=False, stop=True)
                            v_ = self.inc(mm, "mm", 1)
                            if first:
                                VMM0[0] = v_ - 1
                                first = False
                            self.free_bank_at(bank, "ac", AC0 + s4 * 2 + half + 1)
                xw_free[slot] = self.at("mm")

                @block.scalar
                def _(sc, c=c, VMM0=VMM0):
                    for s4 in range(4):
                        for half in range(2):
                            pb = (self.psC, self.psD)[half]
                            sc.wait_ge(self.h["mm"], VMM0[0] + s4 * 2 + half + 1)
                            self.inc(sc.copy(vwin[:, c * 4 + s4, half * 512:(half + 1) * 512], pb[:]), "ac", 1)
            KVW_AC = self.at("ac")

            if self.debug and TH == 0:
                @block.sync
                def _(s, KVW_AC=KVW_AC):
                    self.wt(s, "ac", KVW_AC)
                    self.inc(s.dma_start(self.dbg["d_kw"], kwin[:]), "sDBG", 16)
                    self.inc(s.dma_start(self.dbg["d_vw"], vwin[:]), "sDBG", 16)
                self.kvw_dbg = ("sDBG", self.at("sDBG"))

            pend = [None]

            def p5_front(t):
                ti = t % 4
                slot = t % 2
                OSLD = OCLD = 0
                if t < NSEL_T:
                    @block.sync
                    def _(s, t=t, slot=slot):
                        self.wt(s, "dv", self.FKQDV)
                        self.use_buf(s, "oselb%d" % slot)
                        self.inc(s.dma_start(oselb[:, slot, :], self.sosel[t]), "sOSL%d" % slot, 16)
                    OSLD = self.at("sOSL%d" % slot)
                if t < NCOMP_T:
                    @block.sync
                    def _(s, t=t, slot=slot):
                        self.use_buf(s, "ocompb%d" % slot)
                        self.inc(s.dma_start(ocompb[:, slot, :], self.socomp[t]),
                                 "sOCB%d" % slot, 16)
                    OCLD = self.at("sOCB%d" % slot)

                @block.sync
                def _(s, t=t, slot=slot):
                    self.use_buf(s, "xot%d" % slot)
                    self.inc(s.dma_start(xot[:, slot, :], di["xown"][t * 128:(t + 1) * 128, :]),
                             "sXOT%d" % slot, 16)
                XOTLD = self.at("sXOT%d" % slot)
                ctx = self.front_a(t, qfm, ti, kwin, S, di["mwin"], kind="win",
                                   win_third=T0, gates=(xob, wg_s, gex, gcolb, slot))
                ctx.update(OSLD=OSLD, OCLD=OCLD, XOTLD=XOTLD, slot=slot)
                return ctx

            pendb = [None]
            for t in range(T0, T1):
                g, ti = t // 4, t % 4
                if pendb[0] is None:
                    if ti == 0:
                        self.emit_qproj(di["xownT"], g, xob, qfm, wq_s, WLD)
                    ctx = p5_front(t)
                    self.front_b(ctx, vwin, comb=comb)
                    self.kvw_use = self.at("mm")
                else:
                    ctx = pendb[0]
                slot = ctx["slot"]
                OSLD, OCLD, XOTLD = ctx["OSLD"], ctx["OCLD"], ctx["XOTLD"]
                GDV = ctx["GDV"]

                if self.debug and t < 4:
                    @block.sync
                    def _(s, t=t, GDV=GDV, slot=slot):
                        self.wt(s, "dv", GDV)
                        self.inc(s.dma_start(self.dbg["d_gate"][:, t, :], gcolb[:, slot, :]), "sDBG", 16)
                    self.free_buf("gcol%d" % slot, "sDBG")

                if PIPELINE and t + 1 < T1:
                    if (t + 1) % 4 == 0:
                        self.emit_qproj(di["xownT"], (t + 1) // 4, xob, qfm, wq_s, WLD)
                    pend[0] = p5_front(t + 1)
                else:
                    pend[0] = None

                @block.vector
                def _(v, t=t, slot=slot, OSLD=OSLD, OCLD=OCLD):
                    self.wt(v, "ac", self.comb_ac)
                    self.use_buf(v, "combb")
                    if t < NSEL_T:
                        self.wt(v, "sOSL%d" % slot, OSLD)
                        v.scalar_tensor_tensor(comb[:], oselb[:, slot, :], gcolb[:, slot, 1:2],
                                               comb[:], ALU.mult, ALU.add)
                    if t < NCOMP_T:
                        self.wt(v, "sOCB%d" % slot, OCLD)
                        v.scalar_tensor_tensor(comb[:], ocompb[:, slot, :], gcolb[:, slot, 0:1],
                                               comb[:], ALU.mult, ALU.add)
                    self.inc(v.tensor_copy(combb[:], comb[:]), "dv", 1)
                CBDV = self.at("dv")
                self.free_buf("comb", "dv")
                self.free_buf("gcol%d" % slot, "dv")
                if t < NSEL_T:
                    self.free_buf("oselb%d" % slot, "dv")
                if t < NCOMP_T:
                    self.free_buf("ocompb%d" % slot, "dv")

                if self.debug and t < 4:
                    @block.sync
                    def _(s, t=t, CBDV=CBDV):
                        self.wt(s, "dv", CBDV)
                        self.inc(s.dma_start(self.dbg["d_comb"][t], comb[:]), "sDBG", 16)
                    self.free_buf("comb", "sDBG")

                CTMM = [0]

                @block.tensor
                def _(tt, CBDV=CBDV, CTMM=CTMM):
                    self.wt(tt, "dv", CBDV)
                    self.use_bank(tt, "T")
                    for dj in range(8):
                        mm = tt.transpose(self.psT[:, dj, :], combb[:, dj * 128:(dj + 1) * 128], self.idb[:])
                    CTMM[0] = self.inc(mm, "mm", 1)
                    self.free_bank_at("T", "dv", self.at("dv") + 1)
                self.free_buf("combb", "mm")

                @block.vector
                def _(v, CTMM=CTMM):
                    self.wt(v, "mm", CTMM[0])
                    self.use_buf(v, "combT")
                    self.inc(v.tensor_copy(combT[:], self.psT[:]), "dv", 1)
                CTDV = self.at("dv")

                ZDV = 0
                for half in range(2):
                    WOMM = [0]

                    @block.tensor
                    def _(tt, half=half, CTDV=CTDV, WOMM=WOMM):
                        self.wt(tt, "dv", CTDV)
                        self.use_bank(tt, "G")
                        for dj in range(8):
                            tt.matmul(self.psG[:], combT[:, dj, :],
                                      wo_s[:, dj, half * 512:(half + 1) * 512],
                                      start=(dj == 0), stop=False)
                        mm = tt.matmul(self.psG[:], self.onesb[:],
                                       self.borow[:, half * 512:(half + 1) * 512],
                                       start=False, stop=True)
                        WOMM[0] = self.inc(mm, "mm", 1)

                    @block.vector
                    def _(v, half=half, slot=slot, WOMM=WOMM, XOTLD=XOTLD):
                        self.wt(v, "mm", WOMM[0])
                        self.wt(v, "sXOT%d" % slot, XOTLD)
                        if half == 0:
                            self.use_buf(v, "ztile")
                        self.inc(v.tensor_tensor(ztile[:, half * 512:(half + 1) * 512], self.psG[:],
                                                 xot[:, slot, half * 512:(half + 1) * 512], ALU.add), "dv", 1)
                    self.free_bank("G", "dv")
                    ZDV = self.at("dv")
                self.free_buf("combT", "mm")
                self.free_buf("xot%d" % slot, "dv")

                if pend[0] is not None:
                    self.front_b(pend[0], vwin, comb=comb)
                    self.kvw_use = self.at("mm")
                pendb[0] = pend[0]

                if self.debug and t < 4:
                    @block.sync
                    def _(s, t=t, ZDV=ZDV):
                        self.wt(s, "dv", ZDV)
                        self.inc(s.dma_start(self.dbg["d_z"][t], ztile[:]), "sDBG", 16)
                    self.free_buf("ztile", "sDBG")

                self.emit_ln(ztile, y1, zscr, cols, ZDV, dst_buf="y1ot")
                Y1DV = self.at("dv")

                if self.debug and t < 4:
                    @block.sync
                    def _(s, t=t, Y1DV=Y1DV):
                        self.wt(s, "dv", Y1DV)
                        self.inc(s.dma_start(self.dbg["d_y1"][t], y1[:]), "sDBG", 16)

                @block.vector
                def _(v, Y1DV=Y1DV):
                    self.wt(v, "dv", Y1DV)
                    self.inc(v.tensor_copy(zscr[:], y1[:]), "dv", 1)
                Y1BDV = self.at("dv")

                FTMM = [0]

                @block.tensor
                def _(tt, Y1BDV=Y1BDV, FTMM=FTMM):
                    self.wt(tt, "dv", Y1BDV)
                    self.use_bank(tt, "T")
                    for dj in range(8):
                        mm = tt.transpose(self.psT[:, dj, :], zscr[:, dj * 128:(dj + 1) * 128], self.idb[:])
                    FTMM[0] = self.inc(mm, "mm", 1)
                    self.free_bank_at("T", "dv", self.at("dv") + 1)

                @block.vector
                def _(v, FTMM=FTMM):
                    self.wt(v, "mm", FTMM[0])
                    self.use_buf(v, "y1q")
                    self.inc(v.tensor_scalar(y1q[:].rearrange("p c d -> p (c d)"),
                                             self.psT[:].rearrange("p c d -> p (c d)"),
                                             self.scal[:, 0:1], None, ALU.mult), "dv", 1)
                YQDV = self.at("dv")

                for half in range(2):
                    FFMM = [0]

                    @block.tensor
                    def _(tt, half=half, YQDV=YQDV, FFMM=FFMM):
                        self.wt(tt, "dv", YQDV)
                        self.use_bank(tt, "G")
                        for dj in range(8):
                            mm = tt.matmul(self.psG[:], y1q[:, dj, :],
                                           fkq[:, dj, half * 512:(half + 1) * 512],
                                           start=(dj == 0), stop=(dj == 7))
                        FFMM[0] = self.inc(mm, "mm", 1)

                    @block.scalar
                    def _(sc, half=half, FFMM=FFMM):
                        self.wt(sc, "mm", FFMM[0])
                        if half == 0:
                            self.use_buf(sc, "ffnb")
                        self.inc(sc.activation(ffnb[:, half * 512:(half + 1) * 512], self.psG[:],
                                               AF.Identity, scale=self.scal[:, 1:2]), "ac", 1)
                    self.free_bank("G", "ac")
                FFAC = self.at("ac")
                self.free_buf("y1q", "mm")

                @block.vector
                def _(v, FFAC=FFAC):
                    self.wt(v, "ac", FFAC)
                    v.tensor_tensor(ffnb[:], ffnb[:], fbbc[:], ALU.add)
                    self.inc(v.tensor_tensor(ztile[:], y1[:], ffnb[:], ALU.add), "dv", 1)
                Z2DV = self.at("dv")
                self.free_buf("ffnb", "dv")

                self.emit_ln(ztile, y1, zscr, cols, Z2DV, dst_buf="y1ot")
                ODV = self.at("dv")
                self.free_buf("ztile", "dv")

                @block.sync
                def _(s, t=t, ODV=ODV):
                    self.wt(s, "dv", ODV)
                    self.inc(s.dma_start(self.out[t * 128:(t + 1) * 128, :], y1[:]), "sOUT", 16)
                self.free_buf("y1ot", "sOUT")

    def emit_ln(self, src, dst, scr, cols, SRCDV, dst_buf=None):
        block = self.block

        @block.scalar
        def _(sc, SRCDV=SRCDV):
            self.wt(sc, "dv", SRCDV)
            self.use_buf(sc, "cols")
            self.inc(sc.activation(self.lnw[:], src[:], AF.Identity, accum_out=cols[:, 0:1]), "ac", 1)
        SAC = self.at("ac")

        @block.vector
        def _(v, SAC=SAC):
            self.wt(v, "ac", SAC)
            self.vsync(v, v.scalar_tensor_tensor(scr[:], src[:], self.cons[:, ONEC:ONEC + 1],
                                                 src[:], ALU.mult, ALU.mult,
                                                 accum_out=cols[:, 1:2]))
            self.vsync(v, v.tensor_scalar(cols[:, 2:3], cols[:, 0:1],
                                          self.cons[:, INV1024:INV1024 + 1], None, ALU.mult))
            self.vsync(v, v.tensor_tensor(cols[:, 0:1], cols[:, 2:3], cols[:, 2:3], ALU.mult))
            self.inc(v.tensor_scalar(cols[:, 3:4], cols[:, 1:2], self.cons[:, INV1024:INV1024 + 1],
                                     cols[:, 0:1], ALU.mult, ALU.subtract), "dv", 1)
        VDV = self.at("dv")

        @block.scalar
        def _(sc, VDV=VDV):
            self.wt(sc, "dv", VDV)
            c = self.inc(sc.activation(cols[:, 1:2], cols[:, 3:4], AF.Ln,
                                       bias=self.cons[:, EPSC:EPSC + 1]), "acs", 1)
            sc.wait_ge(self.h["acs"], c)
            self.inc(sc.activation(cols[:, 3:4], cols[:, 1:2], AF.Exp,
                                   scale=self.cons[:, NEGHALF:NEGHALF + 1]), "ac", 1)
        RAC = self.at("ac")

        @block.vector
        def _(v, RAC=RAC, dst_buf=dst_buf):
            self.wt(v, "ac", RAC)
            if dst_buf:
                self.use_buf(v, dst_buf)
            self.inc(v.tensor_scalar(dst[:], src[:], cols[:, 2:3], cols[:, 3:4],
                                     ALU.subtract, ALU.mult), "dv", 1)
        self.free_buf("cols", "dv")

    def finish(self):
        block = self.block

        @block.sync
        def _(s):
            for n in ("sSSC", "sIDX", "sOSEL", "sOCMP", "sOUT", "sDBG"):
                if n in self.c and self.c[n]:
                    s.wait_ge(self.h[n], self.c[n])


def build_nc(debug=False):
    b = Builder(debug=debug)
    b.msl = 0
    b.scpar = 0
    nc = b.build()
    return nc, b


def kernel(L0_attn_x, L0_attn_w_qkv, L0_attn_b_qkv, L0_attn_w_comp, L0_attn_b_comp,
           L0_attn_w_score, L0_attn_b_score, L0_attn_w_gate, L0_attn_b_gate,
           L0_attn_w_o, L0_attn_b_o, L0_ffn_kernel, L0_ffn_input_scale,
           L0_ffn_kernel_scale, L0_ffn_bias):
    x = np.asarray(L0_attn_x)
    w = {
        "w_qkv": np.asarray(L0_attn_w_qkv), "b_qkv": np.asarray(L0_attn_b_qkv),
        "w_comp": np.asarray(L0_attn_w_comp), "b_comp": np.asarray(L0_attn_b_comp),
        "w_score": np.asarray(L0_attn_w_score), "b_score": np.asarray(L0_attn_b_score),
        "w_gate": np.asarray(L0_attn_w_gate), "b_gate": np.asarray(L0_attn_b_gate),
        "w_o": np.asarray(L0_attn_w_o), "b_o": np.asarray(L0_attn_b_o),
        "fker": np.asarray(L0_ffn_kernel), "s_in": np.asarray(L0_ffn_input_scale),
        "s_k": np.asarray(L0_ffn_kernel_scale), "fbias": np.asarray(L0_ffn_bias),
    }
    nc, _ = build_nc(debug=False)
    in_maps = [stage_core(x, w, core) for core in range(8)]
    res = run_bass_kernel_spmd(nc, in_maps, core_ids=list(range(8)))
    out = np.empty((4, S, D), np.float32)
    for core in range(8):
        b, h = core // 2, core % 2
        out[b, h * CQ:(h + 1) * CQ] = res.results[core]["out"]
    return out


# version 10
# speedup vs baseline: 1.0337x; 1.0209x over previous
import numpy as np
import ml_dtypes
from contextlib import ExitStack

import concourse.bass as bass
import concourse.mybir as mybir
from concourse.bass_utils import run_bass_kernel_spmd

F32 = mybir.dt.float32
BF16 = mybir.dt.bfloat16
FP8 = mybir.dt.float8e4
U32 = mybir.dt.uint32
AF = mybir.ActivationFunctionType
ALU = mybir.AluOpType
AX = mybir.AxisListType

D = 1024
S = 8192
CQ = 4096
WIN = 256
NTILE = 32
NSEL_T = 18
NCOMP_T = 4
THIRDS = [(0, 8), (8, 16), (16, 24), (24, 32)]
NEG = -1.0e30
bfl = ml_dtypes.bfloat16
PIPELINE = True
JCOL, PAR1000, NEG1, NEGHALF, EPSC, ONEC, INV1024, EIGHTH, BIGC = range(9)


def sel_window(t, nkv):
    lo = max(0, 128 * t - 256)
    hi = min(nkv, 128 * t + 128 + 256)
    return lo, hi - lo


def _win_mask(t, B):
    r = np.arange(128)[:, None]
    c = np.arange(640)[None, :]
    j = B + 128 * t - 256 + c
    i = B + 128 * t + r
    valid = (j >= i - WIN) & (j <= i + WIN) & (j >= 0) & (j < S)
    return np.where(valid, 0.0, NEG).astype(bfl)


def _sel_mask(t, kv_len):
    lo, n = sel_window(t, kv_len)
    r = np.arange(128)[:, None]
    c = np.arange(640)[None, :]
    j = lo + c
    i = 128 * t + r
    valid = (j >= i - WIN) & (j <= i + WIN) & (c < n)
    return np.where(valid, 0.0, NEG).astype(bfl)


def stage_core(x, w, core):
    b, h = core // 2, core % 2
    B = h * CQ
    xb = x[b]
    st = {}
    st["xT"] = np.ascontiguousarray(xb.T)
    wt = np.zeros((D, 5120), np.float32)
    lo, hi = B - 256, B + 4864
    slo, shi = max(lo, 0), min(hi, S)
    wt[:, slo - lo:shi - lo] = xb[slo:shi].T
    st["xwinT"] = wt
    st["xownT"] = np.ascontiguousarray(xb[B:B + CQ].T)
    st["xown"] = np.ascontiguousarray(xb[B:B + CQ])
    st["xtm"] = xb

    wq3 = w["w_qkv"].reshape(16, 3, 64, D)
    for nm, idx in (("wq", 0), ("wk", 1), ("wv", 2)):
        m = np.ascontiguousarray(wq3[:, idx].reshape(D, D))
        st[nm + "T"] = np.ascontiguousarray(m.T)
        if nm != "wq":
            st[nm + "nat"] = m
    bq3 = w["b_qkv"].reshape(16, 3, 64)
    fm = lambda v: np.ascontiguousarray(v.reshape(8, 128).T)
    st["bq_fm"] = fm(bq3[:, 0].reshape(D))
    st["bk_fm"] = fm(bq3[:, 1].reshape(D))
    st["bv_fm"] = fm(bq3[:, 2].reshape(D))
    st["bv_row"] = bq3[:, 2].reshape(1, D).copy()
    st["ws_fm"] = fm(w["w_score"][0])
    st["wgT"] = np.ascontiguousarray(w["w_gate"].T)
    st["bg_row"] = w["b_gate"].reshape(1, 3).copy()
    st["woT"] = np.ascontiguousarray(w["w_o"].T)
    st["bo_row"] = w["b_o"].reshape(1, D).copy()
    st["wcT"] = np.ascontiguousarray(w["w_comp"].T)
    st["bc_row"] = w["b_comp"].reshape(1, D).copy()
    st["fker"] = np.ascontiguousarray(w["fker"])
    st["fbias_bc"] = np.broadcast_to(np.asarray(w["fbias"]).reshape(1, D), (128, D)).copy()
    sc2 = np.array([[float(np.asarray(w["s_in"]).reshape(-1)[0]),
                     float(np.asarray(w["s_k"]).reshape(-1)[0])]], np.float32)
    st["scales_bc"] = np.broadcast_to(sc2, (128, 2)).copy()

    st["mwin"] = np.stack([_win_mask(t, B) for t in range(NTILE)])
    st["msel"] = np.stack([_sel_mask(t, 2048) for t in range(NSEL_T)])
    st["mcomp"] = np.stack([_sel_mask(t, 256) for t in range(NCOMP_T)])
    one = np.ones((128, 1), np.float32)
    f = 1.0 if h == 0 else 0.0
    st["flags"] = np.concatenate([one * f, one * f], 1)
    st["identb"] = np.eye(128, dtype=bfl)
    p = np.arange(128, dtype=np.float32)[:, None]
    st["consts"] = np.concatenate([
        (p % 64) + 1000.0 * (p // 64), 1000.0 * (p % 2), -one, -0.5 * one,
        1e-5 * one, one, one / 1024.0, 0.125 * one, 1e5 * one,
        0 * one, 0 * one, 0 * one], 1)
    st["iota64"] = np.broadcast_to(np.arange(64, dtype=np.float32), (128, 64)).copy()
    st["ones_row"] = np.ones((1, 128), np.float32)
    return st


class Builder:
    def __init__(self, debug=False):
        self.debug = debug
        nc = self.nc = bass.Bass("TRN2", target_bir_lowering=False, debug=False,
                                 detect_race_conditions=False)
        self.di = {}
        for name, shape, dt in [
            ("xT", (D, S), F32), ("xwinT", (D, 5120), F32), ("xownT", (D, CQ), F32),
            ("xown", (CQ, D), F32), ("xtm", (S, D), F32),
            ("wqT", (D, D), F32), ("wkT", (D, D), F32), ("wvT", (D, D), F32),
            ("wknat", (D, D), F32), ("wvnat", (D, D), F32),
            ("bq_fm", (128, 8), F32), ("bk_fm", (128, 8), F32), ("bv_fm", (128, 8), F32),
            ("bv_row", (1, D), F32), ("ws_fm", (128, 8), F32),
            ("wgT", (D, 3), F32), ("bg_row", (1, 3), F32),
            ("woT", (D, D), F32), ("bo_row", (1, D), F32),
            ("wcT", (32 * D, D), F32), ("bc_row", (1, D), F32),
            ("fker", (D, D), BF16), ("fbias_bc", (128, D), BF16),
            ("scales_bc", (128, 2), F32),
            ("mwin", (NTILE, 128, 640), BF16), ("msel", (NSEL_T, 128, 640), BF16),
            ("mcomp", (NCOMP_T, 128, 640), BF16), ("flags", (128, 2), F32),
            ("identb", (128, 128), BF16),
            ("consts", (128, 12), F32), ("iota64", (128, 64), F32),
            ("ones_row", (1, 128), F32),
        ]:
            self.di[name] = nc.dram_tensor(name, list(shape), dt, kind="ExternalInput").ap()
        self.out = nc.dram_tensor("out", [CQ, D], F32, kind="ExternalOutput").ap()
        self.sscore = nc.dram_tensor("sscore", [2, S], F32, kind="ExternalOutput").ap()
        self.sidx = nc.dram_tensor("sidx", [2, 2048], F32, kind="ExternalOutput").ap()
        self.sosel = nc.dram_tensor("sosel", [NSEL_T, 128, D], BF16, kind="ExternalOutput").ap()
        self.socomp = nc.dram_tensor("socomp", [NCOMP_T, 128, D], BF16, kind="ExternalOutput").ap()
        self.dbg = {}
        if debug:
            for name, shape, dt in [
                ("d_ck", (128, 8, 256), BF16), ("d_cv", (128, 2, D), BF16),
                ("d_sk", (128, 8, 2048), BF16), ("d_sv", (128, 16, D), BF16),
                ("d_kw", (128, 8, 1536), BF16), ("d_vw", (128, 12, D), BF16),
                ("d_q", (128, 8, 512), BF16), ("d_gate", (128, 4, 3), F32),
                ("d_comb", (4, 128, D), F32), ("d_z", (4, 128, D), F32),
                ("d_probs", (128, 640), BF16), ("d_rsum", (128, 4), F32),
                ("d_pT", (128, 5, 128), BF16),
                ("d_y1", (4, 128, D), F32),
            ]:
                self.dbg[name] = nc.dram_tensor(name, list(shape), dt, kind="ExternalOutput").ap()
        self.bankfree = {}
        self.buffree = {}

    def init_sems(self, E):
        self._E = E
        self.h, self.c = {}, {}
        for n in ("mm", "dv", "ac"):
            self.h[n] = E(self.nc.semaphore("sem_" + n))
            self.c[n] = 0

    def sem(self, n):
        if n not in self.h:
            self.h[n] = self._E(self.nc.semaphore("sem_" + n))
            self.c[n] = 0
        return self.h[n]

    def inc(self, inst, n, k=1):
        self.sem(n)
        inst.then_inc(self.h[n], k)
        self.c[n] += k
        return self.c[n]

    def at(self, n):
        self.sem(n)
        return self.c[n]

    def wt(self, eng, n, v=None):
        self.sem(n)
        v = self.c[n] if v is None else v
        if v > 0:
            eng.wait_ge(self.h[n], v)

    def use_bank(self, eng, bank):
        if bank in self.bankfree:
            n, v = self.bankfree.pop(bank)
            if v > 0:
                eng.wait_ge(self.h[n], v)

    def free_bank(self, bank, n):
        self.bankfree[bank] = (n, self.c[n])

    def free_bank_at(self, bank, n, count):
        self.sem(n)
        self.bankfree[bank] = (n, count)

    def fence(self, eng):
        for n in ("mm", "dv", "ac"):
            self.wt(eng, n)

    def vsync(self, v, ins):
        c = self.inc(ins, "dvs", 1)
        v.wait_ge(self.h["dvs"], c)
        return ins

    def use_buf(self, eng, name):
        for n, v in self.buffree.pop(name, ()):
            if v > 0:
                eng.wait_ge(self.h[n], v)

    def free_buf(self, name, n):
        self.buffree.setdefault(name, []).append((n, self.c[n]))

    def build(self):
        nc = self.nc
        with ExitStack() as ctx:
            E = ctx.enter_context
            self.init_sems(E)

            def sbuf(name, shape, dt):
                return E(nc.sbuf_tensor(name, list(shape), dt))

            self.wk_s = sbuf("wk_s", (128, 8, D), BF16)
            self.wv_s = sbuf("wv_s", (128, 8, D), BF16)
            self.bq_c = sbuf("bq_c", (128, 8), F32)
            self.bk_c = sbuf("bk_c", (128, 8), F32)
            self.bv_c = sbuf("bv_c", (128, 8), F32)
            self.ws_s = sbuf("ws_s", (128, 8), F32)
            self.cons = sbuf("cons", (128, 12), F32)
            self.iota64 = sbuf("iota64_s", (128, 64), F32)
            self.flg = sbuf("flg", (128, 2), F32)
            self.idb = sbuf("idb", (128, 128), BF16)
            self.u2b = sbuf("u2b", (128, 8, 2), BF16)
            self.scal = sbuf("scal", (128, 4), F32)
            self.onesb = sbuf("onesb", (1, 128), BF16)
            self.bvrow = sbuf("bvrow", (1, D), BF16)
            self.bcrow = sbuf("bcrow", (1, D), BF16)
            self.borow = sbuf("borow", (1, D), BF16)
            self.bgrow = sbuf("bgrow", (1, 3), BF16)
            self.psA = E(nc.psum_tensor("psA", [128, 512], F32))
            self.psB = E(nc.psum_tensor("psB", [128, 512], F32))
            self.psC = E(nc.psum_tensor("psC", [128, 512], F32))
            self.psD = E(nc.psum_tensor("psD", [128, 512], F32))
            self.psE = E(nc.psum_tensor("psE", [128, 512], F32))
            self.psF = E(nc.psum_tensor("psF", [128, 512], F32))
            self.psT = E(nc.psum_tensor("psT", [128, 8, 128], BF16))
            self.psG = E(nc.psum_tensor("psG", [128, 512], F32))

            self.block = E(nc.Block())

            with ExitStack() as c14:
                self.ckfm = c14.enter_context(nc.sbuf_tensor("ckfm", [128, 8, 256], BF16))
                self.cvtm = c14.enter_context(nc.sbuf_tensor("cvtm", [128, 2, D], BF16))
                with ExitStack() as c0:
                    self.p0_setup(c0)
                with ExitStack() as c1:
                    self.p1_kv_compress(c1)
                with ExitStack() as c2:
                    self.p2_topk(c2)
                with ExitStack() as c3:
                    self.p34_sel(c3)
                self.P4MM = self.at("mm")
            with ExitStack() as c5:
                self.p5_main(c5)
            self.finish()
        return nc

    def p0_setup(self, ctx):
        nc, di, block = self.nc, self.di, self.block
        E = ctx.enter_context
        fm128 = lambda ap: ap.rearrange("(c p) d -> p c d", p=128)

        @block.sync
        def _(s):
            for dst, src in [
                (self.bq_c[:], di["bq_fm"]), (self.bk_c[:], di["bk_fm"]),
                (self.bv_c[:], di["bv_fm"]), (self.ws_s[:], di["ws_fm"]),
                (self.cons[:], di["consts"]), (self.flg[:], di["flags"]),
                (self.iota64[:], di["iota64"]),
                (self.idb[:], di["identb"]), (self.scal[:, 0:2], di["scales_bc"]),
            ]:
                self.inc(s.dma_start(dst, src), "sP0S", 16)

        @block.gpsimd
        def _(g):
            for dst, src in [
                (self.wk_s[:], fm128(di["wkT"])),
                (self.wv_s[:], fm128(di["wvT"])),
                (self.onesb[:], di["ones_row"]),
                (self.bvrow[:], di["bv_row"]), (self.bcrow[:], di["bc_row"]),
                (self.borow[:], di["bo_row"]), (self.bgrow[:], di["bg_row"]),
            ]:
                self.inc(g.dma_start(dst, src), "sP0G", 16)

        @block.vector
        def _(v):
            self.wt(v, "sP0S")
            self.wt(v, "sP0G")
            self.vsync(v, v.tensor_copy(self.scal[:, 2:3], self.scal[:, 1:2]))
            self.vsync(v, v.tensor_tensor(self.scal[:, 1:2], self.scal[:, 0:1], self.scal[:, 1:2], ALU.mult))
            self.vsync(v, v.reciprocal(self.scal[:, 0:1], self.scal[:, 0:1]))
            self.vsync(v, v.reciprocal(self.scal[:, 2:3], self.scal[:, 2:3]))
            v.tensor_scalar(self.bq_c[:], self.bq_c[:], self.cons[:, EIGHTH:EIGHTH + 1], None, ALU.mult)
            v.memset(self.ckfm[:], 0.0)
            self.inc(v.memset(self.cvtm[:], 0.0), "dv", 1)

        unat = E(nc.sbuf_tensor("unat", [128, 8, D], F32))
        u2f = E(nc.sbuf_tensor("u2f", [128, 16], F32))

        @block.sync
        def _(s):
            self.inc(s.dma_start(unat[:], fm128(di["wknat"])), "sUN", 16)
        UKLD = self.at("sUN")

        @block.tensor
        def _(t):
            self.wt(t, "sUN", UKLD)
            self.wt(t, "sP0S")
            for dj in range(8):
                for cj in range(8):
                    mm = t.matmul(self.psG[:, dj:dj + 1], unat[:, cj, dj * 128:(dj + 1) * 128],
                                  self.ws_s[:, cj:cj + 1], start=(cj == 0), stop=(cj == 7))
            self.inc(mm, "mm", 1)
        UKMM = self.at("mm")

        @block.vector
        def _(v):
            self.wt(v, "mm", UKMM)
            self.inc(v.tensor_copy(u2f[:, 0:8], self.psG[:, 0:8]), "dv", 1)
        UKEV = self.at("dv")

        @block.sync
        def _(s):
            self.wt(s, "dv", UKEV)
            self.inc(s.dma_start(unat[:], fm128(di["wvnat"])), "sUN", 16)
        UVLD = self.at("sUN")

        @block.tensor
        def _(t):
            self.wt(t, "sUN", UVLD)
            for dj in range(8):
                for cj in range(8):
                    mm = t.matmul(self.psG[:, 8 + dj:9 + dj], unat[:, cj, dj * 128:(dj + 1) * 128],
                                  self.ws_s[:, cj:cj + 1], start=(cj == 0), stop=(cj == 7))
            self.inc(mm, "mm", 1)
        UVMM = self.at("mm")

        @block.vector
        def _(v):
            self.wt(v, "mm", UVMM)
            v.tensor_copy(u2f[:, 8:16], self.psG[:, 8:16])
            v.tensor_copy(self.u2b[:, :, 0:1], u2f[:, 0:8].rearrange("p (c o) -> p c o", o=1))
            self.inc(v.tensor_copy(self.u2b[:, :, 1:2], u2f[:, 8:16].rearrange("p (c o) -> p c o", o=1)), "dv", 1)
        self.free_bank("G", "dv")
        self.U2B = self.at("dv")

    def p1_kv_compress(self, ctx):
        nc, di, block = self.nc, self.di, self.block
        E = ctx.enter_context
        kfm = E(nc.sbuf_tensor("kfm", [128, 8, 4096], BF16))
        vfm = E(nc.sbuf_tensor("vfm", [128, 8, 4096], BF16))
        xb = E(nc.sbuf_tensor("xb", [128, 8, 2, 512], BF16))
        wcb = E(nc.sbuf_tensor("wcb", [128, 4, D], BF16))
        ssc = E(nc.sbuf_tensor("ssc", [2, 512], F32))
        cktmp = E(nc.sbuf_tensor("cktmp", [128, 128], BF16))

        xb_free = [0, 0]
        wc_free = [0] * 4
        ssc_free = [0]

        for SC in range(2):
            for ch in range(8):
                gc = SC * 8 + ch
                slot = gc % 2

                @block.gpsimd
                def _(g, gc=gc, slot=slot):
                    if xb_free[slot]:
                        g.wait_ge(self.h["mm"], xb_free[slot])
                    src = di["xT"].rearrange("(c p) t -> p c t", p=128)
                    for half in range(2):
                        self.inc(g.dma_start(xb[:, half * 4:half * 4 + 4, slot, :],
                                             src[:, half * 4:half * 4 + 4, gc * 512:(gc + 1) * 512]),
                                 "sXB%d" % slot, 16)
                XLD = self.at("sXB%d" % slot)

                KVMM0 = [0]

                @block.tensor
                def _(t, gc=gc, slot=slot, XLD=XLD, KVMM0=KVMM0):
                    self.wt(t, "sXB%d" % slot, XLD)
                    if gc == 0:
                        self.wt(t, "dv", self.U2B)
                    self.use_bank(t, "G")
                    if ssc_free[0]:
                        t.wait_ge(self.h["sSSC"], ssc_free[0])
                    for dj in range(8):
                        mm = t.matmul(self.psG[0:2, :], self.u2b[:, dj, :], xb[:, dj, slot, :],
                                      start=(dj == 0), stop=(dj == 7))
                    KVMM0[0] = self.inc(mm, "mm", 1)
                    self.free_bank_at("G", "dv", self.at("dv") + 1)
                    DV0 = self.at("dv") + 1
                    for kind, wsb in enumerate((self.wk_s, self.wv_s)):
                        for cc in range(8):
                            bank = "ABCD"[kind * 2 + cc % 2]
                            self.use_bank(t, bank)
                            pb = (self.psA, self.psB, self.psC, self.psD)[kind * 2 + cc % 2]
                            for dj in range(8):
                                mm = t.matmul(pb[:], wsb[:, dj, cc * 128:(cc + 1) * 128],
                                              xb[:, dj, slot, :], start=(dj == 0), stop=(dj == 7))
                            self.inc(mm, "mm", 1)
                            self.free_bank_at(bank, "dv", DV0 + kind * 8 + cc + 1)
                xb_free[slot] = self.at("mm")

                @block.vector
                def _(v, KVMM0=KVMM0):
                    self.wt(v, "mm", KVMM0[0])
                    self.inc(v.tensor_copy(ssc[:], self.psG[0:2, :]), "dv", 1)
                SSC_DV = self.at("dv")

                @block.sync
                def _(s, gc=gc, SSC_DV=SSC_DV):
                    self.wt(s, "dv", SSC_DV)
                    self.inc(s.dma_start(self.sscore[:, gc * 512:(gc + 1) * 512], ssc[:]), "sSSC", 16)
                ssc_free[0] = self.at("sSSC")

                @block.vector
                def _(v, ch=ch, KVMM0=KVMM0):
                    for kind, (buf, bias) in enumerate(((kfm, self.bk_c), (vfm, self.bv_c))):
                        for cc in range(8):
                            pb = (self.psA, self.psB, self.psC, self.psD)[kind * 2 + cc % 2]
                            v.wait_ge(self.h["mm"], KVMM0[0] + kind * 8 + cc + 1)
                            self.inc(v.tensor_scalar(buf[:, cc, ch * 512:(ch + 1) * 512], pb[:],
                                                     bias[:, cc:cc + 1], None, ALU.add), "dv", 1)
            KV_EVAC = self.at("dv")

            wcld = []
            COMP0 = self.at("mm")

            @block.gpsimd
            def _(g, SC=SC, wcld=wcld, COMP0=COMP0):
                for w in range(256):
                    slot = w % 4
                    if w >= 4:
                        g.wait_ge(self.h["mm"], COMP0 + w - 3)
                    elif wc_free[slot]:
                        g.wait_ge(self.h["mm"], wc_free[slot])
                    src = di["wcT"].rearrange("(w p) c -> p w c", p=128)[:, w, :]
                    wcld.append((slot, self.inc(g.dma_start(wcb[:, slot, :], src), "sWC%d" % slot, 16)))

            @block.tensor
            def _(t, SC=SC, wcld=wcld, KV_EVAC=KV_EVAC):
                self.wt(t, "dv", KV_EVAC)
                for b_ in "ABCD":
                    self.use_bank(t, b_)
                for w in range(256):
                    slot = w % 4
                    p_, dj = w // 8, w % 8
                    t.wait_ge(self.h["sWC%d" % wcld[w][0]], wcld[w][1])
                    st_ = (w == 0)
                    kap = kfm[:, dj, p_::32]
                    vap = vfm[:, dj, p_::32]
                    t.matmul(self.psA[:], kap, wcb[:, slot, 0:512], start=st_, stop=False)
                    t.matmul(self.psB[:], kap, wcb[:, slot, 512:1024], start=st_, stop=False)
                    t.matmul(self.psC[:], vap, wcb[:, slot, 0:512], start=st_, stop=False)
                    mm = t.matmul(self.psD[:], vap, wcb[:, slot, 512:1024], start=st_, stop=False)
                    wc_free[slot] = self.inc(mm, "mm", 1)
                t.matmul(self.psA[:], self.onesb[:], self.bcrow[:, 0:512], start=False, stop=True)
                t.matmul(self.psB[:], self.onesb[:], self.bcrow[:, 512:1024], start=False, stop=True)
                t.matmul(self.psC[:], self.onesb[:], self.bcrow[:, 0:512], start=False, stop=True)
                mm = t.matmul(self.psD[:], self.onesb[:], self.bcrow[:, 512:1024], start=False, stop=True)
                self.inc(mm, "mm", 1)
            COMPMM = self.at("mm")

            @block.vector
            def _(v, SC=SC, COMPMM=COMPMM):
                self.wt(v, "mm", COMPMM)
                v.tensor_copy(self.cvtm[:, SC, 0:512], self.psC[:])
                self.inc(v.tensor_copy(self.cvtm[:, SC, 512:1024], self.psD[:]), "dv", 1)
            self.free_bank("C", "dv")
            self.free_bank("D", "dv")

            for half, pb in ((0, self.psA), (1, self.psB)):
                for q4 in range(4):
                    cc = half * 4 + q4

                    @block.vector
                    def _(v, pb=pb, q4=q4):
                        self.use_buf(v, "cktmp")
                        self.inc(v.tensor_copy(cktmp[:], pb[:, q4 * 128:(q4 + 1) * 128]), "dv", 1)
                    CKDV = self.at("dv")

                    @block.tensor
                    def _(t, CKDV=CKDV):
                        self.wt(t, "dv", CKDV)
                        self.use_bank(t, "T")
                        self.inc(t.transpose(self.psT[:, 0, :], cktmp[:], self.idb[:]), "mm", 1)
                        self.free_bank_at("T", "dv", self.at("dv") + 1)
                    self.free_buf("cktmp", "mm")
                    CKMM = self.at("mm")

                    @block.vector
                    def _(v, SC=SC, cc=cc, CKMM=CKMM):
                        self.wt(v, "mm", CKMM)
                        self.inc(v.tensor_copy(self.ckfm[:, cc, SC * 128:SC * 128 + 128],
                                               self.psT[:, 0, :]), "dv", 1)
            self.free_bank("A", "dv")
            self.free_bank("B", "dv")

        if self.debug:
            DBGDV = self.at("dv")

            @block.sync
            def _(s, DBGDV=DBGDV):
                self.wt(s, "dv", DBGDV)
                self.inc(s.dma_start(self.dbg["d_ck"], self.ckfm[:]), "sDBG", 16)
                self.inc(s.dma_start(self.dbg["d_cv"], self.cvtm[:]), "sDBG", 16)

    def p2_topk(self, ctx):
        nc, block = self.nc, self.block
        E = ctx.enter_context
        blk = E(nc.sbuf_tensor("blk", [128, 2, 64], F32))
        wrk = E(nc.sbuf_tensor("wrk", [128, 64], F32))
        prev = E(nc.sbuf_tensor("prev", [128, 64], F32))
        scr64 = E(nc.sbuf_tensor("scr64", [128, 64], F32))
        e1 = E(nc.sbuf_tensor("e1", [128, 64], F32))
        cand = E(nc.sbuf_tensor("cand", [128, 64], F32))
        t16v = E(nc.sbuf_tensor("t16v", [128, 16], F32))
        one8 = E(nc.sbuf_tensor("one8", [128, 8], F32))
        idxf = E(nc.sbuf_tensor("idxf", [128, 16], F32))

        @block.sync
        def _(s):
            self.wt(s, "sSSC")
            self.fence(s)
            for Ti in range(2):
                self.inc(s.dma_start(blk[:, Ti, :],
                                     self.sscore[Ti, :].rearrange("(p f) -> p f", p=128)), "sBLK", 16)
        BLKLD = self.at("sBLK")

        for Ti in range(2):
            @block.vector
            def _(v, Ti=Ti, BLKLD=BLKLD):
                self.wt(v, "sBLK", BLKLD)
                if Ti == 1:
                    self.use_buf(v, "idxf")
                self.vsync(v, v.tensor_copy(wrk[:], blk[:, Ti, :]))
                self.vsync(v, v.max(out=t16v[:, 0:8], in_=wrk[:]))
                self.vsync(v, v.match_replace(scr64[:], t16v[:, 0:8], wrk[:], NEG))
                self.vsync(v, v.max(out=t16v[:, 8:16], in_=scr64[:]))
                v.tensor_copy(prev[:], wrk[:])
                v.memset(one8[:], 1e30)
                for r in range(16):
                    self.vsync(v, v.tensor_copy(one8[:, 0:1], t16v[:, r:r + 1]))
                    self.vsync(v, v.match_replace(wrk[:], one8[:], wrk[:], NEG))
                    v.tensor_tensor(e1[:], wrk[:], prev[:], ALU.is_equal)
                    v.tensor_scalar(cand[:], e1[:], self.cons[:, BIGC:BIGC + 1], None, ALU.mult)
                    self.vsync(v, v.tensor_tensor(cand[:], cand[:], self.iota64[:], ALU.add))
                    self.vsync(v, v.tensor_reduce(idxf[:, r:r + 1], cand[:], AX.X, ALU.min))
                    v.tensor_copy(prev[:], wrk[:])
                self.inc(v.tensor_scalar(idxf[:], idxf[:], self.cons[:, PAR1000:PAR1000 + 1],
                                         None, ALU.add), "dv", 1)
            IDXDV = self.at("dv")

            @block.sync
            def _(s, Ti=Ti, IDXDV=IDXDV):
                self.wt(s, "dv", IDXDV)
                self.inc(s.dma_start(self.sidx[Ti, :].rearrange("(p f) -> p f", p=128), idxf[:]), "sIDX", 16)
            self.free_buf("idxf", "sIDX")
        self.IDXST = self.at("sIDX")

    def p34_sel(self, ctx):
        nc, di, block = self.nc, self.di, self.block
        E = ctx.enter_context
        skfm = E(nc.sbuf_tensor("skfm", [128, 8, 2048], BF16))
        svtm = E(nc.sbuf_tensor("svtm", [128, 16, D], BF16))
        xselK = E(nc.sbuf_tensor("xselK", [128, 8, 512], BF16))
        xselV = E(nc.sbuf_tensor("xselV", [128, 8, 512], BF16))
        bcidx = E(nc.sbuf_tensor("bcidx", [128, 2, 512], F32))
        xtmb = E(nc.sbuf_tensor("xtmb", [128, 4, D], BF16))
        oneh = E(nc.sbuf_tensor("oneh", [128, 2, 32], BF16))
        wq_s = E(nc.sbuf_tensor("wq_s4", [128, 8, D], BF16))
        xob = E(nc.sbuf_tensor("xob4", [128, 8, 512], BF16))
        qfm = E(nc.sbuf_tensor("qfm4", [128, 8, 512], BF16))

        @block.gpsimd
        def _(g):
            self.fence(g)
            self.inc(g.dma_start(wq_s[:],
                                 di["wqT"].rearrange("(c p) d -> p c d", p=128)), "sWQ4", 16)
        self.WQLD = ("sWQ4", self.at("sWQ4"))
        FENCED = [False]

        xtm_free = [0] * 4
        for g4 in range(4):
            @block.sync
            def _(s, g4=g4):
                if not FENCED[0]:
                    self.fence(s)
                    FENCED[0] = True
                self.wt(s, "sIDX", self.IDXST)
                self.use_buf(s, "bcidx")
                for Ti in range(2):
                    src = self.sidx[Ti:Ti + 1, g4 * 512:(g4 + 1) * 512].to_broadcast([128, 512])
                    self.inc(s.dma_start(bcidx[:, Ti, :], src), "sBC", 16)
            BCLD = self.at("sBC")

            for c16 in range(16):
                c64 = g4 * 16 + c16
                slot = c64 % 4

                @block.gpsimd
                def _(g, c64=c64, slot=slot):
                    if xtm_free[slot]:
                        g.wait_ge(self.h["mm"], xtm_free[slot])
                    src = di["xtm"].rearrange("(a p) d -> p a d", p=128)[:, c64, :]
                    self.inc(g.dma_start(xtmb[:, slot, :], src), "sXT%d" % slot, 16)
                XTLD = self.at("sXT%d" % slot)

                @block.vector
                def _(v, c16=c16, BCLD=BCLD):
                    self.wt(v, "sBC", BCLD)
                    self.use_buf(v, "oneh")
                    for Ti in range(2):
                        self.inc(v.tensor_scalar(oneh[:, Ti, :], bcidx[:, Ti, c16 * 32:(c16 + 1) * 32],
                                                 self.cons[:, JCOL:JCOL + 1], None, ALU.is_equal), "dv", 1)
                OHDV = self.at("dv")

                @block.tensor
                def _(t, slot=slot, XTLD=XTLD, OHDV=OHDV):
                    self.wt(t, "sXT%d" % slot, XTLD)
                    self.wt(t, "dv", OHDV)
                    for Ti, pb, bk in ((0, self.psE, "E"), (1, self.psF, "F")):
                        self.use_bank(t, bk)
                        for dj in range(8):
                            mm = t.matmul(pb[:, dj * 32:(dj + 1) * 32],
                                          xtmb[:, slot, dj * 128:(dj + 1) * 128],
                                          oneh[:, Ti, :], start=True, stop=True)
                        self.inc(mm, "mm", 1)
                        self.free_bank_at(bk, "dv", self.at("dv") + 1)
                xtm_free[slot] = self.at("mm")
                self.free_buf("oneh", "mm")
                GAMM = self.at("mm")

                @block.vector
                def _(v, c16=c16, GAMM=GAMM):
                    self.wt(v, "mm", GAMM)
                    v.tensor_copy(xselK[:, 0:8, c16 * 32:(c16 + 1) * 32],
                                  self.psE[:, 0:256].rearrange("p (c f) -> p c f", c=8))
                    self.inc(v.tensor_copy(xselV[:, 0:8, c16 * 32:(c16 + 1) * 32],
                                           self.psF[:, 0:256].rearrange("p (c f) -> p c f", c=8)), "dv", 1)
            self.free_buf("bcidx", "dv")
            XSELDV = self.at("dv")

            SKMM0 = [0]

            @block.tensor
            def _(t, XSELDV=XSELDV, SKMM0=SKMM0):
                self.wt(t, "dv", XSELDV)
                first = True
                AC0 = self.at("ac")
                for cc in range(8):
                    bank = "AB"[cc % 2]
                    self.use_bank(t, bank)
                    pb = (self.psA, self.psB)[cc % 2]
                    for dj in range(8):
                        mm = t.matmul(pb[:], self.wk_s[:, dj, cc * 128:(cc + 1) * 128],
                                      xselK[:, dj, :], start=(dj == 0), stop=(dj == 7))
                    v_ = self.inc(mm, "mm", 1)
                    if first:
                        SKMM0[0] = v_ - 1
                        first = False
                    self.free_bank_at(bank, "ac", AC0 + cc + 1)

            @block.scalar
            def _(sc, g4=g4, SKMM0=SKMM0):
                for cc in range(8):
                    pb = (self.psA, self.psB)[cc % 2]
                    sc.wait_ge(self.h["mm"], SKMM0[0] + cc + 1)
                    self.inc(sc.activation(skfm[:, cc, g4 * 512:(g4 + 1) * 512], pb[:],
                                           AF.Identity, bias=self.bk_c[:, cc:cc + 1]), "ac", 1)

            SVMM0 = [0]

            @block.tensor
            def _(t, SVMM0=SVMM0):
                first = True
                AC0 = self.at("ac")
                for s4 in range(4):
                    for half in range(2):
                        bank = "CD"[half]
                        self.use_bank(t, bank)
                        pb = (self.psC, self.psD)[half]
                        for dj in range(8):
                            t.matmul(pb[:], xselV[:, dj, s4 * 128:(s4 + 1) * 128],
                                     self.wv_s[:, dj, half * 512:(half + 1) * 512],
                                     start=(dj == 0), stop=False)
                        mm = t.matmul(pb[:], self.onesb[:], self.bvrow[:, half * 512:(half + 1) * 512],
                                      start=False, stop=True)
                        v_ = self.inc(mm, "mm", 1)
                        if first:
                            SVMM0[0] = v_ - 1
                            first = False
                        self.free_bank_at(bank, "ac", AC0 + s4 * 2 + half + 1)

            @block.scalar
            def _(sc, g4=g4, SVMM0=SVMM0):
                for s4 in range(4):
                    for half in range(2):
                        pb = (self.psC, self.psD)[half]
                        sc.wait_ge(self.h["mm"], SVMM0[0] + s4 * 2 + half + 1)
                        self.inc(sc.copy(svtm[:, g4 * 4 + s4, half * 512:(half + 1) * 512], pb[:]), "ac", 1)
        P3AC = self.at("ac")

        if self.debug:
            @block.sync
            def _(s, P3AC=P3AC):
                self.wt(s, "ac", P3AC)
                self.inc(s.dma_start(self.dbg["d_sk"], skfm[:]), "sDBG", 16)
                self.inc(s.dma_start(self.dbg["d_sv"], svtm[:]), "sDBG", 16)

        mbuf = E(nc.sbuf_tensor("mbuf4", [128, 2, 640], BF16))
        probs = E(nc.sbuf_tensor("probs4", [128, 2, 640], BF16))
        probsT = E(nc.sbuf_tensor("probsT4", [128, 2, 5, 128], BF16))
        negmax = E(nc.sbuf_tensor("negmax4", [128, 2], F32))
        rowsum = E(nc.sbuf_tensor("rowsum4", [128, 2], F32))
        rsum = E(nc.sbuf_tensor("rsum4", [128, 2], F32))
        oebuf = E(nc.sbuf_tensor("oebuf4", [128, D], BF16))
        self.fr = dict(mbuf=mbuf, probs=probs, probsT=probsT, negmax=negmax,
                       rowsum=rowsum, rsum=rsum, npar=2)

        jobs = []
        for t in range(NSEL_T):
            jobs.append((t, "sel"))
            if t < NCOMP_T:
                jobs.append((t, "comp"))

        def p4_front(j):
            t, kind = jobs[j]
            if kind == "sel" and t % 4 == 0:
                self.emit_qproj(di["xownT"], t // 4, xob, qfm, wq_s, self.WQLD)
            if kind == "sel":
                return self.front_a(t, qfm, t % 4, skfm, 2048, di["msel"], kind="sel", par=j)
            return self.front_a(t, qfm, t % 4, self.ckfm, 256, di["mcomp"], kind="comp", par=j)

        pend4 = p4_front(0)
        for j in range(len(jobs)):
            t, kind = jobs[j]
            ctx = pend4
            if j + 1 < len(jobs):
                pend4 = p4_front(j + 1)
            if kind == "sel":
                self.front_b(ctx, svtm, obuf=oebuf, flag_col=1)
            else:
                self.front_b(ctx, self.cvtm, obuf=oebuf, flag_col=0)

    def emit_qproj(self, xsrc, g, xob, qfm, wq_s, wrdy):
        block = self.block

        @block.gpsimd
        def _(gp, g=g):
            self.use_buf(gp, "xob")
            src = xsrc.rearrange("(c p) t -> p c t", p=128)
            for half in range(2):
                self.inc(gp.dma_start(xob[:, half * 4:half * 4 + 4, :],
                                      src[:, half * 4:half * 4 + 4, g * 512:(g + 1) * 512]), "sXOB", 16)
        XOLD = self.at("sXOB")

        QMM0 = [0]

        @block.tensor
        def _(t, XOLD=XOLD, QMM0=QMM0, wrdy=wrdy):
            self.wt(t, "sXOB", XOLD)
            self.wt(t, wrdy[0], wrdy[1])
            self.use_buf(t, "qfm")
            first = True
            AC0 = self.at("ac")
            for cc in range(8):
                self.use_bank(t, "G")
                for dj in range(8):
                    mm = t.matmul(self.psG[:], wq_s[:, dj, cc * 128:(cc + 1) * 128],
                                  xob[:, dj, :], start=(dj == 0), stop=(dj == 7))
                v_ = self.inc(mm, "mm", 1)
                if first:
                    QMM0[0] = v_ - 1
                    first = False
                self.free_bank_at("G", "ac", AC0 + cc + 1)
        self.free_buf("xob", "mm")

        @block.scalar
        def _(sc, QMM0=QMM0):
            for cc in range(8):
                sc.wait_ge(self.h["mm"], QMM0[0] + cc + 1)
                self.inc(sc.activation(qfm[:, cc, :], self.psG[:], AF.Identity,
                                       bias=self.bq_c[:, cc:cc + 1],
                                       scale=self.cons[:, EIGHTH:EIGHTH + 1]), "ac", 1)
        self.q_ready = self.at("ac")

    def front_a(self, t, qfm, ti, kbuf, nkv, msrc, kind, win_third=None, gates=None, par=0):
        block = self.block
        fr = self.fr
        par %= fr["npar"]
        mbuf = fr["mbuf"]
        probs = fr["probs"][:, par]
        negmax = fr["negmax"][:, par:par + 1]
        rowsum = fr["rowsum"][:, par:par + 1]
        rsum = fr["rsum"][:, par:par + 1]
        P = "%d" % par
        if kind == "win":
            lo, n = 128 * (t - win_third), 640
            vchunk0 = t - win_third
        else:
            lo, n = sel_window(t, nkv)
            vchunk0 = lo // 128
        nck = (n + 127) // 128
        mslot = self.msl
        self.msl = (self.msl + 1) % 2
        sb0, sb1 = (("A", "B"), ("C", "D"))[self.scpar]
        pb0, pb1 = ((self.psA, self.psB), (self.psC, self.psD))[self.scpar]
        self.scpar ^= 1
        n0 = min(n, 512)
        n1 = n - n0

        @block.sync
        def _(s, t=t, n=n, mslot=mslot):
            self.use_buf(s, "mbuf%d" % mslot)
            self.inc(s.dma_start(mbuf[:, mslot, 0:n], msrc[t, :, 0:n]), "sMB%d" % mslot, 16)
        MLD = self.at("sMB%d" % mslot)

        SCMM = [0]
        GMM = [0]

        @block.tensor
        def _(tt, ti=ti, lo=lo, n0=n0, n1=n1, SCMM=SCMM, GMM=GMM, gates=gates):
            self.wt(tt, "ac", self.q_ready)
            self.use_bank(tt, sb0)
            self.use_bank(tt, sb1)
            if gates is not None:
                xob, wg_s = gates[0], gates[1]
                self.use_bank(tt, "E")
                for dj in range(8):
                    tt.matmul(self.psE[:, 0:3], xob[:, dj, ti * 128:(ti + 1) * 128],
                              wg_s[:, dj, :], start=(dj == 0), stop=False)
                mm = tt.matmul(self.psE[:, 0:3], self.onesb[:], self.bgrow[:], start=False, stop=True)
                GMM[0] = self.inc(mm, "mm", 1)
            last = None
            for dj in range(8):
                last = tt.matmul(pb0[:, 0:n0], qfm[:, dj, ti * 128:(ti + 1) * 128],
                                 kbuf[:, dj, lo:lo + n0], start=(dj == 0), stop=(dj == 7))
                if n1 > 0:
                    last = tt.matmul(pb1[:, 0:n1], qfm[:, dj, ti * 128:(ti + 1) * 128],
                                     kbuf[:, dj, lo + n0:lo + n], start=(dj == 0), stop=(dj == 7))
            SCMM[0] = self.inc(last, "mm", 1)
        if gates is not None:
            self.free_buf("xob", "mm")

        @block.vector
        def _(v, n=n, SCMM=SCMM, MLD=MLD, mslot=mslot):
            self.wt(v, "mm", SCMM[0])
            self.wt(v, "sMB%d" % mslot, MLD)
            self.use_buf(v, "negmax" + P)
            if n > 512:
                self.vsync(v, v.tensor_tensor(pb0[:, 0:512], pb0[:, 0:512], mbuf[:, mslot, 0:512], ALU.add))
                self.vsync(v, v.tensor_tensor(pb1[:, 0:n - 512], pb1[:, 0:n - 512], mbuf[:, mslot, 512:n], ALU.add))
                self.vsync(v, v.reduce_max(negmax[:], pb0[:, 0:512], axis=AX.X, negate=True))
                self.vsync(v, v.reduce_max(rsum[:], pb1[:, 0:n - 512], axis=AX.X, negate=True))
                self.inc(v.tensor_tensor(negmax[:], negmax[:], rsum[:], ALU.min), "dv", 1)
            else:
                self.vsync(v, v.tensor_tensor(pb0[:, 0:n], pb0[:, 0:n], mbuf[:, mslot, 0:n], ALU.add))
                self.inc(v.reduce_max(negmax[:], pb0[:, 0:n], axis=AX.X, negate=True), "dv", 1)
        MSKDV = self.at("dv")
        self.free_buf("mbuf%d" % mslot, "dv")

        GEXAC = [0]

        @block.scalar
        def _(sc, n=n, MSKDV=MSKDV, GMM=GMM, gates=gates, GEXAC=GEXAC, SCMM=SCMM):
            if gates is not None:
                gex = gates[2]
                sc.wait_ge(self.h["mm"], GMM[0])
                self.use_buf(sc, "gex")
                GEXAC[0] = self.inc(sc.activation(gex[:], self.psE[:, 0:3], AF.Exp,
                                                  scale=self.cons[:, NEG1:NEG1 + 1]), "ac", 1)
            self.wt(sc, "dv", MSKDV)
            self.use_buf(sc, "probs" + P)
            self.use_buf(sc, "rowsum" + P)
            if n > 512:
                sc.activation(probs[:, 0:512], pb0[:, 0:512], AF.Exp,
                              bias=negmax[:, 0:1], accum_out=rowsum[:, 0:1])
                self.inc(sc.activation(probs[:, 512:n], pb1[:, 0:n - 512], AF.Exp,
                                       bias=negmax[:, 0:1], accum_out=rsum[:, 0:1]), "ac", 1)
            else:
                self.inc(sc.activation(probs[:, 0:n], pb0[:, 0:n], AF.Exp,
                                       bias=negmax[:, 0:1], accum_out=rowsum[:, 0:1]), "ac", 1)
        EXPAC = self.at("ac")
        self.free_buf("negmax" + P, "ac")
        self.free_bank(sb0, "ac")
        self.free_bank(sb1, "ac")
        if gates is not None:
            self.free_bank("E", "ac")

        GDV = 0
        if gates is not None:
            gex, gcolb, gslot = gates[2], gates[3], gates[4]

            @block.vector
            def _(v, GEXAC=GEXAC, gslot=gslot):
                self.wt(v, "ac", GEXAC[0])
                self.use_buf(v, "gcol%d" % gslot)
                self.vsync(v, v.tensor_scalar(gcolb[:, gslot, :], gex[:],
                                              self.cons[:, ONEC:ONEC + 1], None, ALU.add))
                self.vsync(v, v.reciprocal(gcolb[:, gslot, :], gcolb[:, gslot, :]))
                self.inc(v.engine_nop(), "dv", 1)
            GDV = self.at("dv")
            self.free_buf("gex", "dv")

        @block.vector
        def _(v, n=n, EXPAC=EXPAC):
            self.wt(v, "ac", EXPAC)
            self.use_buf(v, "rsum" + P)
            if n > 512:
                self.vsync(v, v.tensor_tensor(rowsum[:], rowsum[:], rsum[:], ALU.add))
            self.vsync(v, v.reciprocal(rsum[:], rowsum[:]))
            self.inc(v.engine_nop(), "dv", 1)
        RDV = self.at("dv")
        self.free_buf("rowsum" + P, "dv")
        return dict(t=t, kind=kind, n=n, n0=n0, n1=n1, nck=nck, lo=lo, vchunk0=vchunk0,
                    EXPAC=EXPAC, RDV=RDV, GDV=GDV, gates=gates, par=par)

    def front_b(self, ctx, vbuf, obuf=None, flag_col=None, comb=None):
        block = self.block
        fr = self.fr
        par = ctx["par"]
        probs = fr["probs"][:, par]
        probsT = fr["probsT"][:, par]
        rsum = fr["rsum"][:, par:par + 1]
        P = "%d" % par
        t, kind, n, nck, vchunk0 = ctx["t"], ctx["kind"], ctx["n"], ctx["nck"], ctx["vchunk0"]
        EXPAC, RDV = ctx["EXPAC"], ctx["RDV"]
        TMM = [0]

        @block.tensor
        def _(tt, nck=nck, n=n, EXPAC=EXPAC, TMM=TMM):
            self.wt(tt, "ac", EXPAC)
            self.use_bank(tt, "T")
            for ci in range(nck):
                w_ = min(128, n - ci * 128)
                mm = tt.transpose(self.psT[0:w_, ci, :], probs[:, ci * 128:ci * 128 + w_], self.idb[:])
            TMM[0] = self.inc(mm, "mm", 1)
            self.free_bank_at("T", "dv", self.at("dv") + 1)
        self.free_buf("probs" + P, "mm")

        @block.vector
        def _(v, nck=nck, TMM=TMM):
            self.wt(v, "mm", TMM[0])
            self.use_buf(v, "probsT" + P)
            self.inc(v.tensor_copy(probsT[:, 0:nck, :], self.psT[:, 0:nck, :]), "dv", 1)
        PTDV = self.at("dv")

        AVMM = [0]

        @block.tensor
        def _(tt, nck=nck, n=n, vchunk0=vchunk0, PTDV=PTDV, AVMM=AVMM):
            self.wt(tt, "dv", PTDV)
            self.use_bank(tt, "E")
            self.use_bank(tt, "F")
            for half, pe in ((0, self.psE), (1, self.psF)):
                for ci in range(nck):
                    w_ = min(128, n - ci * 128)
                    mm = tt.matmul(pe[:], probsT[0:w_, ci, :],
                                   vbuf[0:w_, vchunk0 + ci, half * 512:(half + 1) * 512],
                                   start=(ci == 0), stop=(ci == nck - 1))
                self.inc(mm, "mm", 1)
            AVMM[0] = self.at("mm")
        self.free_buf("probsT" + P, "mm")

        if kind == "win":
            gcolb, gslot = ctx["gates"][3], ctx["gates"][4]
            if self.debug and t == 0:
                @block.sync
                def _(s, EXPAC=EXPAC, RDV=RDV, PTDV=PTDV):
                    self.wt(s, "ac", EXPAC)
                    self.wt(s, "dv", PTDV)
                    self.inc(s.dma_start(self.dbg["d_probs"], probs[:]), "sDBG", 16)
                    self.inc(s.dma_start(self.dbg["d_pT"], probsT[:]), "sDBG", 16)
                self.free_buf("probs" + P, "sDBG")
                self.free_buf("probsT" + P, "sDBG")

            @block.vector
            def _(v, RDV=RDV, gslot=gslot):
                self.wt(v, "dv", RDV)
                self.wt(v, "dv", ctx["GDV"])
                self.inc(v.tensor_tensor(rsum[:], rsum[:], gcolb[:, gslot, 2:3], ALU.mult), "dv", 1)
            RG2 = self.at("dv")

            if self.debug and t == 0:
                @block.sync
                def _(s, RG2=RG2):
                    self.wt(s, "dv", RG2)
                    self.inc(s.dma_start(self.dbg["d_rsum"][:, 0:1], rsum[:], allow_slow_non_contiguous=True), "sDBG", 16)
                self.free_buf("rsum" + P, "sDBG")

            @block.scalar
            def _(sc, AVMM=AVMM, RG2=RG2):
                self.wt(sc, "mm", AVMM[0])
                self.wt(sc, "dv", RG2)
                self.use_buf(sc, "comb")
                sc.activation(comb[:, 0:512], self.psE[:], AF.Identity, scale=rsum[:, 0:1])
                self.inc(sc.activation(comb[:, 512:1024], self.psF[:], AF.Identity, scale=rsum[:, 0:1]), "ac", 1)
            self.free_buf("rsum" + P, "ac")
            self.free_bank("E", "ac")
            self.free_bank("F", "ac")
            self.comb_ac = self.at("ac")
        else:
            @block.vector
            def _(v, RDV=RDV, AVMM=AVMM, flag_col=flag_col):
                self.wt(v, "dv", RDV)
                self.wt(v, "mm", AVMM[0])
                self.use_buf(v, "oebuf")
                self.vsync(v, v.tensor_tensor(rsum[:], rsum[:], self.flg[:, flag_col:flag_col + 1], ALU.mult))
                v.tensor_scalar(obuf[:, 0:512], self.psE[:], rsum[:, 0:1], None, ALU.mult)
                self.inc(v.tensor_scalar(obuf[:, 512:1024], self.psF[:], rsum[:, 0:1], None, ALU.mult), "dv", 1)
            ODV = self.at("dv")
            self.free_buf("rsum" + P, "dv")
            self.free_bank("E", "dv")
            self.free_bank("F", "dv")
            dst = self.sosel[t] if kind == "sel" else self.socomp[t]
            stn = "sOSEL" if kind == "sel" else "sOCMP"

            @block.sync
            def _(s, dst=dst, ODV=ODV):
                self.wt(s, "dv", ODV)
                self.inc(s.dma_start(dst, obuf[:]), stn, 16)
            self.free_buf("oebuf", stn)

    def p5_main(self, ctx):
        nc, di, block = self.nc, self.di, self.block
        E = ctx.enter_context
        kwin = E(nc.sbuf_tensor("kwin", [128, 8, 1536], BF16))
        vwin = E(nc.sbuf_tensor("vwin", [128, 12, D], BF16))
        wq_s = E(nc.sbuf_tensor("wq_s5", [128, 8, D], BF16))
        wo_s = E(nc.sbuf_tensor("wo_s", [128, 8, D], BF16))
        wg_s = E(nc.sbuf_tensor("wg_s", [128, 8, 3], BF16))
        fkq = E(nc.sbuf_tensor("fkq", [128, 8, D], FP8))
        fbbc = E(nc.sbuf_tensor("fbbc", [128, D], BF16))
        xwb = E(nc.sbuf_tensor("xwb", [128, 8, 2, 512], BF16))
        xob = E(nc.sbuf_tensor("xob5", [128, 8, 512], BF16))
        qfm = E(nc.sbuf_tensor("qfm5", [128, 8, 512], BF16))
        mbuf = E(nc.sbuf_tensor("mbuf5", [128, 2, 640], BF16))
        probs = E(nc.sbuf_tensor("probs5", [128, 1, 640], BF16))
        probsT = E(nc.sbuf_tensor("probsT5", [128, 1, 5, 128], BF16))
        negmax = E(nc.sbuf_tensor("negmax5", [128, 1], F32))
        rowsum = E(nc.sbuf_tensor("rowsum5", [128, 1], F32))
        rsum = E(nc.sbuf_tensor("rsum5", [128, 1], F32))
        gcolb = E(nc.sbuf_tensor("gcolb", [128, 2, 3], F32))
        gex = E(nc.sbuf_tensor("gex", [128, 3], F32))
        comb = E(nc.sbuf_tensor("comb", [128, D], BF16))
        combT = E(nc.sbuf_tensor("combT", [128, 8, 128], BF16))
        oselb = E(nc.sbuf_tensor("oselb", [128, 2, D], BF16))
        ocompb = E(nc.sbuf_tensor("ocompb", [128, 2, D], BF16))
        xot = E(nc.sbuf_tensor("xot", [128, 2, D], F32))
        ztile = E(nc.sbuf_tensor("ztile", [128, D], F32))
        zscr = E(nc.sbuf_tensor("zscr", [128, D], BF16))
        y1 = E(nc.sbuf_tensor("y1", [128, D], F32))
        y1q = E(nc.sbuf_tensor("y1q", [128, 8, 128], FP8))
        ffnb = E(nc.sbuf_tensor("ffnb", [128, D], BF16))
        cols = E(nc.sbuf_tensor("cols", [128, 4], F32))
        lnw = E(nc.sbuf_tensor("lnw", [128, D], BF16))
        self.lnw = lnw
        self.fr = dict(mbuf=mbuf, probs=probs, probsT=probsT, negmax=negmax,
                       rowsum=rowsum, rsum=rsum, npar=1)

        @block.gpsimd
        def _(g):
            self.fence(g)
            fm128 = lambda ap: ap.rearrange("(c p) d -> p c d", p=128)
            for dst, src in [
                (wq_s[:], fm128(di["wqT"])),
                (wo_s[:], fm128(di["woT"])),
                (wg_s[:], fm128(di["wgT"])),
            ]:
                self.inc(g.dma_start(dst, src), "sW5", 16)
        WLD = ("sW5", self.at("sW5"))

        @block.sync
        def _(s):
            self.fence(s)
            self.wt(s, "sOSEL")
            self.wt(s, "sOCMP")
            self.inc(s.dma_start(xwb[:].rearrange("p c a b -> p c (a b)"),
                                 di["fker"].rearrange("(c p) d -> p c d", p=128)), "sFK", 16)
            self.inc(s.dma_start(fbbc[:], di["fbias_bc"]), "sFK", 16)
        FKLD = self.at("sFK")

        @block.vector
        def _(v, FKLD=FKLD):
            self.wt(v, "sFK", FKLD)
            for j in range(8):
                self.inc(v.tensor_scalar(fkq[:, j, :], xwb[:, j, :, :].rearrange("p a b -> p (a b)"),
                                         self.scal[:, 2:3], None, ALU.mult), "dv", 1)
        self.FKQDV = self.at("dv")

        xw_free = [0, 0]
        self.kvw_use = 0

        for TH, (T0, T1) in enumerate(THIRDS):
            for c in range(3):
                slot = c % 2

                @block.gpsimd
                def _(g, T0=T0, c=c, slot=slot):
                    self.wt(g, "dv", self.FKQDV)
                    if xw_free[slot]:
                        g.wait_ge(self.h["mm"], xw_free[slot])
                    src = di["xwinT"].rearrange("(c p) t -> p c t", p=128)
                    base = 128 * T0 + c * 512
                    for half in range(2):
                        self.inc(g.dma_start(xwb[:, half * 4:half * 4 + 4, slot, :],
                                             src[:, half * 4:half * 4 + 4, base:base + 512]),
                                 "sXW%d" % slot, 16)
                XWLD = self.at("sXW%d" % slot)

                KMM0 = [0]

                @block.tensor
                def _(t, c=c, slot=slot, XWLD=XWLD, KMM0=KMM0, KVW=self.kvw_use,
                      KDBG=getattr(self, "kvw_dbg", None)):
                    self.wt(t, "sXW%d" % slot, XWLD)
                    if c == 0:
                        self.wt(t, WLD[0], WLD[1])
                        if KVW:
                            t.wait_ge(self.h["mm"], KVW)
                        if KDBG:
                            self.wt(t, KDBG[0], KDBG[1])
                    first = True
                    AC0 = self.at("ac")
                    for cc in range(8):
                        bank = "AB"[cc % 2]
                        self.use_bank(t, bank)
                        pb = (self.psA, self.psB)[cc % 2]
                        for dj in range(8):
                            mm = t.matmul(pb[:], self.wk_s[:, dj, cc * 128:(cc + 1) * 128],
                                          xwb[:, dj, slot, :], start=(dj == 0), stop=(dj == 7))
                        v_ = self.inc(mm, "mm", 1)
                        if first:
                            KMM0[0] = v_ - 1
                            first = False
                        self.free_bank_at(bank, "ac", AC0 + cc + 1)

                @block.scalar
                def _(sc, c=c, KMM0=KMM0):
                    for cc in range(8):
                        pb = (self.psA, self.psB)[cc % 2]
                        sc.wait_ge(self.h["mm"], KMM0[0] + cc + 1)
                        self.inc(sc.activation(kwin[:, cc, c * 512:(c + 1) * 512], pb[:],
                                               AF.Identity, bias=self.bk_c[:, cc:cc + 1]), "ac", 1)

                VMM0 = [0]

                @block.tensor
                def _(t, slot=slot, VMM0=VMM0):
                    first = True
                    AC0 = self.at("ac")
                    for s4 in range(4):
                        for half in range(2):
                            bank = "CD"[half]
                            self.use_bank(t, bank)
                            pb = (self.psC, self.psD)[half]
                            for dj in range(8):
                                t.matmul(pb[:], xwb[:, dj, slot, s4 * 128:(s4 + 1) * 128],
                                         self.wv_s[:, dj, half * 512:(half + 1) * 512],
                                         start=(dj == 0), stop=False)
                            mm = t.matmul(pb[:], self.onesb[:],
                                          self.bvrow[:, half * 512:(half + 1) * 512],
                                          start=False, stop=True)
                            v_ = self.inc(mm, "mm", 1)
                            if first:
                                VMM0[0] = v_ - 1
                                first = False
                            self.free_bank_at(bank, "ac", AC0 + s4 * 2 + half + 1)
                xw_free[slot] = self.at("mm")

                @block.scalar
                def _(sc, c=c, VMM0=VMM0):
                    for s4 in range(4):
                        for half in range(2):
                            pb = (self.psC, self.psD)[half]
                            sc.wait_ge(self.h["mm"], VMM0[0] + s4 * 2 + half + 1)
                            self.inc(sc.copy(vwin[:, c * 4 + s4, half * 512:(half + 1) * 512], pb[:]), "ac", 1)
            KVW_AC = self.at("ac")

            if self.debug and TH == 0:
                @block.sync
                def _(s, KVW_AC=KVW_AC):
                    self.wt(s, "ac", KVW_AC)
                    self.inc(s.dma_start(self.dbg["d_kw"], kwin[:]), "sDBG", 16)
                    self.inc(s.dma_start(self.dbg["d_vw"], vwin[:]), "sDBG", 16)
                self.kvw_dbg = ("sDBG", self.at("sDBG"))

            pend = [None]

            def p5_front(t):
                ti = t % 4
                slot = t % 2
                OSLD = OCLD = 0
                if t < NSEL_T:
                    @block.sync
                    def _(s, t=t, slot=slot):
                        self.wt(s, "dv", self.FKQDV)
                        self.use_buf(s, "oselb%d" % slot)
                        self.inc(s.dma_start(oselb[:, slot, :], self.sosel[t]), "sOSL%d" % slot, 16)
                    OSLD = self.at("sOSL%d" % slot)
                if t < NCOMP_T:
                    @block.sync
                    def _(s, t=t, slot=slot):
                        self.use_buf(s, "ocompb%d" % slot)
                        self.inc(s.dma_start(ocompb[:, slot, :], self.socomp[t]),
                                 "sOCB%d" % slot, 16)
                    OCLD = self.at("sOCB%d" % slot)

                @block.sync
                def _(s, t=t, slot=slot):
                    self.use_buf(s, "xot%d" % slot)
                    self.inc(s.dma_start(xot[:, slot, :], di["xown"][t * 128:(t + 1) * 128, :]),
                             "sXOT%d" % slot, 16)
                XOTLD = self.at("sXOT%d" % slot)
                ctx = self.front_a(t, qfm, ti, kwin, S, di["mwin"], kind="win",
                                   win_third=T0, gates=(xob, wg_s, gex, gcolb, slot))
                ctx.update(OSLD=OSLD, OCLD=OCLD, XOTLD=XOTLD, slot=slot)
                return ctx

            pendb = [None]
            for t in range(T0, T1):
                g, ti = t // 4, t % 4
                if pendb[0] is None:
                    if ti == 0:
                        self.emit_qproj(di["xownT"], g, xob, qfm, wq_s, WLD)
                    ctx = p5_front(t)
                    self.front_b(ctx, vwin, comb=comb)
                    self.kvw_use = self.at("mm")
                else:
                    ctx = pendb[0]
                slot = ctx["slot"]
                OSLD, OCLD, XOTLD = ctx["OSLD"], ctx["OCLD"], ctx["XOTLD"]
                GDV = ctx["GDV"]

                if self.debug and t < 4:
                    @block.sync
                    def _(s, t=t, GDV=GDV, slot=slot):
                        self.wt(s, "dv", GDV)
                        self.inc(s.dma_start(self.dbg["d_gate"][:, t, :], gcolb[:, slot, :]), "sDBG", 16)
                    self.free_buf("gcol%d" % slot, "sDBG")

                if PIPELINE and t + 1 < T1:
                    if (t + 1) % 4 == 0:
                        self.emit_qproj(di["xownT"], (t + 1) // 4, xob, qfm, wq_s, WLD)
                    pend[0] = p5_front(t + 1)
                else:
                    pend[0] = None

                @block.vector
                def _(v, t=t, slot=slot, OSLD=OSLD, OCLD=OCLD):
                    self.wt(v, "ac", self.comb_ac)
                    if t < NSEL_T:
                        self.wt(v, "sOSL%d" % slot, OSLD)
                        v.scalar_tensor_tensor(comb[:], oselb[:, slot, :], gcolb[:, slot, 1:2],
                                               comb[:], ALU.mult, ALU.add)
                    if t < NCOMP_T:
                        self.wt(v, "sOCB%d" % slot, OCLD)
                        self.inc(v.scalar_tensor_tensor(comb[:], ocompb[:, slot, :], gcolb[:, slot, 0:1],
                                                        comb[:], ALU.mult, ALU.add), "dv", 1)
                    elif t < NSEL_T:
                        self.inc(v.engine_nop(), "dv", 1)
                    else:
                        self.inc(v.engine_nop(), "dv", 1)
                CBDV = self.at("dv")
                self.free_buf("comb", "dv")
                self.free_buf("gcol%d" % slot, "dv")
                if t < NSEL_T:
                    self.free_buf("oselb%d" % slot, "dv")
                if t < NCOMP_T:
                    self.free_buf("ocompb%d" % slot, "dv")



                CTMM = [0]

                @block.tensor
                def _(tt, CBDV=CBDV, CTMM=CTMM):
                    self.wt(tt, "dv", CBDV)
                    self.use_bank(tt, "T")
                    for dj in range(8):
                        mm = tt.transpose(self.psT[:, dj, :], comb[:, dj * 128:(dj + 1) * 128], self.idb[:])
                    CTMM[0] = self.inc(mm, "mm", 1)
                    self.free_bank_at("T", "dv", self.at("dv") + 1)
                self.free_buf("comb", "mm")

                @block.vector
                def _(v, CTMM=CTMM):
                    self.wt(v, "mm", CTMM[0])
                    self.use_buf(v, "combT")
                    self.inc(v.tensor_copy(combT[:], self.psT[:]), "dv", 1)
                CTDV = self.at("dv")

                ZDV = 0
                for half in range(2):
                    WOMM = [0]

                    @block.tensor
                    def _(tt, half=half, CTDV=CTDV, WOMM=WOMM):
                        self.wt(tt, "dv", CTDV)
                        self.use_bank(tt, "G")
                        for dj in range(8):
                            tt.matmul(self.psG[:], combT[:, dj, :],
                                      wo_s[:, dj, half * 512:(half + 1) * 512],
                                      start=(dj == 0), stop=False)
                        mm = tt.matmul(self.psG[:], self.onesb[:],
                                       self.borow[:, half * 512:(half + 1) * 512],
                                       start=False, stop=True)
                        WOMM[0] = self.inc(mm, "mm", 1)

                    @block.vector
                    def _(v, half=half, slot=slot, WOMM=WOMM, XOTLD=XOTLD):
                        self.wt(v, "mm", WOMM[0])
                        self.wt(v, "sXOT%d" % slot, XOTLD)
                        if half == 0:
                            self.use_buf(v, "ztile")
                        self.inc(v.tensor_tensor(ztile[:, half * 512:(half + 1) * 512], self.psG[:],
                                                 xot[:, slot, half * 512:(half + 1) * 512], ALU.add), "dv", 1)
                    self.free_bank("G", "dv")
                    ZDV = self.at("dv")
                self.free_buf("combT", "mm")
                self.free_buf("xot%d" % slot, "dv")

                if pend[0] is not None:
                    self.front_b(pend[0], vwin, comb=comb)
                    self.kvw_use = self.at("mm")
                pendb[0] = pend[0]

                if self.debug and t < 4:
                    @block.sync
                    def _(s, t=t, ZDV=ZDV):
                        self.wt(s, "dv", ZDV)
                        self.inc(s.dma_start(self.dbg["d_z"][t], ztile[:]), "sDBG", 16)
                    self.free_buf("ztile", "sDBG")

                self.emit_ln(ztile, y1, zscr, cols, ZDV, dst_buf="y1ot")
                Y1DV = self.at("dv")

                if self.debug and t < 4:
                    @block.sync
                    def _(s, t=t, Y1DV=Y1DV):
                        self.wt(s, "dv", Y1DV)
                        self.inc(s.dma_start(self.dbg["d_y1"][t], y1[:]), "sDBG", 16)

                @block.vector
                def _(v, Y1DV=Y1DV):
                    self.wt(v, "dv", Y1DV)
                    self.inc(v.tensor_copy(zscr[:], y1[:]), "dv", 1)
                Y1BDV = self.at("dv")

                FTMM = [0]

                @block.tensor
                def _(tt, Y1BDV=Y1BDV, FTMM=FTMM):
                    self.wt(tt, "dv", Y1BDV)
                    self.use_bank(tt, "T")
                    for dj in range(8):
                        mm = tt.transpose(self.psT[:, dj, :], zscr[:, dj * 128:(dj + 1) * 128], self.idb[:])
                    FTMM[0] = self.inc(mm, "mm", 1)
                    self.free_bank_at("T", "dv", self.at("dv") + 1)

                @block.vector
                def _(v, FTMM=FTMM):
                    self.wt(v, "mm", FTMM[0])
                    self.use_buf(v, "y1q")
                    self.inc(v.tensor_scalar(y1q[:].rearrange("p c d -> p (c d)"),
                                             self.psT[:].rearrange("p c d -> p (c d)"),
                                             self.scal[:, 0:1], None, ALU.mult), "dv", 1)
                YQDV = self.at("dv")

                for half in range(2):
                    FFMM = [0]

                    @block.tensor
                    def _(tt, half=half, YQDV=YQDV, FFMM=FFMM):
                        self.wt(tt, "dv", YQDV)
                        self.use_bank(tt, "G")
                        for dj in range(8):
                            mm = tt.matmul(self.psG[:], y1q[:, dj, :],
                                           fkq[:, dj, half * 512:(half + 1) * 512],
                                           start=(dj == 0), stop=(dj == 7))
                        FFMM[0] = self.inc(mm, "mm", 1)

                    @block.scalar
                    def _(sc, half=half, FFMM=FFMM):
                        self.wt(sc, "mm", FFMM[0])
                        if half == 0:
                            self.use_buf(sc, "ffnb")
                        self.inc(sc.activation(ffnb[:, half * 512:(half + 1) * 512], self.psG[:],
                                               AF.Identity, scale=self.scal[:, 1:2]), "ac", 1)
                    self.free_bank("G", "ac")
                FFAC = self.at("ac")
                self.free_buf("y1q", "mm")

                @block.vector
                def _(v, FFAC=FFAC):
                    self.wt(v, "ac", FFAC)
                    v.tensor_tensor(ffnb[:], ffnb[:], fbbc[:], ALU.add)
                    self.inc(v.tensor_tensor(ztile[:], y1[:], ffnb[:], ALU.add), "dv", 1)
                Z2DV = self.at("dv")
                self.free_buf("ffnb", "dv")

                self.emit_ln(ztile, y1, zscr, cols, Z2DV, dst_buf="y1ot")
                ODV = self.at("dv")
                self.free_buf("ztile", "dv")

                @block.sync
                def _(s, t=t, ODV=ODV):
                    self.wt(s, "dv", ODV)
                    self.inc(s.dma_start(self.out[t * 128:(t + 1) * 128, :], y1[:]), "sOUT", 16)
                self.free_buf("y1ot", "sOUT")

    def emit_ln(self, src, dst, scr, cols, SRCDV, dst_buf=None):
        block = self.block

        @block.scalar
        def _(sc, SRCDV=SRCDV):
            self.wt(sc, "dv", SRCDV)
            self.use_buf(sc, "cols")
            self.inc(sc.activation(self.lnw[:], src[:], AF.Identity, accum_out=cols[:, 0:1]), "ac", 1)
        SAC = self.at("ac")

        @block.vector
        def _(v, SAC=SAC):
            self.wt(v, "ac", SAC)
            self.vsync(v, v.scalar_tensor_tensor(scr[:], src[:], self.cons[:, ONEC:ONEC + 1],
                                                 src[:], ALU.mult, ALU.mult,
                                                 accum_out=cols[:, 1:2]))
            self.vsync(v, v.tensor_scalar(cols[:, 2:3], cols[:, 0:1],
                                          self.cons[:, INV1024:INV1024 + 1], None, ALU.mult))
            self.vsync(v, v.tensor_tensor(cols[:, 0:1], cols[:, 2:3], cols[:, 2:3], ALU.mult))
            self.inc(v.tensor_scalar(cols[:, 3:4], cols[:, 1:2], self.cons[:, INV1024:INV1024 + 1],
                                     cols[:, 0:1], ALU.mult, ALU.subtract), "dv", 1)
        VDV = self.at("dv")

        @block.scalar
        def _(sc, VDV=VDV):
            self.wt(sc, "dv", VDV)
            c = self.inc(sc.activation(cols[:, 1:2], cols[:, 3:4], AF.Ln,
                                       bias=self.cons[:, EPSC:EPSC + 1]), "acs", 1)
            sc.wait_ge(self.h["acs"], c)
            self.inc(sc.activation(cols[:, 3:4], cols[:, 1:2], AF.Exp,
                                   scale=self.cons[:, NEGHALF:NEGHALF + 1]), "ac", 1)
        RAC = self.at("ac")

        @block.vector
        def _(v, RAC=RAC, dst_buf=dst_buf):
            self.wt(v, "ac", RAC)
            if dst_buf:
                self.use_buf(v, dst_buf)
            self.inc(v.tensor_scalar(dst[:], src[:], cols[:, 2:3], cols[:, 3:4],
                                     ALU.subtract, ALU.mult), "dv", 1)
        self.free_buf("cols", "dv")

    def finish(self):
        block = self.block

        @block.sync
        def _(s):
            for n in ("sSSC", "sIDX", "sOSEL", "sOCMP", "sOUT", "sDBG"):
                if n in self.c and self.c[n]:
                    s.wait_ge(self.h[n], self.c[n])


def build_nc(debug=False):
    b = Builder(debug=debug)
    b.msl = 0
    b.scpar = 0
    nc = b.build()
    return nc, b


def kernel(L0_attn_x, L0_attn_w_qkv, L0_attn_b_qkv, L0_attn_w_comp, L0_attn_b_comp,
           L0_attn_w_score, L0_attn_b_score, L0_attn_w_gate, L0_attn_b_gate,
           L0_attn_w_o, L0_attn_b_o, L0_ffn_kernel, L0_ffn_input_scale,
           L0_ffn_kernel_scale, L0_ffn_bias):
    x = np.asarray(L0_attn_x)
    w = {
        "w_qkv": np.asarray(L0_attn_w_qkv), "b_qkv": np.asarray(L0_attn_b_qkv),
        "w_comp": np.asarray(L0_attn_w_comp), "b_comp": np.asarray(L0_attn_b_comp),
        "w_score": np.asarray(L0_attn_w_score), "b_score": np.asarray(L0_attn_b_score),
        "w_gate": np.asarray(L0_attn_w_gate), "b_gate": np.asarray(L0_attn_b_gate),
        "w_o": np.asarray(L0_attn_w_o), "b_o": np.asarray(L0_attn_b_o),
        "fker": np.asarray(L0_ffn_kernel), "s_in": np.asarray(L0_ffn_input_scale),
        "s_k": np.asarray(L0_ffn_kernel_scale), "fbias": np.asarray(L0_ffn_bias),
    }
    nc, _ = build_nc(debug=False)
    in_maps = [stage_core(x, w, core) for core in range(8)]
    res = run_bass_kernel_spmd(nc, in_maps, core_ids=list(range(8)))
    out = np.empty((4, S, D), np.float32)
    for core in range(8):
        b, h = core // 2, core % 2
        out[b, h * CQ:(h + 1) * CQ] = res.results[core]["out"]
    return out
